# version 17
# baseline (speedup 1.0000x reference)
import contextlib
import numpy as np
import concourse.bass as bass
import concourse.mybir as mybir
from concourse.bass_utils import run_bass_kernel_spmd

F32 = mybir.dt.float32
I32 = mybir.dt.int32
AF = mybir.ActivationFunctionType
OP = mybir.AluOpType
AX = mybir.AxisListType

D = 1024
HA = 16
FA = 64
HB = 16
FB = 16
F2 = 256
HD = 64
SUB = 128
SUP = 256
LN_EPS = 1e-5
EPS_A = 1e-6
EPS_B = 1e-5
NORM_A = float(64.0 ** -0.25)
QSCALE_B = float(FB ** -0.5)

ENGINES = ("tensor", "vector", "scalar", "gpsimd", "sync")


class Prog:
    def __init__(self):
        self.streams = {e: [] for e in ENGINES}

    def emit(self, eng, fn):
        self.streams[eng].append(fn)

    def replay(self, eng_name):
        def run(eng):
            for fn in self.streams[eng_name]:
                fn(eng)
        return run


class Sem:
    def __init__(self, stack, nc, name):
        self.h = stack.enter_context(nc.semaphore(name))
        self.n = 0

    def bump(self, k):
        self.n += k
        return self.n


class Builder:
    def __init__(self, T, seq_cores, dbg=False, stop_after=None):
        self.T = T
        self.seq_cores = seq_cores
        self.dbg = dbg
        self.stop_after = stop_after
        self.NSUP = T // SUP
        self.nc = bass.Bass("TRN2", target_bir_lowering=False,
                            detect_race_conditions=False)
        self.stack = contextlib.ExitStack()
        self.prog = Prog()
        self.dram = {}
        self.sb = {}

    def dram_in(self, name, shape):
        self.dram[name] = self.nc.dram_tensor(name, list(shape), F32,
                                              kind="ExternalInput").ap()
        return self.dram[name]

    def dram_out(self, name, shape):
        self.dram[name] = self.nc.dram_tensor(name, list(shape), F32,
                                              kind="ExternalOutput").ap()
        return self.dram[name]

    def dram_tmp(self, name, shape):
        self.dram[name] = self.nc.dram_tensor(name, list(shape), F32,
                                              kind="Internal").ap()
        return self.dram[name]

    def sbuf(self, name, shape, dtype=F32, stack=None):
        t = (stack or self.stack).enter_context(
            self.nc.sbuf_tensor(name, list(shape), dtype))
        self.sb[name] = t
        return t

    def psum(self, name, shape, stack=None):
        assert shape[0] == SUB
        cols = ((shape[1] + 511) // 512) * 512
        t = (stack or self.stack).enter_context(
            self.nc.psum_tensor(name, [SUB, cols], F32))
        self.sb[name] = t
        return t

    def sem(self, name):
        return Sem(self.stack, self.nc, name)

    def dma(self, eng, out, in_, sem):
        sem.bump(16)
        self.prog.emit(eng, lambda e, out=out, in_=in_, s=sem: e.dma_start(
            out=out, in_=in_).then_inc(s.h, 16))
        return sem.n

    def wait(self, eng, sem, val=None):
        v = sem.n if val is None else val
        self.prog.emit(eng, lambda e, s=sem, v=v: e.wait_ge(s.h, v))

    def op(self, eng, fn, sem=None):
        if sem is None:
            self.prog.emit(eng, fn)
        else:
            sem.bump(1)
            self.prog.emit(eng, lambda e, fn=fn, s=sem: fn(e).then_inc(s.h, 1))

    def barrier(self):
        bar = self.bar
        self.wait("sync", self.sems["ld"])
        self.wait("gpsimd", self.sems["st"])
        target = bar.n + len(ENGINES)
        for e in ENGINES:
            bar.bump(1)
            self.prog.emit(e, lambda eng, s=bar: eng.sem_inc(s.h, 1))
        for e in ENGINES:
            self.prog.emit(e, lambda eng, s=bar, t=target: eng.wait_ge(s.h, t))

    def build(self):
        b, nc = self, self.nc
        T, NSUP = self.T, self.NSUP

        b.dram_in("x", [T, D])
        for nm, shp in [("Wqa", [D, D]), ("Wka", [D, D]), ("bqc", [64, HA]),
                        ("bkc", [64, HA]), ("WvaT", [D, D]),
                        ("WoaT", [D, D]), ("PT", [FA, FA]),
                        ("bv", [1, D]), ("bo", [1, D]),
                        ("WT", [D, D]), ("bf", [1, D]), ("Vr", [D, 32]), ("CT", [32, D]),
                        ("Wqb", [D, F2]), ("Wkb", [D, F2]), ("Wvb", [D, D]), ("Wob", [D, D]),
                        ("pmask", [1, 4])]:
            b.dram_in(nm, shp)
        b.dram_out("out", [T, D])

        mk_sp = b.dram_out if self.dbg else b.dram_tmp
        mk_sp("qfT_sp", [D, T])
        mk_sp("x2_sp", [T, D])
        mk_sp("qT_sp", [F2, T])
        mk_sp("kT_sp", [F2, T])
        b.dram_tmp("vpb_sp", [T // SUB, SUB, HB * 65])
        b.dram_tmp("u_sp", [NSUP, SUB, 32 * 65])
        b.dram_tmp("kv_cc_in", [SUB, 8 * 65])
        b.dram_tmp("s_cc_in", [SUB, 32 * 65])
        if self.seq_cores > 1:
            b.dram_tmp("kv_cc_out", [self.seq_cores * SUB, 8 * 65])
            b.dram_tmp("s_cc_out", [self.seq_cores * SUB, 32 * 65])
        if self.dbg:
            for nm in ("qfT_sp", "x2_sp", "qT_sp", "kT_sp"):
                pass
            for nm, shp in [("dbg_xn1", [T, D]), ("dbg_kv", [SUB, 8 * 65]),
                            ("dbg_x1", [T, D]), ("dbg_phk", [T, HB * F2]),
                            ("dbg_ob", [T, D]), ("dbg_x3", [T, D]),
                            ("dbg_onum", [T, HB * 65]),
                            ("dbg_phq", [T, HB * F2]),
                            ("dbg_srun", [NSUP, SUB, 32 * 65]),
                            ("dbg_r1raw", [SUB, HB * 2 * SUB]),
                            ("dbg_r2", [2, SUB, 8 * SUB])]:
                b.dram_out(nm, shp)

        b.sbuf("WT_e", [SUB, 8 * D])
        b.sbuf("CT_e", [33, D])
        b.sbuf("Vr_e", [SUB, 8 * 32])
        b.sbuf("ident", [SUB, SUB])
        b.sbuf("triu", [SUB, SUB])
        
        b.sbuf("ones_row", [1, 256])
        b.sbuf("pm_s", [SUB, 4])
        b.sbuf("normPT2", [SUB, FA])
        b.sbuf("bias_of", [1, 2 * D])

        self.sems = {k: self.sem("s_" + k) for k in
                     ("ld", "st", "mm", "ve", "sc", "gp", "cc")}
        self.bar = self.sem("s_bar")

        self.emit_init()
        phases = [("A", self.emit_phase_a),
                  ("C1", self.emit_phase_c1), ("C2", self.emit_phase_c2),
                  ("E", self.emit_phase_e)]
        barrier_before = {"C1", "C2", "E"}
        done = []
        for nm, fn in phases:
            if nm in barrier_before:
                self.barrier()
            fn()
            done.append(nm)
            if self.stop_after == nm:
                break
        if self.stop_after is not None and "E" not in done:
            self.barrier()
            zt = self.sbuf("z_fill", [SUB, D])
            ve, st = self.sems["ve"], self.sems["st"]
            self.op("vector", lambda e: e.memset(zt[:], 0.0), ve)
            self.wait("gpsimd", ve)
            for r in range(T // SUB):
                self.dma("gpsimd", self.dram["out"][r * SUB:(r + 1) * SUB, :], zt[:], st)
            self.wait("gpsimd", st)

        with self.stack:
            with nc.Block() as block:
                block.tensor(self.prog.replay("tensor"))
                block.vector(self.prog.replay("vector"))
                block.scalar(self.prog.replay("scalar"))
                block.gpsimd(self.prog.replay("gpsimd"))
                block.sync(self.prog.replay("sync"))
        return nc

    def emit_init(self):
        b, sb, dram = self, self.sb, self.dram
        ld, ve, gp = self.sems["ld"], self.sems["ve"], self.sems["gp"]

        b.op("gpsimd", lambda e: e.iota(idx32[:], pattern=[[1, SUB]], base=0,
                                        channel_multiplier=-1), gp)
        b.wait("vector", gp)
        b.op("vector", lambda e: e.tensor_single_scalar(sb["ident"][:], idx32[:],
                                                        0, op=OP.is_equal))
        b.op("vector", lambda e: e.tensor_single_scalar(sb["triu"][:], idx32[:],
                                                        0, op=OP.is_ge))
        b.op("vector", lambda e: e.memset(sb["ones_row"][:], 1.0))

        b.dma("sync", sb["pm_s"][:], dram["pmask"].partition_broadcast(SUB), ld)
        for i, nm in enumerate(["bo", "bf"]):
            b.dma("sync", sb["bias_of"][0:1, i * D:(i + 1) * D], dram[nm], ld)
        insc = contextlib.ExitStack()
        idx32 = b.sbuf("idx32", [SUB, SUB], I32, stack=insc)
        pt_raw = b.sbuf("pt_raw", [FA, FA], stack=insc)
        self._init_scope = insc
        b.dma("sync", pt_raw[:], dram["PT"], ld)
        b.wait("vector", ld)
        for half in (0, 1):
            b.op("vector", lambda e, h=half: e.tensor_single_scalar(
                sb["normPT2"][h * 64:(h + 1) * 64, :], pt_raw[:], NORM_A, op=OP.mult),
                ve)

    def emit_ln(self, x_ap, xn_ap, stats, mrneg, rstd, bump_ve=True):
        b = self
        ve, sc = self.sems["ve"], self.sems["sc"]
        b.wait("vector", sc)
        b.wait("vector", ve)
        b.op("vector", lambda e: e.reduce_sum(stats[:, 0:1], x_ap, axis=AX.X), ve)
        b.wait("vector", ve)
        b.wait("scalar", ve)
        b.op("scalar", lambda e: e.activation(xn_ap, x_ap, AF.Square,
                                              accum_out=stats[:, 1:2]), sc)
        b.wait("vector", sc)
        b.op("vector", lambda e: e.tensor_scalar_mul(stats[:, 2:3], stats[:, 0:1],
                                                     1.0 / D))
        b.op("vector", lambda e: e.tensor_scalar_mul(stats[:, 3:4], stats[:, 1:2],
                                                     1.0 / D), ve)
        b.wait("vector", ve)
        b.op("vector", lambda e: e.tensor_mul(stats[:, 4:5], stats[:, 2:3],
                                              stats[:, 2:3]), ve)
        b.wait("vector", ve)
        b.op("vector", lambda e: e.scalar_tensor_tensor(
            stats[:, 5:6], stats[:, 4:5], -1.0, stats[:, 3:4],
            op0=OP.mult, op1=OP.add), ve)
        b.wait("vector", ve)
        b.op("vector", lambda e: e.tensor_scalar_add(stats[:, 6:7], stats[:, 5:6],
                                                     LN_EPS), ve)
        b.wait("scalar", ve)
        b.op("scalar", lambda e: e.activation(stats[:, 7:8], stats[:, 6:7],
                                              AF.Sqrt), sc)
        b.wait("vector", sc)
        b.op("vector", lambda e: e.reciprocal(rstd[:], stats[:, 7:8]), ve)
        b.wait("vector", ve)
        b.op("vector", lambda e: e.scalar_tensor_tensor(
            mrneg[:], stats[:, 2:3], -1.0, rstd[:], op0=OP.mult, op1=OP.mult), ve)
        b.wait("vector", ve)
        b.op("vector", lambda e: e.tensor_scalar(
            xn_ap, x_ap, rstd[:], mrneg[:], op0=OP.mult, op1=OP.add),
            ve if bump_ve else None)

    def emit_ln2(self, x2_ap, xn2_ap, stats, mrneg, rstd, sq=None):
        b = self
        ve, sc = self.sems["ve"], self.sems["sc"]
        sq0 = sq if sq is not None else xn2_ap[:, 0:D]
        sq1 = sq if sq is not None else xn2_ap[:, D:2 * D]
        b.wait("vector", sc)
        b.wait("vector", ve)
        b.op("vector", lambda e: e.reduce_sum(stats[:, 0:1], x2_ap[:, 0:D], axis=AX.X))
        b.op("vector", lambda e: e.reduce_sum(stats[:, 1:2], x2_ap[:, D:2 * D],
                                              axis=AX.X), ve)
        b.wait("vector", ve)
        b.wait("scalar", ve)
        b.op("scalar", lambda e: e.activation(sq0, x2_ap[:, 0:D], AF.Square,
                                              accum_out=stats[:, 2:3]))
        b.op("scalar", lambda e: e.activation(sq1, x2_ap[:, D:2 * D],
                                              AF.Square, accum_out=stats[:, 3:4]), sc)
        b.wait("vector", sc)
        b.op("vector", lambda e: e.tensor_scalar_mul(stats[:, 4:6], stats[:, 0:2],
                                                     1.0 / D))
        b.op("vector", lambda e: e.tensor_scalar_mul(stats[:, 6:8], stats[:, 2:4],
                                                     1.0 / D), ve)
        b.wait("vector", ve)
        b.op("vector", lambda e: e.tensor_mul(stats[:, 8:10], stats[:, 4:6],
                                              stats[:, 4:6]), ve)
        b.wait("vector", ve)
        b.op("vector", lambda e: e.scalar_tensor_tensor(
            stats[:, 10:12], stats[:, 8:10], -1.0, stats[:, 6:8],
            op0=OP.mult, op1=OP.add), ve)
        b.wait("vector", ve)
        b.op("vector", lambda e: e.tensor_scalar_add(stats[:, 12:14], stats[:, 10:12],
                                                     LN_EPS), ve)
        b.wait("scalar", ve)
        b.op("scalar", lambda e: e.activation(stats[:, 14:16], stats[:, 12:14],
                                              AF.Sqrt), sc)
        b.wait("vector", sc)
        b.op("vector", lambda e: e.reciprocal(rstd[:, 0:2], stats[:, 14:16]), ve)
        b.wait("vector", ve)
        b.op("vector", lambda e: e.scalar_tensor_tensor(
            mrneg[:, 0:2], stats[:, 4:6], -1.0, rstd[:, 0:2],
            op0=OP.mult, op1=OP.mult), ve)
        b.wait("vector", ve)
        for j in (0, 1):
            b.op("vector", lambda e, j=j: e.tensor_scalar(
                xn2_ap[:, j * D:(j + 1) * D], x2_ap[:, j * D:(j + 1) * D],
                rstd[:, j:j + 1], mrneg[:, j:j + 1], op0=OP.mult, op1=OP.add), ve)

    def emit_transpose_1024(self, src, dst, off, stride, psl, bank_free, ve_src=None):
        b, sb = self, self.sb
        mm, ve = self.sems["mm"], self.sems["ve"]
        dstv = dst[:].rearrange("p (c t) -> p c t", t=stride)
        b.wait("tensor", ve, ve_src)
        for g in (0, 1):
            ps = psl[g]
            key = id(ps)
            if bank_free.get(key):
                b.wait("tensor", ve, bank_free[key])
            for cc in range(4):
                c = g * 4 + cc
                b.op("tensor", lambda e, c=c, cc=cc, ps=ps: e.transpose(
                    ps[:, cc * SUB:(cc + 1) * SUB], src[:, c * SUB:(c + 1) * SUB],
                    sb["ident"][:]), mm if cc == 3 else None)
            b.wait("vector", mm)
            b.op("vector", lambda e, g=g, ps=ps: e.tensor_copy(
                dstv[:, 4 * g:4 * g + 4, off // 1:off // 1 + SUB]
                if stride != SUB else dst[:, g * 512:(g + 1) * 512],
                ps[:, 0:512].rearrange("p (c t) -> p c t", t=SUB)
                if stride != SUB else ps[:, 0:512]), ve)
            bank_free[key] = ve.n

    def bwait(self, ps, bf):
        v = bf.get(id(ps))
        if v:
            self.wait("tensor", self.sems["ve"], v)

    def bdone(self, ps, bf):
        bf[id(ps)] = self.sems["ve"].n

    def emit_phase_a(self):
        b, sb, dram = self, self.sb, self.dram
        T, NSUP = self.T, self.NSUP
        ld, st, mm, ve, sc = (self.sems[k] for k in ("ld", "st", "mm", "ve", "sc"))
        self._init_scope.close()
        ph = contextlib.ExitStack()
        b.wait("sync", self.sems["ve"])

        WpqT = b.sbuf("WpqT_s", [SUB, 8 * D], stack=ph)
        WpkT = b.sbuf("WpkT_s", [SUB, 8 * D], stack=ph)
        WvaT = b.sbuf("WvaT_s", [SUB, 8 * D], stack=ph)
        bpq = b.sbuf("bpq_s", [SUB, 8], stack=ph)
        bpk = b.sbuf("bpk_s", [SUB, 8], stack=ph)
        for c in range(8):
            b.dma("sync", WvaT[:, c * D:(c + 1) * D], dram["WvaT"][c * SUB:(c + 1) * SUB, :], ld)
        ld_w = ld.n
        for c in range(8):
            b.dma("sync", sb["WT_e"][:, c * D:(c + 1) * D],
                  dram["WT"][c * SUB:(c + 1) * SUB, :], ld)
            b.dma("sync", sb["Vr_e"][:, c * 32:(c + 1) * 32],
                  dram["Vr"][c * SUB:(c + 1) * SUB, :], ld)
        b.dma("sync", sb["CT_e"][0:32, :], dram["CT"], ld)
        b.dma("sync", sb["CT_e"][32:33, :], dram["bf"], ld)
        ld_ffnw = ld.n

        x_s = b.sbuf("a_x", [SUB, 2 * D], stack=ph)
        xn_s = b.sbuf("a_xn", [SUB, 2 * D], stack=ph)
        xnT = b.sbuf("a_xnT", [SUB, 2 * D], stack=ph)
        QfT = b.sbuf("a_QfT", [SUB, 8 * SUP], stack=ph)
        KfT = b.sbuf("a_KfT", [SUB, 8 * SUP], stack=ph)
        Kf = b.sbuf("a_Kf", [SUB, D], stack=ph)
        vpa = b.sbuf("a_vpa", [SUB, HA * 65], stack=ph)
        stats = b.sbuf("a_stats", [SUB, 16], stack=ph)
        mrneg = b.sbuf("a_mrneg", [SUB, 2], stack=ph)
        rstd = b.sbuf("a_rstd", [SUB, 2], stack=ph)
        kv_s = b.sbuf("a_kv", [SUB, 8 * 65], stack=ph)
        bias_qkv = b.sbuf("bias_qkv", [1, D], stack=ph)
        b.dma("sync", bias_qkv[0:1, 0:D], dram["bv"], ld)

        for h in range(HA):
            b.op("vector", lambda e, h=h: e.memset(
                vpa[:, h * 65 + 64: h * 65 + 65], 1.0))
        b.op("vector", lambda e: e.engine_nop(), ve)

        ps_p0 = b.psum("ps_p0", [SUB, 512], stack=ph)
        ps_p1 = b.psum("ps_p1", [SUB, 512], stack=ph)
        ps_tr = b.psum("ps_tr", [SUB, 512], stack=ph)
        ps_v0 = b.psum("ps_v0", [SUB, 512], stack=ph)
        ps_v1 = b.psum("ps_v1", [SUB, 512], stack=ph)
        kv_psl = [b.psum(f"kv_ps{k}", [SUB, 512], stack=ph) for k in range(3)]
        ps_f = ps_p0
        bf = {}

        pre = contextlib.ExitStack()
        tmp64 = b.sbuf("a_tmp64", [64, D], stack=pre)
        bc64 = b.sbuf("a_bc64", [64, HA], stack=pre)
        prot = (ps_p0, ps_p1)
        for wi, (wd, WpT, bd, bp) in enumerate(
                ((dram["Wqa"], WpqT, dram["bqc"], bpq),
                 (dram["Wka"], WpkT, dram["bkc"], bpk))):
            for h in range(HA):
                b.wait("sync", ve)
                b.dma("sync", tmp64[:], wd[h * 64:(h + 1) * 64, :], ld)
                ps = prot[h % 2]
                b.bwait(ps, bf)
                b.wait("tensor", ld)
                for c in range(8):
                    b.op("tensor", lambda e, c=c, ps=ps: e.matmul(
                        ps[:, c * 64:(c + 1) * 64], tmp64[:, c * SUB:(c + 1) * SUB],
                        sb["normPT2"][0:64, :], start=(c == 0), stop=(c == 7)),
                        mm if c == 7 else None)
                b.wait("vector", mm)
                off = (h // 2) * SUB + (h % 2) * 64
                b.op("vector", lambda e, off=off, ps=ps, WpT=WpT: e.tensor_copy(
                    WpT[:].rearrange("p (c t) -> p c t", t=8 * SUB)
                    [:, :, off:off + 64],
                    ps[:, 0:512].rearrange("p (c t) -> p c t", t=64)), ve)
                b.bdone(ps, bf)
            b.wait("sync", ve)
            b.dma("sync", bc64[:], bd, ld)
            b.wait("tensor", ld)
            b.bwait(ps_f, bf)
            b.op("tensor", lambda e, bp=bp: e.matmul(
                ps_f[0:64, 0:HA], sb["normPT2"][0:64, :], bc64[:],
                start=True, stop=True), mm)
            b.wait("vector", mm)
            for half in (0, 1):
                b.op("vector", lambda e, half=half, bp=bp: e.tensor_copy(
                    bp[half * 64:(half + 1) * 64, :],
                    ps_f[0:64, 0:HA].rearrange("p (m two) -> p m two", two=2)
                    [:, :, half]), ve)
            b.wait("vector", ve)
            b.op("vector", lambda e, bp=bp: e.tensor_scalar_add(bp[:], bp[:], EPS_A),
                 ve)
            b.bdone(ps_f, bf)
        pre.close()

        kv_first = [True] * 3

        def kv_pair(p):
            return (kv_psl[p // 3], (p % 3) * 130)

        st_qf = st.n
        prot3 = (ps_p0, ps_p1)
        for j in (0, 1):
            b.dma("sync", x_s[:, j * D:(j + 1) * D],
                  dram["x"][j * SUB:(j + 1) * SUB, :], ld)
        b.wait("vector", ld)
        self.emit_ln2(x_s[:], xn_s[:], stats, mrneg, rstd)
        ve_ln = ve.n
        for i in range(NSUP):
            tok0 = i * SUP
            if self.dbg:
                b.wait("gpsimd", ve, ve_ln)
                for j in (0, 1):
                    b.dma("gpsimd", dram["dbg_xn1"][tok0 + j * SUB: tok0 + (j + 1) * SUB, :],
                          xn_s[:, j * D:(j + 1) * D], st)
            for j in (0, 1):
                self.emit_transpose_1024(xn_s[:, j * D:(j + 1) * D], xnT, j * SUB, SUP,
                                         (ps_tr, ps_v0), bf, ve_src=ve_ln)
            ve_xnT = ve.n
            b.wait("tensor", ld, ld_w)
            b.wait("tensor", ve, ve_xnT)
            for (W, dst, bp) in ((WpqT, QfT, bpq), (WpkT, KfT, bpk)):
                for m in range(8):
                    ps = prot3[m % 2]
                    b.bwait(ps, bf)
                    for c in range(8):
                        b.op("tensor", lambda e, W=W, m=m, c=c, ps=ps: e.matmul(
                            ps[:, 0:SUP], W[:, (c * 8 + m) * SUB:(c * 8 + m + 1) * SUB],
                            xnT[:, c * SUP:(c + 1) * SUP],
                            start=(c == 0), stop=(c == 7)), mm if c == 7 else None)
                    b.wait("vector", mm)
                    if dst is QfT:
                        b.wait("vector", st, st_qf)
                    b.op("vector", lambda e, m=m, ps=ps, dst=dst, bp=bp: e.tensor_scalar(
                        dst[:, m * SUP:(m + 1) * SUP], ps[:, 0:SUP],
                        bp[:, m:m + 1], EPS_A, op0=OP.add, op1=OP.max), ve)
                    b.bdone(ps, bf)
            ve_qf = ve.n
            b.wait("gpsimd", ve, ve_qf)
            for m in range(8):
                b.dma("gpsimd", dram["qfT_sp"][m * SUB:(m + 1) * SUB, tok0:tok0 + SUP],
                      QfT[:, m * SUP:(m + 1) * SUP], st)
            st_qf = st.n

            if i + 1 < NSUP:
                b.wait("sync", ve, ve_ln)
                for j in (0, 1):
                    b.dma("sync", x_s[:, j * D:(j + 1) * D],
                          dram["x"][tok0 + SUP + j * SUB: tok0 + SUP + (j + 1) * SUB, :],
                          ld)
                b.wait("vector", ld)
                self.emit_ln2(x_s[:], xn_s[:], stats, mrneg, rstd)
                ve_ln = ve.n

            b.wait("tensor", ve, ve_qf)
            for j in (0, 1):
                for g in (0, 1):
                    ps = (ps_tr, ps_v0)[g]
                    b.bwait(ps, bf)
                    for cc in range(4):
                        m = g * 4 + cc
                        b.op("tensor", lambda e, m=m, cc=cc, j=j, ps=ps: e.transpose(
                            ps[:, cc * SUB:(cc + 1) * SUB],
                            KfT[:, m * SUP + j * SUB: m * SUP + (j + 1) * SUB],
                            sb["ident"][:]), mm if cc == 3 else None)
                    b.wait("vector", mm)
                    b.op("vector", lambda e, g=g, ps=ps: e.tensor_copy(
                        Kf[:, g * 512:(g + 1) * 512], ps[:, 0:512]), ve)
                    b.bdone(ps, bf)
                for n in (0, 1):
                    ps = (ps_v0, ps_v1)[n]
                    b.bwait(ps, bf)
                    for c in range(8):
                        b.op("tensor", lambda e, c=c, j=j, n=n, ps=ps: e.matmul(
                            ps[:, 0:512],
                            xnT[:, c * SUP + j * SUB: c * SUP + (j + 1) * SUB],
                            WvaT[:, c * D + n * 512: c * D + (n + 1) * 512],
                            start=(c == 0), stop=False))
                    b.op("tensor", lambda e, n=n, ps=ps: e.matmul(
                        ps[:, 0:512], sb["ones_row"][0:1, 0:SUB],
                        bias_qkv[0:1, n * 512:(n + 1) * 512],
                        start=False, stop=True), mm)
                    b.wait("vector", mm)
                    b.op("vector", lambda e, n=n, ps=ps: e.tensor_copy(
                        vpa[:].rearrange("p (h c) -> p h c", h=16)
                        [:, 8 * n:8 * (n + 1), 0:64],
                        ps[:, 0:512].rearrange("p (h c) -> p h c", h=8)), ve)
                    b.bdone(ps, bf)
                b.wait("tensor", ve)
                for p in range(8):
                    ps, col = kv_pair(p)
                    bank = p // 3
                    first = kv_first[bank]
                    kv_first[bank] = False
                    stop = (i == NSUP - 1 and j == 1 and
                            (p % 3 == 2 or p == 7))
                    last_sup = (j == 1 and p == 7)
                    b.op("tensor", lambda e, p=p, ps=ps, col=col, first=first,
                         stop=stop: e.matmul(
                        ps[:, col:col + 130],
                        Kf[:, p * SUB:(p + 1) * SUB],
                        vpa[:, 2 * p * 65: 2 * p * 65 + 130],
                        start=first, stop=stop),
                        mm if last_sup else None)

        b.wait("vector", mm)
        for bi_ in range(3):
            npair = (3, 3, 2)[bi_]
            base = bi_ * 3
            b.op("vector", lambda e, bi_=bi_, npair=npair, base=base: e.tensor_copy(
                kv_s[0:64, base * 65: (base + npair) * 65]
                .rearrange("p (s c) -> p s c", c=65),
                kv_psl[bi_][0:64, 0:npair * 130]
                .rearrange("p (s c) -> p s c", c=130)[:, :, 0:65]))
            b.op("vector", lambda e, bi_=bi_, npair=npair, base=base: e.tensor_copy(
                kv_s[64:128, base * 65: (base + npair) * 65]
                .rearrange("p (s c) -> p s c", c=65),
                kv_psl[bi_][64:128, 0:npair * 130]
                .rearrange("p (s c) -> p s c", c=130)[:, :, 65:130]),
                ve if bi_ == 2 else None)
        ve_kv = ve.n
        b.wait("gpsimd", ve, ve_kv)
        b.dma("gpsimd", dram["kv_cc_in"], kv_s[:], st)
        if self.dbg:
            b.dma("gpsimd", dram["dbg_kv"], kv_s[:], st)
        self.st_kv = st.n
        ph.close()

    def emit_kv_gather(self):
        b, sb, dram = self, self.sb, self.dram
        ld, st, ve, cc = (self.sems[k] for k in ("ld", "st", "ve", "cc"))
        kvscope = contextlib.ExitStack()
        b.sbuf("kvg", [SUB, 8 * 65], stack=kvscope)
        if self.seq_cores > 1:
            b.op("gpsimd", lambda e: e.collective_compute(
                "AllGather", OP.bypass,
                ins=[dram["kv_cc_in"]], outs=[dram["kv_cc_out"]],
                replica_groups=[[g * self.seq_cores + r for r in range(self.seq_cores)]
                                for g in range(8 // self.seq_cores)]), cc)
            b.wait("sync", cc)
            phk_ = contextlib.ExitStack()
            tmp = b.sbuf("kv_blk", [SUB, 4 * 520], stack=phk_)
            for r in range(self.seq_cores):
                b.dma("sync", tmp[:, r * 520:(r + 1) * 520],
                      dram["kv_cc_out"][r * SUB:(r + 1) * SUB, :], ld)
            b.wait("vector", ld)
            b.op("vector", lambda e: e.tensor_add(sb["kvg"][:], tmp[:, 0:520],
                                                  tmp[:, 520:1040]))
            if self.seq_cores == 4:
                b.op("vector", lambda e: e.tensor_add(sb["kvg"][:], sb["kvg"][:],
                                                      tmp[:, 1040:1560]))
                b.op("vector", lambda e: e.tensor_add(sb["kvg"][:], sb["kvg"][:],
                                                      tmp[:, 1560:2080]))
            b.op("vector", lambda e: e.engine_nop(), ve)
            phk_.close()
        else:
            b.dma("sync", sb["kvg"][:], dram["kv_cc_in"], ld)
            b.wait("vector", ld)
            b.op("vector", lambda e: e.engine_nop(), ve)
        b.op("vector", lambda e: e.memset(sb["kvg_z"][:], 0.0), ve)
        b.wait("vector", ve)
        for h in range(HA):
            b.op("vector", lambda e, h=h: e.tensor_copy(
                sb["kvg_z"][64 * (h % 2):64 * (h % 2) + 64, h * 65:(h + 1) * 65],
                sb["kvg"][64 * (h % 2):64 * (h % 2) + 64,
                          (h // 2) * 65:(h // 2) * 65 + 65]),
                ve if h == HA - 1 else None)
        self.ve_kvg = ve.n
        kvscope.close()

    def emit_div_evac(self, o_banks, o_s, recip, rb, scr, eps, bf=None):
        b = self
        ve, mm = self.sems["ve"], self.sems["mm"]
        b.wait("vector", mm)
        col = 0
        scol = 0
        for (ps, nsl) in o_banks:
            b.op("vector", lambda e, ps=ps, n=nsl, c=col, eps=eps: e.tensor_scalar_add(
                recip[:, c:c + n],
                ps[:, 0:n * 65].rearrange("p (s c) -> p s c", c=65)[:, :, 64], eps))
            b.op("vector", lambda e, ps=ps, n=nsl, sc_=scol: e.tensor_copy(
                scr[:, sc_:sc_ + n * 65], ps[:, 0:n * 65]), ve)
            col += nsl
            scol += nsl * 65
        if bf is not None:
            for (ps, nsl) in o_banks:
                b.bdone(ps, bf)
        b.op("vector", lambda e: e.engine_nop(), ve)
        b.wait("vector", ve)
        b.op("vector", lambda e: e.reciprocal(recip[:, 0:16], recip[:, 0:16]), ve)
        b.wait("vector", ve)
        b.op("vector", lambda e: e.tensor_copy(
            rb[:].rearrange("p (s f) -> p s f", f=64),
            recip[:, 0:16].unsqueeze(2).broadcast_to([SUB, 16, 64])), ve)
        b.wait("vector", ve)
        col = 0
        scol = 0
        for (ps, nsl) in o_banks:
            b.op("vector", lambda e, n=nsl, c=col, sc_=scol: e.tensor_tensor(
                o_s[:, c * 64:(c + n) * 64].rearrange("p (s f) -> p s f", f=64),
                scr[:, sc_:sc_ + n * 65].rearrange("p (s c) -> p s c", c=65)[:, :, 0:64],
                rb[:, c * 64:(c + n) * 64].rearrange("p (s f) -> p s f", f=64),
                op=OP.mult), ve)
            col += nsl
            scol += nsl * 65

    def emit_ffn(self, xnT, res_s, ps_t, ps_w0, ps_w1, tT_s, cb, wts=None, bf=None,
                 ve_ready=None, mid_cb=None):
        b, sb = self, self.sb
        mm, ve = self.sems["mm"], self.sems["ve"]
        WT, CT, Vr = wts if wts is not None else (sb["WT_e"], sb["CT_e"], sb["Vr_e"])
        if bf is None:
            bf = {}
        b.wait("tensor", ve, ve_ready)
        b.bwait(ps_t, bf)
        for c in range(8):
            b.op("tensor", lambda e, c=c: e.matmul(
                ps_t[0:32, 0:SUP], Vr[:, c * 32:(c + 1) * 32],
                xnT[:, c * SUP:(c + 1) * SUP], start=(c == 0), stop=(c == 7)),
                mm if c == 7 else None)
        b.wait("vector", mm)
        b.op("vector", lambda e: e.tensor_copy(tT_s[0:32, 0:SUP], ps_t[0:32, 0:SUP]), ve)
        b.bdone(ps_t, bf)
        ve_tT = ve.n
        if mid_cb is not None:
            mid_cb()
        b.wait("tensor", ve, ve_tT)
        for j in (0, 1):
            for n in (0, 1):
                ps = (ps_w0, ps_w1)[n]
                b.bwait(ps, bf)
                for c in range(8):
                    b.op("tensor", lambda e, c=c, j=j, n=n, ps=ps: e.matmul(
                        ps[:, 0:512], xnT[:, c * SUP + j * SUB: c * SUP + (j + 1) * SUB],
                        WT[:, c * D + n * 512: c * D + (n + 1) * 512],
                        start=(c == 0), stop=False))
                b.op("tensor", lambda e, j=j, n=n, ps=ps: e.matmul(
                    ps[:, 0:512], tT_s[0:33, j * SUB:(j + 1) * SUB],
                    CT[0:33, n * 512:(n + 1) * 512], start=False, stop=True), mm)
            b.wait("vector", mm)
            out_ap = cb(j)
            for n in (0, 1):
                ps = (ps_w0, ps_w1)[n]
                b.op("vector", lambda e, j=j, n=n, ps=ps, out_ap=out_ap: e.tensor_tensor(
                    out_ap[:, n * 512:(n + 1) * 512], ps[:, 0:512],
                    res_s[:, j * D + n * 512: j * D + (n + 1) * 512], op=OP.add), ve)
                b.bdone(ps, bf)

    def emit_phase_c1(self):
        b, sb, dram = self, self.sb, self.dram
        T, NSUP = self.T, self.NSUP
        ld, st, mm, ve, sc = (self.sems[k] for k in ("ld", "st", "mm", "ve", "sc"))
        ph = contextlib.ExitStack()

        WoaT = b.sbuf("WoaT_s", [SUB, 8 * D], stack=ph)
        b.sbuf("kvg_z", [SUB, HA * 65], stack=ph)
        for c in range(8):
            b.dma("sync", WoaT[:, c * D:(c + 1) * D], dram["WoaT"][c * SUB:(c + 1) * SUB, :], ld)
        ld_w = ld.n

        x_s = b.sbuf("c_x", [SUB, 2 * D], stack=ph)
        QfT = b.sbuf("c_QfT", [SUB, 2 * D], stack=ph)
        o_s = b.sbuf("c_o", [SUB, D], stack=ph)
        oT = b.sbuf("c_oT", [SUB, D], stack=ph)
        x1_s = b.sbuf("c_x1", [SUB, 2 * D], stack=ph)
        xn_s = b.sbuf("c_xn", [SUB, 2 * D], stack=ph)
        xnT = b.sbuf("c_xnT", [SUB, 2 * D], stack=ph)
        x2_s = b.sbuf("c_x2", [SUB, 2 * D], stack=ph)
        tT_s = b.sbuf("c_tT", [33, SUP], stack=ph)
        b.op("vector", lambda e: e.memset(tT_s[32:33, :], 1.0))
        recip = b.sbuf("c_recip", [SUB, 16], stack=ph)
        scr_dv = b.sbuf("c_scrdv", [SUB, 1040], stack=ph)
        rb = b.sbuf("c_rb", [SUB, D], stack=ph)
        stats = b.sbuf("c_stats", [SUB, 16], stack=ph)
        mrneg = b.sbuf("c_mrneg", [SUB, 2], stack=ph)
        rstd = b.sbuf("c_rstd", [SUB, 2], stack=ph)

        ps_o0 = b.psum("c_ps_o0", [SUB, 512], stack=ph)
        ps_o1 = b.psum("c_ps_o1", [SUB, 512], stack=ph)
        ps_o2 = b.psum("c_ps_o2", [SUB, 512], stack=ph)
        ps_tr = b.psum("c_ps_tr", [SUB, 512], stack=ph)
        ps_w0 = b.psum("c_ps_w0", [SUB, 512], stack=ph)
        ps_w1 = b.psum("c_ps_w1", [SUB, 512], stack=ph)
        ps_t = b.psum("c_ps_t", [SUB, 512], stack=ph)
        o_banks = [(ps_o0, 6), (ps_o1, 6), (ps_o2, 4)]
        psl = (ps_tr, ps_t)
        bf = {}

        qfT_r = dram["qfT_sp"].rearrange("(m p) t -> p m t", m=8)

        def load_sup(i, guard_ve):
            tok0 = i * SUP
            if guard_ve:
                b.wait("sync", ve, guard_ve)
            for j in (0, 1):
                b.dma("sync", x_s[:, j * D:(j + 1) * D],
                      dram["x"][tok0 + j * SUB: tok0 + (j + 1) * SUB, :], ld)
            b.dma("sync", QfT[:].rearrange("p (m t) -> p m t", m=8),
                  qfT_r[:, :, tok0:tok0 + SUP], ld)
            return ld.n

        ld_x = load_sup(0, 0)
        self.emit_kv_gather()
        st_x2 = st.n
        for i in range(NSUP):
            tok0 = i * SUP
            if i == 0:
                b.wait("tensor", ve, self.ve_kvg)
            for j in (0, 1):
                b.wait("tensor", ld, ld_x)
                hbase = 0
                for (ps, nsl) in o_banks:
                    b.bwait(ps, bf)
                    for pr in range(nsl // 2):
                        h0 = hbase + 2 * pr
                        b.op("tensor", lambda e, h0=h0, ps=ps, pr=pr, j=j,
                             nsl=nsl: e.matmul(
                            ps[:, pr * 130:(pr + 1) * 130],
                            QfT[:, (h0 // 2) * SUP + j * SUB:(h0 // 2) * SUP + (j + 1) * SUB],
                            sb["kvg_z"][:, h0 * 65:(h0 + 2) * 65],
                            start=(pr == 0), stop=(pr == nsl // 2 - 1)),
                            mm if pr == nsl // 2 - 1 else None)
                    hbase += nsl
                self.emit_div_evac(o_banks, o_s, recip, rb, scr_dv, EPS_A, bf=bf)
                self.emit_transpose_1024(o_s, oT, 0, SUB, psl, bf)
                for n in (0, 1):
                    ps = (ps_w0, ps_w1)[n]
                    b.bwait(ps, bf)
                    for c in range(8):
                        b.op("tensor", lambda e, c=c, n=n, ps=ps: e.matmul(
                            ps[:, 0:512], oT[:, c * SUB:(c + 1) * SUB],
                            WoaT[:, c * D + n * 512: c * D + (n + 1) * 512],
                            start=(c == 0), stop=False))
                    b.op("tensor", lambda e, n=n, ps=ps: e.matmul(
                        ps[:, 0:512], sb["ones_row"][0:1, 0:SUB],
                        sb["bias_of"][0:1, n * 512:(n + 1) * 512],
                        start=False, stop=True), mm)
                b.wait("vector", mm)
                for n in (0, 1):
                    ps = (ps_w0, ps_w1)[n]
                    b.op("vector", lambda e, j=j, n=n, ps=ps: e.tensor_tensor(
                        x1_s[:, j * D + n * 512: j * D + (n + 1) * 512], ps[:, 0:512],
                        x_s[:, j * D + n * 512: j * D + (n + 1) * 512], op=OP.add), ve)
                    b.bdone(ps, bf)
                if self.dbg:
                    b.wait("gpsimd", ve)
                    b.dma("gpsimd", dram["dbg_x1"][tok0 + j * SUB: tok0 + (j + 1) * SUB, :],
                          x1_s[:, j * D:(j + 1) * D], st)
            self.emit_ln2(x1_s[:], xn_s[:], stats, mrneg, rstd)
            for j in (0, 1):
                self.emit_transpose_1024(xn_s[:, j * D:(j + 1) * D], xnT, j * SUB, SUP,
                                         psl, bf)
            if i + 1 < NSUP:
                ld_x = load_sup(i + 1, ve.n)

            b.wait("vector", st, st_x2)
            self.emit_ffn(xnT, x1_s, ps_t, ps_w0, ps_w1, tT_s,
                          lambda j: x2_s[:, j * D:(j + 1) * D], bf=bf)
            ve_x2 = ve.n
            b.wait("gpsimd", ve, ve_x2)
            for j in (0, 1):
                b.dma("gpsimd", dram["x2_sp"][tok0 + j * SUB: tok0 + (j + 1) * SUB, :],
                      x2_s[:, j * D:(j + 1) * D], st)
            st_x2 = st.n
        ph.close()

    def emit_phase_c2(self):
        b, sb, dram = self, self.sb, self.dram
        T, NSUP = self.T, self.NSUP
        ld, st, mm, ve, sc = (self.sems[k] for k in ("ld", "st", "mm", "ve", "sc"))
        b.sbuf("s_run", [SUB, 32 * 65])
        b.op("vector", lambda e: e.memset(sb["s_run"][:], 0.0))
        ph = contextlib.ExitStack()


        Wqb = b.sbuf("Wqb_s", [SUB, 2 * D], stack=ph)
        Wkb = b.sbuf("Wkb_s", [SUB, 2 * D], stack=ph)
        Wvb = b.sbuf("Wvb_s", [SUB, 8 * D], stack=ph)
        for c in range(8):
            b.dma("sync", Wqb[:, c * F2:(c + 1) * F2], dram["Wqb"][c * SUB:(c + 1) * SUB, :], ld)
            b.dma("sync", Wkb[:, c * F2:(c + 1) * F2], dram["Wkb"][c * SUB:(c + 1) * SUB, :], ld)
            b.dma("sync", Wvb[:, c * D:(c + 1) * D], dram["Wvb"][c * SUB:(c + 1) * SUB, :], ld)
        ld_w = ld.n

        x2_s = b.sbuf("c2_x2", [SUB, 2 * D], stack=ph)
        xn_s = b.sbuf("c2_xn", [SUB, 2 * D], stack=ph)
        xnT = b.sbuf("c2_xnT", [SUB, 2 * D], stack=ph)
        qT_s = b.sbuf("c2_qT", [SUB, 2 * SUP], stack=ph)
        kT_s = b.sbuf("c2_kT", [SUB, 2 * SUP], stack=ph)
        k_s = b.sbuf("c2_k", [SUB, 2 * F2], stack=ph)
        phk = b.sbuf("c2_phk", [SUB, 2 * HB * F2], stack=ph)
        vpb = b.sbuf("c2_vpb", [SUB, 1040], stack=ph)
        U_s = b.sbuf("c2_U", [SUB, 2080], stack=ph)
        S_loc = b.sbuf("c2_Sloc", [SUB, 2080], stack=ph)
        stats = b.sbuf("c2_stats", [SUB, 16], stack=ph)
        mrneg = b.sbuf("c2_mrneg", [SUB, 2], stack=ph)
        rstd = b.sbuf("c2_rstd", [SUB, 2], stack=ph)

        ps_a = b.psum("c2_ps_a", [SUB, 512], stack=ph)
        ps_b = b.psum("c2_ps_b", [SUB, 512], stack=ph)
        ps_tr = b.psum("c2_ps_tr", [SUB, 512], stack=ph)
        u_ps = [b.psum(f"c2_ps_u{k}", [SUB, 512], stack=ph) for k in range(5)]
        psl = (ps_tr, ps_a)
        bf = {}

        b.op("vector", lambda e: e.memset(S_loc[:], 0.0))
        for h in range(HB):
            b.op("vector", lambda e, h=h: e.memset(
                vpb[:, h * 65 + 64: h * 65 + 65], 1.0))
        b.op("vector", lambda e: e.engine_nop(), ve)

        qT_r = dram["qT_sp"].rearrange("(m p) t -> p m t", m=2)
        kT_r = dram["kT_sp"].rearrange("(m p) t -> p m t", m=2)
        st_sp = st.n
        mm_u = {0: 0, 1: 0}
        gp = self.sems["gp"]
        for j in (0, 1):
            b.dma("sync", x2_s[:, j * D:(j + 1) * D],
                  dram["x2_sp"][j * SUB:(j + 1) * SUB, :], ld)
        b.wait("vector", ld)
        self.emit_ln2(x2_s[:], xn_s[:], stats, mrneg, rstd)
        ve_ln = ve.n
        for i in range(NSUP):
            tok0 = i * SUP
            for j in (0, 1):
                self.emit_transpose_1024(xn_s[:, j * D:(j + 1) * D], xnT, j * SUB, SUP,
                                         psl, bf, ve_src=ve_ln)
            ve_xnT = ve.n

            b.wait("tensor", ld, ld_w)
            b.wait("tensor", ve, ve_xnT)

            def proj_qk(W, dst, scale):
                for mq in (0, 1):
                    ps = (ps_a, ps_b)[mq]
                    b.bwait(ps, bf)
                    for c in range(8):
                        b.op("tensor", lambda e, W=W, mq=mq, c=c, ps=ps: e.matmul(
                            ps[:, 0:SUP], W[:, (c * 2 + mq) * SUB:(c * 2 + mq + 1) * SUB],
                            xnT[:, c * SUP:(c + 1) * SUP],
                            start=(c == 0), stop=(c == 7)), mm if c == 7 else None)
                    b.wait("vector", mm)
                    if scale is None:
                        b.op("vector", lambda e, mq=mq, ps=ps, dst=dst: e.tensor_copy(
                            dst[:, mq * SUP:(mq + 1) * SUP], ps[:, 0:SUP]), ve)
                    else:
                        b.op("vector", lambda e, mq=mq, ps=ps, dst=dst, sc_=scale:
                             e.tensor_single_scalar(dst[:, mq * SUP:(mq + 1) * SUP],
                                                    ps[:, 0:SUP], sc_, op=OP.mult), ve)
                    b.bdone(ps, bf)

            proj_qk(Wkb, kT_s, None)
            gp_j = {}
            ve_phi = {}
            for j in (0, 1):
                b.wait("tensor", ve)
                b.bwait(ps_tr, bf)
                for mq in (0, 1):
                    b.op("tensor", lambda e, mq=mq, j=j: e.transpose(
                        ps_tr[:, mq * SUB:(mq + 1) * SUB],
                        kT_s[:, mq * SUP + j * SUB: mq * SUP + (j + 1) * SUB],
                        sb["ident"][:]), mm if mq == 1 else None)
                b.wait("vector", mm)
                b.op("vector", lambda e, j=j: e.tensor_copy(
                    k_s[:, j * F2:(j + 1) * F2], ps_tr[:, 0:256]), ve)
                b.bdone(ps_tr, bf)
                b.wait("gpsimd", ve)
                b.wait("gpsimd", mm, mm_u[j])
                b.wait("vector", mm, mm_u[j])
                for h in range(HB):
                    eng = "gpsimd" if h < 8 else "vector"
                    sem = gp if h < 8 else ve
                    b.op(eng, lambda e, j=j, h=h: e.tensor_tensor(
                        phk[:, j * HB * F2 + h * F2:
                            j * HB * F2 + (h + 1) * F2]
                        .rearrange("p (a c) -> p a c", c=FB),
                        k_s[:, j * F2 + h * FB: j * F2 + h * FB + FB]
                        .unsqueeze(2).broadcast_to([SUB, FB, FB]),
                        k_s[:, j * F2 + h * FB: j * F2 + h * FB + FB]
                        .unsqueeze(1).broadcast_to([SUB, FB, FB]),
                        op=OP.mult), sem)
                gp_j[j] = gp.n
                ve_phi[j] = ve.n
                if self.dbg:
                    b.wait("gpsimd", gp)
                    b.wait("gpsimd", ve)
                    b.dma("gpsimd", dram["dbg_phk"][tok0 + j * SUB: tok0 + (j + 1) * SUB, :],
                          phk[:, j * HB * F2:(j + 1) * HB * F2], st)

            proj_qk(Wqb, qT_s, QSCALE_B)
            b.wait("gpsimd", ve)
            b.wait("gpsimd", st, st_sp)
            b.dma("gpsimd", qT_r[:, :, tok0:tok0 + SUP],
                  qT_s[:].rearrange("p (m t) -> p m t", m=2), st)
            b.dma("gpsimd", kT_r[:, :, tok0:tok0 + SUP],
                  kT_s[:].rearrange("p (m t) -> p m t", m=2), st)
            ubank_first = [True] * 5
            for j in (0, 1):
                for n in (0, 1):
                    ps = (ps_a, ps_b)[n]
                    b.bwait(ps, bf)
                    for c in range(8):
                        b.op("tensor", lambda e, c=c, j=j, n=n, ps=ps: e.matmul(
                            ps[:, 0:512],
                            xnT[:, c * SUP + j * SUB: c * SUP + (j + 1) * SUB],
                            Wvb[:, c * D + n * 512: c * D + (n + 1) * 512],
                            start=(c == 0), stop=(c == 7)), mm if c == 7 else None)
                    b.wait("vector", mm)
                    b.op("vector", lambda e, n=n, ps=ps: e.tensor_copy(
                        vpb[:].rearrange("p (h c) -> p h c", h=16)
                        [:, 8 * n:8 * (n + 1), 0:64],
                        ps[:, 0:512].rearrange("p (h c) -> p h c", h=8)), ve)
                    b.bdone(ps, bf)
                ve_vpb = ve.n
                b.wait("gpsimd", ve, ve_vpb)
                b.dma("gpsimd", dram["vpb_sp"][i * 2 + j],
                      vpb[:], st)
                b.wait("tensor", gp, gp_j[j])
                b.wait("tensor", ve, max(ve_vpb, ve_phi[j]))
                for h in range(HB):
                    for fc in (0, 1):
                        s_ = h * 2 + fc
                        bank, colk = s_ // 7, (s_ % 7) * 65
                        first = ubank_first[bank] and j == 0
                        if j == 0 and ubank_first[bank]:
                            ubank_first[bank] = False
                        stop = (j == 1 and (s_ % 7 == 6 or s_ == 31))
                        b.op("tensor", lambda e, h=h, fc=fc, j=j, bank=bank, colk=colk,
                             first=first, stop=stop: e.matmul(
                            u_ps[bank][:, colk:colk + 65],
                            phk[:, j * HB * F2 + h * F2 + fc * SUB:
                                j * HB * F2 + h * F2 + (fc + 1) * SUB],
                            vpb[:, h * 65: (h + 1) * 65],
                            start=first, stop=stop),
                            mm if s_ == 31 else None)
                mm_u[j] = mm.n
                if j == 0 and i + 1 < NSUP:
                    b.wait("sync", ve, ve_ln)
                    for jj in (0, 1):
                        b.dma("sync", x2_s[:, jj * D:(jj + 1) * D],
                              dram["x2_sp"][tok0 + SUP + jj * SUB:
                                            tok0 + SUP + (jj + 1) * SUB, :], ld)
                    b.wait("vector", ld)
                    self.emit_ln2(x2_s[:], xn_s[:], stats, mrneg, rstd)
                    ve_ln = ve.n

            b.wait("vector", mm)
            b.wait("vector", st, st_sp)
            for bank in range(5):
                ncols = 455 if bank < 4 else 260
                b.op("vector", lambda e, bank=bank, ncols=ncols: e.tensor_copy(
                    U_s[:, bank * 455: bank * 455 + ncols], u_ps[bank][:, 0:ncols]),
                    ve if bank == 4 else None)
            b.wait("vector", ve)
            b.op("vector", lambda e: e.tensor_add(S_loc[:], S_loc[:], U_s[:]), ve)
            ve_u = ve.n
            b.wait("gpsimd", ve, ve_u)
            b.dma("gpsimd", dram["u_sp"][i], U_s[:], st)
            st_sp = st.n
            ve_free = ve.n

        b.wait("gpsimd", ve)
        b.dma("gpsimd", dram["s_cc_in"], S_loc[:], st)
        self.st_sloc = st.n
        ph.close()

    def emit_s_gather(self):
        b, sb, dram = self, self.sb, self.dram
        ld, st, ve, cc = (self.sems[k] for k in ("ld", "st", "ve", "cc"))
        if self.seq_cores > 1:
            b.op("gpsimd", lambda e: e.collective_compute(
                "AllGather", OP.bypass,
                ins=[dram["s_cc_in"]], outs=[dram["s_cc_out"]],
                replica_groups=[[g * self.seq_cores + r for r in range(self.seq_cores)]
                                for g in range(8 // self.seq_cores)]), cc)
            b.wait("sync", cc)
            phs_ = contextlib.ExitStack()
            blk = b.sbuf("s_blk", [SUB, 1040], stack=phs_)
            for r in range(self.seq_cores):
                for hf in (0, 1):
                    b.dma("sync", blk[:],
                          dram["s_cc_out"][r * SUB:(r + 1) * SUB,
                                           hf * 1040:(hf + 1) * 1040], ld)
                    b.wait("vector", ld)
                    b.op("vector", lambda e, r=r, hf=hf: e.scalar_tensor_tensor(
                        sb["s_run"][:, hf * 1040:(hf + 1) * 1040], blk[:],
                        sb["pm_s"][:, r:r + 1],
                        sb["s_run"][:, hf * 1040:(hf + 1) * 1040],
                        op0=OP.mult, op1=OP.add), ve)
                    b.wait("vector", ve)
                    b.wait("sync", ve)
            phs_.close()
        self.ve_srun = self.sems["ve"].n

    def emit_phase_e(self):
        b, sb, dram = self, self.sb, self.dram
        T, NSUP = self.T, self.NSUP
        ld, st, mm, ve, sc = (self.sems[k] for k in ("ld", "st", "mm", "ve", "sc"))
        ph = contextlib.ExitStack()

        Wob = b.sbuf("Wob_s", [SUB, 8 * D], stack=ph)
        for c in range(8):
            b.dma("sync", Wob[:, c * D:(c + 1) * D], dram["Wob"][c * SUB:(c + 1) * SUB, :], ld)
        WTe, CTe, Vre = sb["WT_e"], sb["CT_e"], sb["Vr_e"]

        x2e = b.sbuf("e_x2", [SUB, 2 * D], stack=ph)
        qTE = b.sbuf("e_qTE", [SUB, 4 * SUP], stack=ph)
        kTE = b.sbuf("e_kTE", [SUB, 4 * SUP], stack=ph)
        vpbe = b.sbuf("e_vpb", [SUB, 2 * 1040], stack=ph)
        Ue = b.sbuf("e_U", [SUB, 2080], stack=ph)
        rep1 = b.sbuf("e_rep1", [SUB, HB * 2 * SUB], stack=ph)
        rep2 = b.sbuf("e_rep2", [SUB, 8 * SUB], stack=ph)
        at_s = b.sbuf("e_at", [SUB, HB * SUB], stack=ph)
        ob_s = b.sbuf("e_ob", [SUB, D], stack=ph)
        obT = b.sbuf("e_obT", [SUB, D], stack=ph)
        x3_s = b.sbuf("e_x3", [SUB, 4 * D], stack=ph)
        xn_s = b.sbuf("e_xn", [SUB, 2 * D], stack=ph)
        xn4T = b.sbuf("e_xn4T", [SUB, 2 * D], stack=ph)
        x4_s = b.sbuf("e_x4", [SUB, D], stack=ph)
        tTe = b.sbuf("e_tT", [33, SUP], stack=ph)
        b.op("vector", lambda e: e.memset(tTe[32:33, :], 1.0))
        recip = b.sbuf("e_recip", [SUB, 16], stack=ph)
        scr_dv = b.sbuf("e_scrdv", [SUB, 1040], stack=ph)
        rb = b.sbuf("e_rb", [SUB, D], stack=ph)
        stats = b.sbuf("e_stats", [SUB, 16], stack=ph)
        mrneg = b.sbuf("e_mrneg", [SUB, 2], stack=ph)
        rstd = b.sbuf("e_rstd", [SUB, 2], stack=ph)

        ps_o0 = b.psum("e_ps_o0", [SUB, 512], stack=ph)
        ps_o1 = b.psum("e_ps_o1", [SUB, 512], stack=ph)
        ps_o2 = b.psum("e_ps_o2", [SUB, 512], stack=ph)
        at_ps0 = b.psum("e_ps_at0", [SUB, 512], stack=ph)
        at_ps1 = b.psum("e_ps_at1", [SUB, 512], stack=ph)
        ps_tr = b.psum("e_ps_tr", [SUB, 512], stack=ph)
        ps_w0 = b.psum("e_ps_w0", [SUB, 512], stack=ph)
        ps_w1 = b.psum("e_ps_w1", [SUB, 512], stack=ph)
        o_banks = [(ps_o0, 7), (ps_o1, 7), (ps_o2, 2)]
        psl = (ps_tr, ps_w0)
        bf = {}

        q_r1 = (dram["qT_sp"].rearrange("(h f2 a) t -> a h f2 t", h=16, f2=2)
                .unsqueeze(1).broadcast_to([8, 16, 16, 2, T]))
        q_r2c = dram["qT_sp"].rearrange("(h bb) t -> bb h t", h=16)

        def group_loads(i, g_mm, g_ve):
            tok0 = i * SUP
            if g_mm:
                b.wait("sync", mm, g_mm)
            if g_ve:
                b.wait("sync", ve, g_ve)
            for j in (0, 1):
                b.dma("sync", x2e[:, j * D:(j + 1) * D],
                      dram["x2_sp"][tok0 + j * SUB: tok0 + (j + 1) * SUB, :], ld)
                b.dma("sync", vpbe[:, j * 1040:(j + 1) * 1040],
                      dram["vpb_sp"][i * 2 + j], ld)
            for hh in range(4):
                for (dstE, srcd) in ((qTE, dram["qT_sp"]), (kTE, dram["kT_sp"])):
                    b.dma("sync",
                          dstE[hh * 32: hh * 32 + 16, :]
                          .rearrange("p (m t) -> p m t", m=4),
                          srcd[:, tok0:tok0 + SUP]
                          .rearrange("(m hf) t -> m hf t", hf=64)
                          [:, 16 * hh:16 * (hh + 1), :].transpose([1, 0, 2]), ld)
            b.dma("sync", Ue[:], dram["u_sp"][i], ld)
            return ld.n

        def phi_load(i, j, guard_mm):
            tok0 = i * SUP
            if guard_mm:
                b.wait("sync", mm, guard_mm)
            for a in range(8):
                b.dma("sync",
                      rep1[a * 16:(a + 1) * 16, :]
                      .rearrange("p (h f2 t) -> p h f2 t", h=16, f2=2),
                      q_r1[a, :, :, :, tok0 + j * SUB: tok0 + (j + 1) * SUB], ld)
            ld1 = ld.n
            for hh in (0, 1):
                for a in range(8):
                    b.dma("sync", rep2[a * 16:(a + 1) * 16, :] if hh == 0 else
                          rep2[a * 16:(a + 1) * 16, :],
                          q_r2c[:, 8 * hh:8 * (hh + 1),
                                tok0 + j * SUB: tok0 + (j + 1) * SUB], ld)
                ld_rep = ld.n
                b.wait("gpsimd", ld, ld_rep)
                gp = self.sems["gp"]
                for f2 in (0, 1):
                    b.op("gpsimd", lambda e, hh=hh, f2=f2: e.tensor_mul(
                        rep1[:].rearrange("p (h f2 t) -> p h f2 t", h=16, f2=2)
                        [:, 8 * hh:8 * (hh + 1), f2, :],
                        rep1[:].rearrange("p (h f2 t) -> p h f2 t", h=16, f2=2)
                        [:, 8 * hh:8 * (hh + 1), f2, :],
                        rep2[:].rearrange("p (h t) -> p h t", h=8)), gp)
                if hh == 0:
                    b.wait("sync", gp)
            return ("gp", self.sems["gp"].n)

        ld_g = group_loads(0, 0, 0)
        ve_phq = phi_load(0, 0, 0)
        self.emit_s_gather()
        st_out = st.n

        for i in range(NSUP):
            tok0 = i * SUP
            x3b = x3_s[:, (i % 2) * 2 * D:(i % 2) * 2 * D + 2 * D]
            mm_oi = {}
            for j in (0, 1):
                for (ps, nsl) in o_banks:
                    b.bwait(ps, bf)
                b.wait("tensor", self.sems[ve_phq[0]], ve_phq[1])
                b.wait("tensor", ld, ld_g)
                if i == 0 and j == 0:
                    b.wait("tensor", ve, self.ve_srun)
                slot = 0
                for (ps, nsl) in o_banks:
                    for s_ in range(nsl):
                        h = slot + s_
                        for fc in (0, 1):
                            b.op("tensor", lambda e, h=h, fc=fc, ps=ps, s_=s_: e.matmul(
                                ps[:, s_ * 65:(s_ + 1) * 65],
                                rep1[:, (2 * h + fc) * SUB:(2 * h + fc + 1) * SUB],
                                sb["s_run"][:, (2 * h + fc) * 65:(2 * h + fc + 1) * 65],
                                start=(s_ == 0 and fc == 0), stop=False),
                                mm if (h == HB - 1 and fc == 1) else None)
                    slot += nsl
                mm_oi[j] = mm.n
                if j == 0:
                    ve_phq = phi_load(i, 1, mm_oi[0])
                else:
                    self.emit_transpose_1024(xn_s[:, 0:D], xn4T, 0, SUP, psl, bf)
                for si in range(j + 1):
                    diag = (si == j)
                    sq_done = []
                    for h in range(HB):
                        r, m4 = (h % 4) * 32, h // 4
                        at_ps = (at_ps0, at_ps1)[h % 2]
                        if h >= 2:
                            b.wait("tensor", sc if not diag else ve, sq_done[h - 2])
                        b.op("tensor", lambda e, h=h, r=r, m4=m4, si=si, j=j,
                             at_ps=at_ps: e.matmul(
                            at_ps[:, 0:SUB],
                            kTE[r:r + 16, m4 * SUP + si * SUB: m4 * SUP + (si + 1) * SUB],
                            qTE[r:r + 16, m4 * SUP + j * SUB: m4 * SUP + (j + 1) * SUB],
                            start=True, stop=True, tile_position=(r, 0),
                            skip_group_check=(r > 0)), mm)
                        if diag:
                            b.wait("vector", mm)
                            b.op("vector", lambda e, h=h, at_ps=at_ps: e.tensor_tensor(
                                at_s[:, h * SUB:(h + 1) * SUB], at_ps[:, 0:SUB],
                                sb["triu"][:], op=OP.mult), ve)
                            b.wait("vector", ve)
                            b.op("vector", lambda e, h=h: e.tensor_mul(
                                at_s[:, h * SUB:(h + 1) * SUB],
                                at_s[:, h * SUB:(h + 1) * SUB],
                                at_s[:, h * SUB:(h + 1) * SUB]), ve)
                            sq_done.append(ve.n)
                        else:
                            b.wait("scalar", mm)
                            b.op("scalar", lambda e, h=h, at_ps=at_ps: e.activation(
                                at_s[:, h * SUB:(h + 1) * SUB], at_ps[:, 0:SUB],
                                AF.Square), sc)
                            sq_done.append(sc.n)
                    b.wait("tensor", sc if not diag else ve)
                    slot = 0
                    for (ps, nsl) in o_banks:
                        for s_ in range(nsl):
                            h = slot + s_
                            stop = (diag and s_ == nsl - 1)
                            b.op("tensor", lambda e, h=h, ps=ps, s_=s_, si=si,
                                 stop=stop: e.matmul(
                                ps[:, s_ * 65:(s_ + 1) * 65],
                                at_s[:, h * SUB:(h + 1) * SUB],
                                vpbe[:, si * 1040 + h * 65: si * 1040 + (h + 1) * 65],
                                start=False, stop=stop),
                                mm if stop and s_ == nsl - 1 and ps is ps_o2 else None)
                        slot += nsl
                mm_intra = mm.n
                self.emit_div_evac(o_banks, ob_s, recip, rb, scr_dv, EPS_B, bf=bf)
                if self.dbg:
                    b.wait("gpsimd", ve)
                    b.dma("gpsimd",
                          dram["dbg_onum"][tok0 + j * SUB: tok0 + (j + 1) * SUB, :],
                          scr_dv[:], st)
                    b.dma("gpsimd", dram["dbg_ob"][tok0 + j * SUB: tok0 + (j + 1) * SUB, :],
                          ob_s[:], st)
                self.emit_transpose_1024(ob_s, obT, 0, SUB, psl, bf)
                b.wait("tensor", ve)
                for n in (0, 1):
                    ps = (ps_w0, ps_w1)[n]
                    b.bwait(ps, bf)
                    for c in range(8):
                        b.op("tensor", lambda e, c=c, n=n, ps=ps: e.matmul(
                            ps[:, 0:512], obT[:, c * SUB:(c + 1) * SUB],
                            Wob[:, c * D + n * 512: c * D + (n + 1) * 512],
                            start=(c == 0), stop=(c == 7)), mm if c == 7 else None)
                b.wait("vector", mm)
                for n in (0, 1):
                    ps = (ps_w0, ps_w1)[n]
                    b.op("vector", lambda e, j=j, n=n, ps=ps, x3b=x3b: e.tensor_tensor(
                        x3b[:, j * D + n * 512: j * D + (n + 1) * 512], ps[:, 0:512],
                        x2e[:, j * D + n * 512: j * D + (n + 1) * 512], op=OP.add), ve)
                    b.bdone(ps, bf)
                if j == 1:
                    ve_x3 = ve.n
                if self.dbg:
                    b.wait("gpsimd", ve)
                    b.dma("gpsimd", dram["dbg_x3"][tok0 + j * SUB: tok0 + (j + 1) * SUB, :],
                          x3b[:, j * D:(j + 1) * D], st)
                if j == 0:
                    self.emit_ln(x3b[:, 0:D], xn_s[:, 0:D], stats,
                                 mrneg[:, 0:1], rstd[:, 0:1])

            self.emit_ln(x3b[:, D:2 * D], xn_s[:, D:2 * D], stats,
                         mrneg[:, 1:2], rstd[:, 1:2])
            self.emit_transpose_1024(xn_s[:, D:2 * D], xn4T, SUB, SUP, psl, bf)

            b.wait("vector", mm, mm_oi[1])
            b.wait("vector", ld, ld_g)
            b.op("vector", lambda e: e.tensor_add(sb["s_run"][:], sb["s_run"][:], Ue[:]),
                 ve)
            ve_sru = ve.n
            if self.dbg and i + 1 < NSUP:
                b.wait("gpsimd", ve)
                b.dma("gpsimd", dram["dbg_srun"][i + 1], sb["s_run"][:], st)

            if i + 1 < NSUP:
                ld_g = group_loads(i + 1, mm_intra, max(ve_x3, ve_sru))
                ve_phq = phi_load(i + 1, 0, mm_oi[1])

            self.emit_ffn_out(xn4T, x3b, at_ps0, ps_w0, ps_w1, tTe, x4_s,
                              (WTe, CTe, Vre), tok0, st, bf=bf)
        b.wait("gpsimd", st)
        ph.close()

    def emit_ffn_out(self, xnT, res_s, ps_t, ps_w0, ps_w1, tT_s, x4_s, wts, tok0, st,
                     bf=None):
        b, sb, dram = self, self.sb, self.dram
        mm, ve = self.sems["mm"], self.sems["ve"]
        WT, CT, Vr = wts
        if bf is None:
            bf = {}
        b.wait("tensor", ve)
        b.bwait(ps_t, bf)
        for c in range(8):
            b.op("tensor", lambda e, c=c: e.matmul(
                ps_t[0:32, 0:SUP], Vr[:, c * 32:(c + 1) * 32],
                xnT[:, c * SUP:(c + 1) * SUP], start=(c == 0), stop=(c == 7)),
                mm if c == 7 else None)
        b.wait("vector", mm)
        b.op("vector", lambda e: e.tensor_copy(tT_s[0:32, 0:SUP], ps_t[0:32, 0:SUP]), ve)
        b.bdone(ps_t, bf)
        b.wait("tensor", ve)
        for j in (0, 1):
            for n in (0, 1):
                ps = (ps_w0, ps_w1)[n]
                b.bwait(ps, bf)
                for c in range(8):
                    b.op("tensor", lambda e, c=c, j=j, n=n, ps=ps: e.matmul(
                        ps[:, 0:512], xnT[:, c * SUP + j * SUB: c * SUP + (j + 1) * SUB],
                        WT[:, c * D + n * 512: c * D + (n + 1) * 512],
                        start=(c == 0), stop=False))
                b.op("tensor", lambda e, j=j, n=n, ps=ps: e.matmul(
                    ps[:, 0:512], tT_s[0:33, j * SUB:(j + 1) * SUB],
                    CT[0:33, n * 512:(n + 1) * 512], start=False, stop=True), mm)
            b.wait("vector", mm)
            b.wait("vector", st)
            for n in (0, 1):
                ps = (ps_w0, ps_w1)[n]
                b.op("vector", lambda e, j=j, n=n, ps=ps: e.tensor_tensor(
                    x4_s[:, n * 512:(n + 1) * 512], ps[:, 0:512],
                    res_s[:, j * D + n * 512: j * D + (n + 1) * 512], op=OP.add), ve)
                b.bdone(ps, bf)
            b.wait("gpsimd", ve)
            b.dma("gpsimd", dram["out"][tok0 + j * SUB: tok0 + (j + 1) * SUB, :],
                  x4_s[:], st)


def build_program(T, seq_cores, dbg=False):
    return Builder(T, seq_cores, dbg=dbg).build()


def _host_layout(inputs, T_core, n_cores, seq_cores):
    A = np.ascontiguousarray
    x = inputs["L0_attn_a_x"]
    B, N, _ = x.shape
    w = {
        "Wqa": A(inputs["L0_attn_a_Wq"]), "Wka": A(inputs["L0_attn_a_Wk"]),
        "WvaT": A(inputs["L0_attn_a_Wv"].T), "WoaT": A(inputs["L0_attn_a_Wo"].T),
        "PT": A(inputs["L0_attn_a_P"].T),
        "bqc": A(inputs["L0_attn_a_bq"].reshape(HA, 64).T),
        "bkc": A(inputs["L0_attn_a_bk"].reshape(HA, 64).T),
        "bv": A(inputs["L0_attn_a_bv"].reshape(1, D)),
        "bo": A(inputs["L0_attn_a_bo"].reshape(1, D)),
        "WT": A(inputs["L0_ffn_W"].T), "bf": A(inputs["L0_ffn_b"].reshape(1, D)),
        "Vr": A(inputs["L0_ffn_V_r"]), "CT": A(inputs["L0_ffn_C"].T),
        "Wqb": A(inputs["L0_attn_b_Wq"]), "Wkb": A(inputs["L0_attn_b_Wk"]),
        "Wvb": A(inputs["L0_attn_b_Wv"]), "Wob": A(inputs["L0_attn_b_Wo"]),
    }
    in_maps = []
    for c in range(n_cores):
        bidx = c // seq_cores
        q = c % seq_cores
        m = dict(w)
        m["x"] = A(x[bidx, q * T_core:(q + 1) * T_core, :])
        m["pmask"] = A(np.asarray([[1.0 if j < q else 0.0 for j in range(4)]],
                                  dtype=np.float32))
        in_maps.append(m)
    return in_maps


def kernel(**inputs):
    n_cores = 8
    seq_cores = 4
    x = np.asarray(inputs["L0_attn_a_x"], dtype=np.float32)
    B, N, _ = x.shape
    T_core = (B * N) // n_cores
    nc = build_program(T_core, seq_cores)
    in_maps = _host_layout({k: np.asarray(v, dtype=np.float32)
                            for k, v in inputs.items()},
                           T_core, n_cores, seq_cores)
    res = run_bass_kernel_spmd(nc, in_maps, core_ids=list(range(n_cores)))
    full = np.empty((B, N, D), dtype=np.float32)
    for c in range(n_cores):
        bidx = c // seq_cores
        q = c % seq_cores
        full[bidx, q * T_core:(q + 1) * T_core, :] = res.results[c]["out"]
    return full


# version 18
# speedup vs baseline: 1.0355x; 1.0355x over previous
import contextlib
import numpy as np
import concourse.bass as bass
import concourse.mybir as mybir
from concourse.bass_utils import run_bass_kernel_spmd

F32 = mybir.dt.float32
I32 = mybir.dt.int32
AF = mybir.ActivationFunctionType
OP = mybir.AluOpType
AX = mybir.AxisListType

D = 1024
HA = 16
FA = 64
HB = 16
FB = 16
F2 = 256
HD = 64
SUB = 128
SUP = 256
LN_EPS = 1e-5
EPS_A = 1e-6
EPS_B = 1e-5
NORM_A = float(64.0 ** -0.25)
QSCALE_B = float(FB ** -0.5)

ENGINES = ("tensor", "vector", "scalar", "gpsimd", "sync")


class Prog:
    def __init__(self):
        self.streams = {e: [] for e in ENGINES}

    def emit(self, eng, fn):
        self.streams[eng].append(fn)

    def replay(self, eng_name):
        def run(eng):
            for fn in self.streams[eng_name]:
                fn(eng)
        return run


class Sem:
    def __init__(self, stack, nc, name):
        self.h = stack.enter_context(nc.semaphore(name))
        self.n = 0

    def bump(self, k):
        self.n += k
        return self.n


class Builder:
    def __init__(self, T, seq_cores, dbg=False, stop_after=None):
        self.T = T
        self.seq_cores = seq_cores
        self.dbg = dbg
        self.stop_after = stop_after
        self.NSUP = T // SUP
        self.nc = bass.Bass("TRN2", target_bir_lowering=False,
                            detect_race_conditions=False)
        self.stack = contextlib.ExitStack()
        self.prog = Prog()
        self.dram = {}
        self.sb = {}

    def dram_in(self, name, shape):
        self.dram[name] = self.nc.dram_tensor(name, list(shape), F32,
                                              kind="ExternalInput").ap()
        return self.dram[name]

    def dram_out(self, name, shape):
        self.dram[name] = self.nc.dram_tensor(name, list(shape), F32,
                                              kind="ExternalOutput").ap()
        return self.dram[name]

    def dram_tmp(self, name, shape):
        self.dram[name] = self.nc.dram_tensor(name, list(shape), F32,
                                              kind="Internal").ap()
        return self.dram[name]

    def sbuf(self, name, shape, dtype=F32, stack=None):
        t = (stack or self.stack).enter_context(
            self.nc.sbuf_tensor(name, list(shape), dtype))
        self.sb[name] = t
        return t

    def psum(self, name, shape, stack=None):
        assert shape[0] == SUB
        cols = ((shape[1] + 511) // 512) * 512
        t = (stack or self.stack).enter_context(
            self.nc.psum_tensor(name, [SUB, cols], F32))
        self.sb[name] = t
        return t

    def sem(self, name):
        return Sem(self.stack, self.nc, name)

    def dma(self, eng, out, in_, sem):
        sem.bump(16)
        self.prog.emit(eng, lambda e, out=out, in_=in_, s=sem: e.dma_start(
            out=out, in_=in_).then_inc(s.h, 16))
        return sem.n

    def wait(self, eng, sem, val=None):
        v = sem.n if val is None else val
        self.prog.emit(eng, lambda e, s=sem, v=v: e.wait_ge(s.h, v))

    def op(self, eng, fn, sem=None):
        if sem is None:
            self.prog.emit(eng, fn)
        else:
            sem.bump(1)
            self.prog.emit(eng, lambda e, fn=fn, s=sem: fn(e).then_inc(s.h, 1))

    def barrier(self):
        bar = self.bar
        self.wait("sync", self.sems["ld"])
        self.wait("gpsimd", self.sems["st"])
        target = bar.n + len(ENGINES)
        for e in ENGINES:
            bar.bump(1)
            self.prog.emit(e, lambda eng, s=bar: eng.sem_inc(s.h, 1))
        for e in ENGINES:
            self.prog.emit(e, lambda eng, s=bar, t=target: eng.wait_ge(s.h, t))

    def build(self):
        b, nc = self, self.nc
        T, NSUP = self.T, self.NSUP

        b.dram_in("x", [T, D])
        for nm, shp in [("Wqa", [D, D]), ("Wka", [D, D]), ("bqc", [64, HA]),
                        ("bkc", [64, HA]), ("WvaT", [D, D]),
                        ("WoaT", [D, D]), ("PT", [FA, FA]),
                        ("bv", [1, D]), ("bo", [1, D]),
                        ("WT", [D, D]), ("bf", [1, D]), ("Vr", [D, 32]), ("CT", [32, D]),
                        ("Wqb", [D, F2]), ("Wkb", [D, F2]), ("Wvb", [D, D]), ("Wob", [D, D]),
                        ("pmask", [1, 4])]:
            b.dram_in(nm, shp)
        b.dram_out("out", [T, D])

        mk_sp = b.dram_out if self.dbg else b.dram_tmp
        mk_sp("qfT_sp", [D, T])
        mk_sp("x2_sp", [T, D])
        mk_sp("qT_sp", [F2, T])
        mk_sp("kT_sp", [F2, T])
        b.dram_tmp("vpb_sp", [T // SUB, SUB, HB * 65])
        b.dram_tmp("u_sp", [NSUP, SUB, 32 * 65])
        b.dram_tmp("kv_cc_in", [SUB, 8 * 65])
        b.dram_tmp("s_cc_in", [SUB, 32 * 65])
        if self.seq_cores > 1:
            b.dram_tmp("kv_cc_out", [self.seq_cores * SUB, 8 * 65])
            b.dram_tmp("s_cc_out", [self.seq_cores * SUB, 32 * 65])
        if self.dbg:
            for nm in ("qfT_sp", "x2_sp", "qT_sp", "kT_sp"):
                pass
            for nm, shp in [("dbg_xn1", [T, D]), ("dbg_kv", [SUB, 8 * 65]),
                            ("dbg_x1", [T, D]), ("dbg_phk", [T, HB * F2]),
                            ("dbg_ob", [T, D]), ("dbg_x3", [T, D]),
                            ("dbg_onum", [T, HB * 65]),
                            ("dbg_phq", [T, HB * F2]),
                            ("dbg_srun", [NSUP, SUB, 32 * 65]),
                            ("dbg_r1raw", [SUB, HB * 2 * SUB]),
                            ("dbg_r2", [2, SUB, 8 * SUB])]:
                b.dram_out(nm, shp)

        b.sbuf("WT_e", [SUB, 8 * D])
        b.sbuf("CT_e", [33, D])
        b.sbuf("Vr_e", [SUB, 8 * 32])
        b.sbuf("ident", [SUB, SUB])
        b.sbuf("triu", [SUB, SUB])
        
        b.sbuf("ones_row", [1, 256])
        b.sbuf("pm_s", [SUB, 4])
        b.sbuf("normPT2", [SUB, FA])
        b.sbuf("bias_of", [1, 2 * D])

        self.sems = {k: self.sem("s_" + k) for k in
                     ("ld", "st", "mm", "ve", "sc", "gp", "cc")}
        self.bar = self.sem("s_bar")

        self.emit_init()
        phases = [("A", self.emit_phase_a),
                  ("C1", self.emit_phase_c1), ("C2", self.emit_phase_c2),
                  ("E", self.emit_phase_e)]
        barrier_before = {"C1", "C2", "E"}
        done = []
        for nm, fn in phases:
            if nm in barrier_before:
                self.barrier()
            fn()
            done.append(nm)
            if self.stop_after == nm:
                break
        if self.stop_after is not None and "E" not in done:
            self.barrier()
            zt = self.sbuf("z_fill", [SUB, D])
            ve, st = self.sems["ve"], self.sems["st"]
            self.op("vector", lambda e: e.memset(zt[:], 0.0), ve)
            self.wait("gpsimd", ve)
            for r in range(T // SUB):
                self.dma("gpsimd", self.dram["out"][r * SUB:(r + 1) * SUB, :], zt[:], st)
            self.wait("gpsimd", st)

        with self.stack:
            with nc.Block() as block:
                block.tensor(self.prog.replay("tensor"))
                block.vector(self.prog.replay("vector"))
                block.scalar(self.prog.replay("scalar"))
                block.gpsimd(self.prog.replay("gpsimd"))
                block.sync(self.prog.replay("sync"))
        return nc

    def emit_init(self):
        b, sb, dram = self, self.sb, self.dram
        ld, ve, gp = self.sems["ld"], self.sems["ve"], self.sems["gp"]

        b.op("gpsimd", lambda e: e.iota(idx32[:], pattern=[[1, SUB]], base=0,
                                        channel_multiplier=-1), gp)
        b.wait("vector", gp)
        b.op("vector", lambda e: e.tensor_single_scalar(sb["ident"][:], idx32[:],
                                                        0, op=OP.is_equal))
        b.op("vector", lambda e: e.tensor_single_scalar(sb["triu"][:], idx32[:],
                                                        0, op=OP.is_ge))
        b.op("vector", lambda e: e.memset(sb["ones_row"][:], 1.0))

        b.dma("sync", sb["pm_s"][:], dram["pmask"].partition_broadcast(SUB), ld)
        for i, nm in enumerate(["bo", "bf"]):
            b.dma("sync", sb["bias_of"][0:1, i * D:(i + 1) * D], dram[nm], ld)
        insc = contextlib.ExitStack()
        idx32 = b.sbuf("idx32", [SUB, SUB], I32, stack=insc)
        pt_raw = b.sbuf("pt_raw", [FA, FA], stack=insc)
        self._init_scope = insc
        b.dma("sync", pt_raw[:], dram["PT"], ld)
        b.wait("vector", ld)
        for half in (0, 1):
            b.op("vector", lambda e, h=half: e.tensor_single_scalar(
                sb["normPT2"][h * 64:(h + 1) * 64, :], pt_raw[:], NORM_A, op=OP.mult),
                ve)

    def emit_ln(self, x_ap, xn_ap, stats, mrneg, rstd, bump_ve=True):
        b = self
        ve, sc = self.sems["ve"], self.sems["sc"]
        b.wait("vector", sc)
        b.wait("vector", ve)
        b.op("vector", lambda e: e.reduce_sum(stats[:, 0:1], x_ap, axis=AX.X), ve)
        b.wait("vector", ve)
        b.wait("scalar", ve)
        b.op("scalar", lambda e: e.activation(xn_ap, x_ap, AF.Square,
                                              accum_out=stats[:, 1:2]), sc)
        b.wait("vector", sc)
        b.op("vector", lambda e: e.tensor_scalar_mul(stats[:, 2:3], stats[:, 0:1],
                                                     1.0 / D))
        b.op("vector", lambda e: e.tensor_scalar_mul(stats[:, 3:4], stats[:, 1:2],
                                                     1.0 / D), ve)
        b.wait("vector", ve)
        b.op("vector", lambda e: e.tensor_mul(stats[:, 4:5], stats[:, 2:3],
                                              stats[:, 2:3]), ve)
        b.wait("vector", ve)
        b.op("vector", lambda e: e.scalar_tensor_tensor(
            stats[:, 5:6], stats[:, 4:5], -1.0, stats[:, 3:4],
            op0=OP.mult, op1=OP.add), ve)
        b.wait("vector", ve)
        b.op("vector", lambda e: e.tensor_scalar_add(stats[:, 6:7], stats[:, 5:6],
                                                     LN_EPS), ve)
        b.wait("scalar", ve)
        b.op("scalar", lambda e: e.activation(stats[:, 7:8], stats[:, 6:7],
                                              AF.Sqrt), sc)
        b.wait("vector", sc)
        b.op("vector", lambda e: e.reciprocal(rstd[:], stats[:, 7:8]), ve)
        b.wait("vector", ve)
        b.op("vector", lambda e: e.scalar_tensor_tensor(
            mrneg[:], stats[:, 2:3], -1.0, rstd[:], op0=OP.mult, op1=OP.mult), ve)
        b.wait("vector", ve)
        b.op("vector", lambda e: e.tensor_scalar(
            xn_ap, x_ap, rstd[:], mrneg[:], op0=OP.mult, op1=OP.add),
            ve if bump_ve else None)

    def emit_ln2(self, x2_ap, xn2_ap, stats, mrneg, rstd, sq=None):
        b = self
        ve, sc = self.sems["ve"], self.sems["sc"]
        sq0 = sq if sq is not None else xn2_ap[:, 0:D]
        sq1 = sq if sq is not None else xn2_ap[:, D:2 * D]
        b.wait("vector", sc)
        b.wait("vector", ve)
        b.op("vector", lambda e: e.reduce_sum(stats[:, 0:1], x2_ap[:, 0:D], axis=AX.X))
        b.op("vector", lambda e: e.reduce_sum(stats[:, 1:2], x2_ap[:, D:2 * D],
                                              axis=AX.X), ve)
        b.wait("vector", ve)
        b.wait("scalar", ve)
        b.op("scalar", lambda e: e.activation(sq0, x2_ap[:, 0:D], AF.Square,
                                              accum_out=stats[:, 2:3]))
        b.op("scalar", lambda e: e.activation(sq1, x2_ap[:, D:2 * D],
                                              AF.Square, accum_out=stats[:, 3:4]), sc)
        b.wait("vector", sc)
        b.op("vector", lambda e: e.tensor_scalar_mul(stats[:, 4:6], stats[:, 0:2],
                                                     1.0 / D))
        b.op("vector", lambda e: e.tensor_scalar_mul(stats[:, 6:8], stats[:, 2:4],
                                                     1.0 / D), ve)
        b.wait("vector", ve)
        b.op("vector", lambda e: e.tensor_mul(stats[:, 8:10], stats[:, 4:6],
                                              stats[:, 4:6]), ve)
        b.wait("vector", ve)
        b.op("vector", lambda e: e.scalar_tensor_tensor(
            stats[:, 10:12], stats[:, 8:10], -1.0, stats[:, 6:8],
            op0=OP.mult, op1=OP.add), ve)
        b.wait("vector", ve)
        b.op("vector", lambda e: e.tensor_scalar_add(stats[:, 12:14], stats[:, 10:12],
                                                     LN_EPS), ve)
        b.wait("scalar", ve)
        b.op("scalar", lambda e: e.activation(stats[:, 14:16], stats[:, 12:14],
                                              AF.Sqrt), sc)
        b.wait("vector", sc)
        b.op("vector", lambda e: e.reciprocal(rstd[:, 0:2], stats[:, 14:16]), ve)
        b.wait("vector", ve)
        b.op("vector", lambda e: e.scalar_tensor_tensor(
            mrneg[:, 0:2], stats[:, 4:6], -1.0, rstd[:, 0:2],
            op0=OP.mult, op1=OP.mult), ve)
        b.wait("vector", ve)
        for j in (0, 1):
            b.op("vector", lambda e, j=j: e.tensor_scalar(
                xn2_ap[:, j * D:(j + 1) * D], x2_ap[:, j * D:(j + 1) * D],
                rstd[:, j:j + 1], mrneg[:, j:j + 1], op0=OP.mult, op1=OP.add), ve)

    def emit_transpose_1024(self, src, dst, off, stride, psl, bank_free, ve_src=None):
        b, sb = self, self.sb
        mm, ve = self.sems["mm"], self.sems["ve"]
        dstv = dst[:].rearrange("p (c t) -> p c t", t=stride)
        b.wait("tensor", ve, ve_src)
        for g in (0, 1):
            ps = psl[g]
            key = id(ps)
            v = bank_free.get(key)
            if v:
                if isinstance(v, tuple):
                    b.wait("tensor", self.sems[v[0]], v[1])
                else:
                    b.wait("tensor", ve, v)
            for cc in range(4):
                c = g * 4 + cc
                b.op("tensor", lambda e, c=c, cc=cc, ps=ps: e.transpose(
                    ps[:, cc * SUB:(cc + 1) * SUB], src[:, c * SUB:(c + 1) * SUB],
                    sb["ident"][:]), mm if cc == 3 else None)
            b.wait("vector", mm)
            b.op("vector", lambda e, g=g, ps=ps: e.tensor_copy(
                dstv[:, 4 * g:4 * g + 4, off // 1:off // 1 + SUB]
                if stride != SUB else dst[:, g * 512:(g + 1) * 512],
                ps[:, 0:512].rearrange("p (c t) -> p c t", t=SUB)
                if stride != SUB else ps[:, 0:512]), ve)
            bank_free[key] = ve.n

    def bwait(self, ps, bf):
        v = bf.get(id(ps))
        if v:
            if isinstance(v, tuple):
                self.wait("tensor", self.sems[v[0]], v[1])
            else:
                self.wait("tensor", self.sems["ve"], v)

    def bdone(self, ps, bf):
        bf[id(ps)] = self.sems["ve"].n

    def emit_phase_a(self):
        b, sb, dram = self, self.sb, self.dram
        T, NSUP = self.T, self.NSUP
        ld, st, mm, ve, sc = (self.sems[k] for k in ("ld", "st", "mm", "ve", "sc"))
        self._init_scope.close()
        ph = contextlib.ExitStack()
        b.wait("sync", self.sems["ve"])

        WpqT = b.sbuf("WpqT_s", [SUB, 8 * D], stack=ph)
        WpkT = b.sbuf("WpkT_s", [SUB, 8 * D], stack=ph)
        WvaT = b.sbuf("WvaT_s", [SUB, 8 * D], stack=ph)
        bpq = b.sbuf("bpq_s", [SUB, 8], stack=ph)
        bpk = b.sbuf("bpk_s", [SUB, 8], stack=ph)
        for c in range(8):
            b.dma("sync", WvaT[:, c * D:(c + 1) * D], dram["WvaT"][c * SUB:(c + 1) * SUB, :], ld)
        ld_w = ld.n
        for c in range(8):
            b.dma("sync", sb["WT_e"][:, c * D:(c + 1) * D],
                  dram["WT"][c * SUB:(c + 1) * SUB, :], ld)
            b.dma("sync", sb["Vr_e"][:, c * 32:(c + 1) * 32],
                  dram["Vr"][c * SUB:(c + 1) * SUB, :], ld)
        b.dma("sync", sb["CT_e"][0:32, :], dram["CT"], ld)
        b.dma("sync", sb["CT_e"][32:33, :], dram["bf"], ld)
        ld_ffnw = ld.n

        x_s = b.sbuf("a_x", [SUB, 2 * D], stack=ph)
        xn_s = b.sbuf("a_xn", [SUB, 2 * D], stack=ph)
        xnT = b.sbuf("a_xnT", [SUB, 2 * D], stack=ph)
        QfT = b.sbuf("a_QfT", [SUB, 8 * SUP], stack=ph)
        KfT = b.sbuf("a_KfT", [SUB, 8 * SUP], stack=ph)
        Kf = b.sbuf("a_Kf", [SUB, D], stack=ph)
        vpa = b.sbuf("a_vpa", [SUB, HA * 65], stack=ph)
        stats = b.sbuf("a_stats", [SUB, 16], stack=ph)
        mrneg = b.sbuf("a_mrneg", [SUB, 2], stack=ph)
        rstd = b.sbuf("a_rstd", [SUB, 2], stack=ph)
        kv_s = b.sbuf("a_kv", [SUB, 8 * 65], stack=ph)
        bias_qkv = b.sbuf("bias_qkv", [1, D], stack=ph)
        b.dma("sync", bias_qkv[0:1, 0:D], dram["bv"], ld)

        for h in range(HA):
            b.op("vector", lambda e, h=h: e.memset(
                vpa[:, h * 65 + 64: h * 65 + 65], 1.0))
        b.op("vector", lambda e: e.engine_nop(), ve)

        ps_p0 = b.psum("ps_p0", [SUB, 512], stack=ph)
        ps_p1 = b.psum("ps_p1", [SUB, 512], stack=ph)
        ps_tr = b.psum("ps_tr", [SUB, 512], stack=ph)
        ps_v0 = b.psum("ps_v0", [SUB, 512], stack=ph)
        ps_v1 = b.psum("ps_v1", [SUB, 512], stack=ph)
        kv_psl = [b.psum(f"kv_ps{k}", [SUB, 512], stack=ph) for k in range(3)]
        ps_f = ps_p0
        bf = {}

        pre = contextlib.ExitStack()
        tmp64 = b.sbuf("a_tmp64", [64, D], stack=pre)
        bc64 = b.sbuf("a_bc64", [64, HA], stack=pre)
        prot = (ps_p0, ps_p1)
        for wi, (wd, WpT, bd, bp) in enumerate(
                ((dram["Wqa"], WpqT, dram["bqc"], bpq),
                 (dram["Wka"], WpkT, dram["bkc"], bpk))):
            for h in range(HA):
                b.wait("sync", ve)
                b.dma("sync", tmp64[:], wd[h * 64:(h + 1) * 64, :], ld)
                ps = prot[h % 2]
                b.bwait(ps, bf)
                b.wait("tensor", ld)
                for c in range(8):
                    b.op("tensor", lambda e, c=c, ps=ps: e.matmul(
                        ps[:, c * 64:(c + 1) * 64], tmp64[:, c * SUB:(c + 1) * SUB],
                        sb["normPT2"][0:64, :], start=(c == 0), stop=(c == 7)),
                        mm if c == 7 else None)
                b.wait("vector", mm)
                off = (h // 2) * SUB + (h % 2) * 64
                b.op("vector", lambda e, off=off, ps=ps, WpT=WpT: e.tensor_copy(
                    WpT[:].rearrange("p (c t) -> p c t", t=8 * SUB)
                    [:, :, off:off + 64],
                    ps[:, 0:512].rearrange("p (c t) -> p c t", t=64)), ve)
                b.bdone(ps, bf)
            b.wait("sync", ve)
            b.dma("sync", bc64[:], bd, ld)
            b.wait("tensor", ld)
            b.bwait(ps_f, bf)
            b.op("tensor", lambda e, bp=bp: e.matmul(
                ps_f[0:64, 0:HA], sb["normPT2"][0:64, :], bc64[:],
                start=True, stop=True), mm)
            b.wait("vector", mm)
            for half in (0, 1):
                b.op("vector", lambda e, half=half, bp=bp: e.tensor_copy(
                    bp[half * 64:(half + 1) * 64, :],
                    ps_f[0:64, 0:HA].rearrange("p (m two) -> p m two", two=2)
                    [:, :, half]), ve)
            b.wait("vector", ve)
            b.op("vector", lambda e, bp=bp: e.tensor_scalar_add(bp[:], bp[:], EPS_A),
                 ve)
            b.bdone(ps_f, bf)
        pre.close()

        kv_first = [True] * 3

        def kv_pair(p):
            return (kv_psl[p // 3], (p % 3) * 130)

        st_qf = st.n
        prot3 = (ps_p0, ps_p1)
        for j in (0, 1):
            b.dma("sync", x_s[:, j * D:(j + 1) * D],
                  dram["x"][j * SUB:(j + 1) * SUB, :], ld)
        b.wait("vector", ld)
        self.emit_ln2(x_s[:], xn_s[:], stats, mrneg, rstd)
        ve_ln = ve.n
        for i in range(NSUP):
            tok0 = i * SUP
            if self.dbg:
                b.wait("gpsimd", ve, ve_ln)
                for j in (0, 1):
                    b.dma("gpsimd", dram["dbg_xn1"][tok0 + j * SUB: tok0 + (j + 1) * SUB, :],
                          xn_s[:, j * D:(j + 1) * D], st)
            for j in (0, 1):
                self.emit_transpose_1024(xn_s[:, j * D:(j + 1) * D], xnT, j * SUB, SUP,
                                         (ps_tr, ps_v0), bf, ve_src=ve_ln)
            ve_xnT = ve.n
            b.wait("tensor", ld, ld_w)
            b.wait("tensor", ve, ve_xnT)
            for (W, dst, bp) in ((WpqT, QfT, bpq), (WpkT, KfT, bpk)):
                for m in range(8):
                    ps = prot3[m % 2]
                    b.bwait(ps, bf)
                    for c in range(8):
                        b.op("tensor", lambda e, W=W, m=m, c=c, ps=ps: e.matmul(
                            ps[:, 0:SUP], W[:, (c * 8 + m) * SUB:(c * 8 + m + 1) * SUB],
                            xnT[:, c * SUP:(c + 1) * SUP],
                            start=(c == 0), stop=(c == 7)), mm if c == 7 else None)
                    b.wait("vector", mm)
                    if dst is QfT:
                        b.wait("vector", st, st_qf)
                    b.op("vector", lambda e, m=m, ps=ps, dst=dst, bp=bp: e.tensor_scalar(
                        dst[:, m * SUP:(m + 1) * SUP], ps[:, 0:SUP],
                        bp[:, m:m + 1], EPS_A, op0=OP.add, op1=OP.max), ve)
                    b.bdone(ps, bf)
            ve_qf = ve.n
            b.wait("gpsimd", ve, ve_qf)
            for m in range(8):
                b.dma("gpsimd", dram["qfT_sp"][m * SUB:(m + 1) * SUB, tok0:tok0 + SUP],
                      QfT[:, m * SUP:(m + 1) * SUP], st)
            st_qf = st.n

            if i + 1 < NSUP:
                b.wait("sync", ve, ve_ln)
                for j in (0, 1):
                    b.dma("sync", x_s[:, j * D:(j + 1) * D],
                          dram["x"][tok0 + SUP + j * SUB: tok0 + SUP + (j + 1) * SUB, :],
                          ld)
                b.wait("vector", ld)
                self.emit_ln2(x_s[:], xn_s[:], stats, mrneg, rstd)
                ve_ln = ve.n

            b.wait("tensor", ve, ve_qf)
            for j in (0, 1):
                for g in (0, 1):
                    ps = (ps_tr, ps_v0)[g]
                    b.bwait(ps, bf)
                    for cc in range(4):
                        m = g * 4 + cc
                        b.op("tensor", lambda e, m=m, cc=cc, j=j, ps=ps: e.transpose(
                            ps[:, cc * SUB:(cc + 1) * SUB],
                            KfT[:, m * SUP + j * SUB: m * SUP + (j + 1) * SUB],
                            sb["ident"][:]), mm if cc == 3 else None)
                    b.wait("vector", mm)
                    b.op("vector", lambda e, g=g, ps=ps: e.tensor_copy(
                        Kf[:, g * 512:(g + 1) * 512], ps[:, 0:512]), ve)
                    b.bdone(ps, bf)
                for n in (0, 1):
                    ps = (ps_v0, ps_v1)[n]
                    b.bwait(ps, bf)
                    for c in range(8):
                        b.op("tensor", lambda e, c=c, j=j, n=n, ps=ps: e.matmul(
                            ps[:, 0:512],
                            xnT[:, c * SUP + j * SUB: c * SUP + (j + 1) * SUB],
                            WvaT[:, c * D + n * 512: c * D + (n + 1) * 512],
                            start=(c == 0), stop=False))
                    b.op("tensor", lambda e, n=n, ps=ps: e.matmul(
                        ps[:, 0:512], sb["ones_row"][0:1, 0:SUB],
                        bias_qkv[0:1, n * 512:(n + 1) * 512],
                        start=False, stop=True), mm)
                    b.wait("vector", mm)
                    b.op("vector", lambda e, n=n, ps=ps: e.tensor_copy(
                        vpa[:].rearrange("p (h c) -> p h c", h=16)
                        [:, 8 * n:8 * (n + 1), 0:64],
                        ps[:, 0:512].rearrange("p (h c) -> p h c", h=8)), ve)
                    b.bdone(ps, bf)
                b.wait("tensor", ve)
                for p in range(8):
                    ps, col = kv_pair(p)
                    bank = p // 3
                    first = kv_first[bank]
                    kv_first[bank] = False
                    stop = (i == NSUP - 1 and j == 1 and
                            (p % 3 == 2 or p == 7))
                    last_sup = (j == 1 and p == 7)
                    b.op("tensor", lambda e, p=p, ps=ps, col=col, first=first,
                         stop=stop: e.matmul(
                        ps[:, col:col + 130],
                        Kf[:, p * SUB:(p + 1) * SUB],
                        vpa[:, 2 * p * 65: 2 * p * 65 + 130],
                        start=first, stop=stop),
                        mm if last_sup else None)

        b.wait("vector", mm)
        for bi_ in range(3):
            npair = (3, 3, 2)[bi_]
            base = bi_ * 3
            b.op("vector", lambda e, bi_=bi_, npair=npair, base=base: e.tensor_copy(
                kv_s[0:64, base * 65: (base + npair) * 65]
                .rearrange("p (s c) -> p s c", c=65),
                kv_psl[bi_][0:64, 0:npair * 130]
                .rearrange("p (s c) -> p s c", c=130)[:, :, 0:65]))
            b.op("vector", lambda e, bi_=bi_, npair=npair, base=base: e.tensor_copy(
                kv_s[64:128, base * 65: (base + npair) * 65]
                .rearrange("p (s c) -> p s c", c=65),
                kv_psl[bi_][64:128, 0:npair * 130]
                .rearrange("p (s c) -> p s c", c=130)[:, :, 65:130]),
                ve if bi_ == 2 else None)
        ve_kv = ve.n
        b.wait("gpsimd", ve, ve_kv)
        b.dma("gpsimd", dram["kv_cc_in"], kv_s[:], st)
        if self.dbg:
            b.dma("gpsimd", dram["dbg_kv"], kv_s[:], st)
        self.st_kv = st.n
        ph.close()

    def emit_kv_gather(self):
        b, sb, dram = self, self.sb, self.dram
        ld, st, ve, cc = (self.sems[k] for k in ("ld", "st", "ve", "cc"))
        kvscope = contextlib.ExitStack()
        b.sbuf("kvg", [SUB, 8 * 65], stack=kvscope)
        if self.seq_cores > 1:
            b.op("gpsimd", lambda e: e.collective_compute(
                "AllGather", OP.bypass,
                ins=[dram["kv_cc_in"]], outs=[dram["kv_cc_out"]],
                replica_groups=[[g * self.seq_cores + r for r in range(self.seq_cores)]
                                for g in range(8 // self.seq_cores)]), cc)
            b.wait("sync", cc)
            phk_ = contextlib.ExitStack()
            tmp = b.sbuf("kv_blk", [SUB, 4 * 520], stack=phk_)
            for r in range(self.seq_cores):
                b.dma("sync", tmp[:, r * 520:(r + 1) * 520],
                      dram["kv_cc_out"][r * SUB:(r + 1) * SUB, :], ld)
            b.wait("vector", ld)
            b.op("vector", lambda e: e.tensor_add(sb["kvg"][:], tmp[:, 0:520],
                                                  tmp[:, 520:1040]))
            if self.seq_cores == 4:
                b.op("vector", lambda e: e.tensor_add(sb["kvg"][:], sb["kvg"][:],
                                                      tmp[:, 1040:1560]))
                b.op("vector", lambda e: e.tensor_add(sb["kvg"][:], sb["kvg"][:],
                                                      tmp[:, 1560:2080]))
            b.op("vector", lambda e: e.engine_nop(), ve)
            phk_.close()
        else:
            b.dma("sync", sb["kvg"][:], dram["kv_cc_in"], ld)
            b.wait("vector", ld)
            b.op("vector", lambda e: e.engine_nop(), ve)
        b.op("vector", lambda e: e.memset(sb["kvg_z"][:], 0.0), ve)
        b.wait("vector", ve)
        for h in range(HA):
            b.op("vector", lambda e, h=h: e.tensor_copy(
                sb["kvg_z"][64 * (h % 2):64 * (h % 2) + 64, h * 65:(h + 1) * 65],
                sb["kvg"][64 * (h % 2):64 * (h % 2) + 64,
                          (h // 2) * 65:(h // 2) * 65 + 65]),
                ve if h == HA - 1 else None)
        self.ve_kvg = ve.n
        kvscope.close()

    def emit_div_evac(self, o_banks, o_s, recip, rb, scr, eps, bf=None):
        b = self
        ve, mm = self.sems["ve"], self.sems["mm"]
        b.wait("vector", mm)
        col = 0
        scol = 0
        for (ps, nsl) in o_banks:
            b.op("vector", lambda e, ps=ps, n=nsl, c=col, eps=eps: e.tensor_scalar_add(
                recip[:, c:c + n],
                ps[:, 0:n * 65].rearrange("p (s c) -> p s c", c=65)[:, :, 64], eps))
            b.op("vector", lambda e, ps=ps, n=nsl, sc_=scol: e.tensor_copy(
                scr[:, sc_:sc_ + n * 65], ps[:, 0:n * 65]), ve)
            col += nsl
            scol += nsl * 65
        if bf is not None:
            for (ps, nsl) in o_banks:
                b.bdone(ps, bf)
        b.op("vector", lambda e: e.engine_nop(), ve)
        b.wait("vector", ve)
        b.op("vector", lambda e: e.reciprocal(recip[:, 0:16], recip[:, 0:16]), ve)
        b.wait("vector", ve)
        b.op("vector", lambda e: e.tensor_copy(
            rb[:].rearrange("p (s f) -> p s f", f=64),
            recip[:, 0:16].unsqueeze(2).broadcast_to([SUB, 16, 64])), ve)
        b.wait("vector", ve)
        col = 0
        scol = 0
        for (ps, nsl) in o_banks:
            b.op("vector", lambda e, n=nsl, c=col, sc_=scol: e.tensor_tensor(
                o_s[:, c * 64:(c + n) * 64].rearrange("p (s f) -> p s f", f=64),
                scr[:, sc_:sc_ + n * 65].rearrange("p (s c) -> p s c", c=65)[:, :, 0:64],
                rb[:, c * 64:(c + n) * 64].rearrange("p (s f) -> p s f", f=64),
                op=OP.mult), ve)
            col += nsl
            scol += nsl * 65

    def emit_ffn(self, xnT, res_s, ps_t, ps_w0, ps_w1, tT_s, cb, wts=None, bf=None,
                 ve_ready=None, mid_cb=None):
        b, sb = self, self.sb
        mm, ve = self.sems["mm"], self.sems["ve"]
        WT, CT, Vr = wts if wts is not None else (sb["WT_e"], sb["CT_e"], sb["Vr_e"])
        if bf is None:
            bf = {}
        b.wait("tensor", ve, ve_ready)
        b.bwait(ps_t, bf)
        for c in range(8):
            b.op("tensor", lambda e, c=c: e.matmul(
                ps_t[0:32, 0:SUP], Vr[:, c * 32:(c + 1) * 32],
                xnT[:, c * SUP:(c + 1) * SUP], start=(c == 0), stop=(c == 7)),
                mm if c == 7 else None)
        b.wait("vector", mm)
        b.op("vector", lambda e: e.tensor_copy(tT_s[0:32, 0:SUP], ps_t[0:32, 0:SUP]), ve)
        b.bdone(ps_t, bf)
        ve_tT = ve.n
        if mid_cb is not None:
            mid_cb()
        b.wait("tensor", ve, ve_tT)
        for j in (0, 1):
            for n in (0, 1):
                ps = (ps_w0, ps_w1)[n]
                b.bwait(ps, bf)
                for c in range(8):
                    b.op("tensor", lambda e, c=c, j=j, n=n, ps=ps: e.matmul(
                        ps[:, 0:512], xnT[:, c * SUP + j * SUB: c * SUP + (j + 1) * SUB],
                        WT[:, c * D + n * 512: c * D + (n + 1) * 512],
                        start=(c == 0), stop=False))
                b.op("tensor", lambda e, j=j, n=n, ps=ps: e.matmul(
                    ps[:, 0:512], tT_s[0:33, j * SUB:(j + 1) * SUB],
                    CT[0:33, n * 512:(n + 1) * 512], start=False, stop=True), mm)
            b.wait("vector", mm)
            out_ap = cb(j)
            for n in (0, 1):
                ps = (ps_w0, ps_w1)[n]
                b.op("vector", lambda e, j=j, n=n, ps=ps, out_ap=out_ap: e.tensor_tensor(
                    out_ap[:, n * 512:(n + 1) * 512], ps[:, 0:512],
                    res_s[:, j * D + n * 512: j * D + (n + 1) * 512], op=OP.add), ve)
                b.bdone(ps, bf)

    def emit_phase_c1(self):
        b, sb, dram = self, self.sb, self.dram
        T, NSUP = self.T, self.NSUP
        ld, st, mm, ve, sc = (self.sems[k] for k in ("ld", "st", "mm", "ve", "sc"))
        ph = contextlib.ExitStack()

        WoaT = b.sbuf("WoaT_s", [SUB, 8 * D], stack=ph)
        b.sbuf("kvg_z", [SUB, HA * 65], stack=ph)
        for c in range(8):
            b.dma("sync", WoaT[:, c * D:(c + 1) * D], dram["WoaT"][c * SUB:(c + 1) * SUB, :], ld)
        ld_w = ld.n

        x_s = b.sbuf("c_x", [SUB, 2 * D], stack=ph)
        QfT = b.sbuf("c_QfT", [SUB, 2 * D], stack=ph)
        o_s = b.sbuf("c_o", [SUB, D], stack=ph)
        oT = b.sbuf("c_oT", [SUB, D], stack=ph)
        x1_s = b.sbuf("c_x1", [SUB, 4 * D], stack=ph)
        xn_s = b.sbuf("c_xn", [SUB, 2 * D], stack=ph)
        xnT = b.sbuf("c_xnT", [SUB, 2 * D], stack=ph)
        x2_s = b.sbuf("c_x2", [SUB, 2 * D], stack=ph)
        tT_s = b.sbuf("c_tT", [33, SUP], stack=ph)
        b.op("vector", lambda e: e.memset(tT_s[32:33, :], 1.0))
        recip = b.sbuf("c_recip", [SUB, 16], stack=ph)
        scr_dv = b.sbuf("c_scrdv", [SUB, 1040], stack=ph)
        rb = b.sbuf("c_rb", [SUB, D], stack=ph)
        stats = b.sbuf("c_stats", [SUB, 16], stack=ph)
        mrneg = b.sbuf("c_mrneg", [SUB, 2], stack=ph)
        rstd = b.sbuf("c_rstd", [SUB, 2], stack=ph)

        ps_o0 = b.psum("c_ps_o0", [SUB, 512], stack=ph)
        ps_o1 = b.psum("c_ps_o1", [SUB, 512], stack=ph)
        ps_o2 = b.psum("c_ps_o2", [SUB, 512], stack=ph)
        ps_tr = b.psum("c_ps_tr", [SUB, 512], stack=ph)
        ps_w0 = b.psum("c_ps_w0", [SUB, 512], stack=ph)
        ps_w1 = b.psum("c_ps_w1", [SUB, 512], stack=ph)
        ps_t = b.psum("c_ps_t", [SUB, 512], stack=ph)
        o_banks = [(ps_o0, 6), (ps_o1, 6), (ps_o2, 4)]
        psl = (ps_tr, ps_t)
        bf = {}

        qfT_r = dram["qfT_sp"].rearrange("(m p) t -> p m t", m=8)

        def load_sup(i, guard_ve):
            tok0 = i * SUP
            if guard_ve:
                b.wait("sync", ve, guard_ve)
            for j in (0, 1):
                b.dma("sync", x_s[:, j * D:(j + 1) * D],
                      dram["x"][tok0 + j * SUB: tok0 + (j + 1) * SUB, :], ld)
            b.dma("sync", QfT[:].rearrange("p (m t) -> p m t", m=8),
                  qfT_r[:, :, tok0:tok0 + SUP], ld)
            return ld.n

        ld_x = load_sup(0, 0)
        self.emit_kv_gather()
        st_x2 = [st.n]

        def stage2(k, res, x1_next):
            for j in (0, 1):
                self.emit_transpose_1024(xn_s[:, j * D:(j + 1) * D], xnT, j * SUB, SUP,
                                         psl, bf)
            ve_xnT = ve.n
            b.wait("vector", st, st_x2[0])
            mid = None
            if x1_next is not None:
                mid = lambda: self.emit_ln2(x1_next, xn_s, stats, mrneg, rstd,
                                            sq=o_s[:, 0:D])
            self.emit_ffn(xnT, res, ps_t, ps_w0, ps_w1, tT_s,
                          lambda j: x2_s[:, j * D:(j + 1) * D], bf=bf,
                          ve_ready=ve_xnT, mid_cb=mid)
            ve_x2 = ve.n
            b.wait("gpsimd", ve, ve_x2)
            for j in (0, 1):
                b.dma("gpsimd",
                      dram["x2_sp"][k * SUP + j * SUB: k * SUP + (j + 1) * SUB, :],
                      x2_s[:, j * D:(j + 1) * D], st)
            st_x2[0] = st.n

        x1_prev = None
        for i in range(NSUP):
            tok0 = i * SUP
            x1b = x1_s[:, (i % 2) * 2 * D:(i % 2) * 2 * D + 2 * D]
            if i == 0:
                b.wait("tensor", ve, self.ve_kvg)
            for j in (0, 1):
                b.wait("tensor", ld, ld_x)
                hbase = 0
                for (ps, nsl) in o_banks:
                    b.bwait(ps, bf)
                    for pr in range(nsl // 2):
                        h0 = hbase + 2 * pr
                        b.op("tensor", lambda e, h0=h0, ps=ps, pr=pr, j=j,
                             nsl=nsl: e.matmul(
                            ps[:, pr * 130:(pr + 1) * 130],
                            QfT[:, (h0 // 2) * SUP + j * SUB:(h0 // 2) * SUP + (j + 1) * SUB],
                            sb["kvg_z"][:, h0 * 65:(h0 + 2) * 65],
                            start=(pr == 0), stop=(pr == nsl // 2 - 1)),
                            mm if pr == nsl // 2 - 1 else None)
                    hbase += nsl
                self.emit_div_evac(o_banks, o_s, recip, rb, scr_dv, EPS_A, bf=bf)
                self.emit_transpose_1024(o_s, oT, 0, SUB, psl, bf)
                for n in (0, 1):
                    ps = (ps_w0, ps_w1)[n]
                    b.bwait(ps, bf)
                    for c in range(8):
                        b.op("tensor", lambda e, c=c, n=n, ps=ps: e.matmul(
                            ps[:, 0:512], oT[:, c * SUB:(c + 1) * SUB],
                            WoaT[:, c * D + n * 512: c * D + (n + 1) * 512],
                            start=(c == 0), stop=False))
                    b.op("tensor", lambda e, n=n, ps=ps: e.matmul(
                        ps[:, 0:512], sb["ones_row"][0:1, 0:SUB],
                        sb["bias_of"][0:1, n * 512:(n + 1) * 512],
                        start=False, stop=True), mm)
                b.wait("vector", mm)
                for n in (0, 1):
                    ps = (ps_w0, ps_w1)[n]
                    b.op("vector", lambda e, j=j, n=n, ps=ps, x1b=x1b: e.tensor_tensor(
                        x1b[:, j * D + n * 512: j * D + (n + 1) * 512], ps[:, 0:512],
                        x_s[:, j * D + n * 512: j * D + (n + 1) * 512], op=OP.add), ve)
                    b.bdone(ps, bf)
                if self.dbg:
                    b.wait("gpsimd", ve)
                    b.dma("gpsimd", dram["dbg_x1"][tok0 + j * SUB: tok0 + (j + 1) * SUB, :],
                          x1b[:, j * D:(j + 1) * D], st)
            if i + 1 < NSUP:
                ld_x = load_sup(i + 1, ve.n)
            if i == 0:
                self.emit_ln2(x1b, xn_s, stats, mrneg, rstd)
            else:
                stage2(i - 1, x1_prev, x1b)
            x1_prev = x1b
        stage2(NSUP - 1, x1_prev, None)
        ph.close()

    def emit_phase_c2(self):
        b, sb, dram = self, self.sb, self.dram
        T, NSUP = self.T, self.NSUP
        ld, st, mm, ve, sc = (self.sems[k] for k in ("ld", "st", "mm", "ve", "sc"))
        b.sbuf("s_run", [SUB, 32 * 65])
        b.op("vector", lambda e: e.memset(sb["s_run"][:], 0.0))
        ph = contextlib.ExitStack()


        Wqb = b.sbuf("Wqb_s", [SUB, 2 * D], stack=ph)
        Wkb = b.sbuf("Wkb_s", [SUB, 2 * D], stack=ph)
        Wvb = b.sbuf("Wvb_s", [SUB, 8 * D], stack=ph)
        for c in range(8):
            b.dma("sync", Wqb[:, c * F2:(c + 1) * F2], dram["Wqb"][c * SUB:(c + 1) * SUB, :], ld)
            b.dma("sync", Wkb[:, c * F2:(c + 1) * F2], dram["Wkb"][c * SUB:(c + 1) * SUB, :], ld)
            b.dma("sync", Wvb[:, c * D:(c + 1) * D], dram["Wvb"][c * SUB:(c + 1) * SUB, :], ld)
        ld_w = ld.n

        x2_s = b.sbuf("c2_x2", [SUB, 2 * D], stack=ph)
        xn_s = b.sbuf("c2_xn", [SUB, 2 * D], stack=ph)
        xnT = b.sbuf("c2_xnT", [SUB, 2 * D], stack=ph)
        qT_s = b.sbuf("c2_qT", [SUB, 2 * SUP], stack=ph)
        kT_s = b.sbuf("c2_kT", [SUB, 2 * SUP], stack=ph)
        k_s = b.sbuf("c2_k", [SUB, 2 * F2], stack=ph)
        phk = b.sbuf("c2_phk", [SUB, 2 * HB * F2], stack=ph)
        vpb = b.sbuf("c2_vpb", [SUB, 1040], stack=ph)
        U_s = b.sbuf("c2_U", [SUB, 2080], stack=ph)
        S_loc = b.sbuf("c2_Sloc", [SUB, 2080], stack=ph)
        stats = b.sbuf("c2_stats", [SUB, 16], stack=ph)
        mrneg = b.sbuf("c2_mrneg", [SUB, 2], stack=ph)
        rstd = b.sbuf("c2_rstd", [SUB, 2], stack=ph)

        ps_a = b.psum("c2_ps_a", [SUB, 512], stack=ph)
        ps_b = b.psum("c2_ps_b", [SUB, 512], stack=ph)
        ps_tr = b.psum("c2_ps_tr", [SUB, 512], stack=ph)
        u_ps = [b.psum(f"c2_ps_u{k}", [SUB, 512], stack=ph) for k in range(5)]
        psl = (ps_tr, ps_a)
        bf = {}

        b.op("vector", lambda e: e.memset(S_loc[:], 0.0))
        for h in range(HB):
            b.op("vector", lambda e, h=h: e.memset(
                vpb[:, h * 65 + 64: h * 65 + 65], 1.0))
        b.op("vector", lambda e: e.engine_nop(), ve)

        qT_r = dram["qT_sp"].rearrange("(m p) t -> p m t", m=2)
        kT_r = dram["kT_sp"].rearrange("(m p) t -> p m t", m=2)
        st_sp = st.n
        mm_u = {0: 0, 1: 0}
        gp = self.sems["gp"]
        for j in (0, 1):
            b.dma("sync", x2_s[:, j * D:(j + 1) * D],
                  dram["x2_sp"][j * SUB:(j + 1) * SUB, :], ld)
        b.wait("vector", ld)
        self.emit_ln2(x2_s[:], xn_s[:], stats, mrneg, rstd)
        ve_ln = ve.n
        for i in range(NSUP):
            tok0 = i * SUP
            for j in (0, 1):
                self.emit_transpose_1024(xn_s[:, j * D:(j + 1) * D], xnT, j * SUB, SUP,
                                         psl, bf, ve_src=ve_ln)
            ve_xnT = ve.n

            b.wait("tensor", ld, ld_w)
            b.wait("tensor", ve, ve_xnT)

            def proj_qk(W, dst, scale):
                for mq in (0, 1):
                    ps = (ps_a, ps_b)[mq]
                    b.bwait(ps, bf)
                    for c in range(8):
                        b.op("tensor", lambda e, W=W, mq=mq, c=c, ps=ps: e.matmul(
                            ps[:, 0:SUP], W[:, (c * 2 + mq) * SUB:(c * 2 + mq + 1) * SUB],
                            xnT[:, c * SUP:(c + 1) * SUP],
                            start=(c == 0), stop=(c == 7)), mm if c == 7 else None)
                    b.wait("vector", mm)
                    if scale is None:
                        b.op("vector", lambda e, mq=mq, ps=ps, dst=dst: e.tensor_copy(
                            dst[:, mq * SUP:(mq + 1) * SUP], ps[:, 0:SUP]), ve)
                    else:
                        b.op("vector", lambda e, mq=mq, ps=ps, dst=dst, sc_=scale:
                             e.tensor_single_scalar(dst[:, mq * SUP:(mq + 1) * SUP],
                                                    ps[:, 0:SUP], sc_, op=OP.mult), ve)
                    b.bdone(ps, bf)

            proj_qk(Wkb, kT_s, None)
            gp_j = {}
            ve_phi = {}
            for j in (0, 1):
                b.wait("tensor", ve)
                b.bwait(ps_tr, bf)
                for mq in (0, 1):
                    b.op("tensor", lambda e, mq=mq, j=j: e.transpose(
                        ps_tr[:, mq * SUB:(mq + 1) * SUB],
                        kT_s[:, mq * SUP + j * SUB: mq * SUP + (j + 1) * SUB],
                        sb["ident"][:]), mm if mq == 1 else None)
                b.wait("vector", mm)
                b.op("vector", lambda e, j=j: e.tensor_copy(
                    k_s[:, j * F2:(j + 1) * F2], ps_tr[:, 0:256]), ve)
                b.bdone(ps_tr, bf)
                b.wait("gpsimd", ve)
                b.wait("gpsimd", mm, mm_u[j])
                b.wait("vector", mm, mm_u[j])
                for h in range(HB):
                    eng = "gpsimd" if h < 8 else "vector"
                    sem = gp if h < 8 else ve
                    b.op(eng, lambda e, j=j, h=h: e.tensor_tensor(
                        phk[:, j * HB * F2 + h * F2:
                            j * HB * F2 + (h + 1) * F2]
                        .rearrange("p (a c) -> p a c", c=FB),
                        k_s[:, j * F2 + h * FB: j * F2 + h * FB + FB]
                        .unsqueeze(2).broadcast_to([SUB, FB, FB]),
                        k_s[:, j * F2 + h * FB: j * F2 + h * FB + FB]
                        .unsqueeze(1).broadcast_to([SUB, FB, FB]),
                        op=OP.mult), sem)
                gp_j[j] = gp.n
                ve_phi[j] = ve.n
                if self.dbg:
                    b.wait("gpsimd", gp)
                    b.wait("gpsimd", ve)
                    b.dma("gpsimd", dram["dbg_phk"][tok0 + j * SUB: tok0 + (j + 1) * SUB, :],
                          phk[:, j * HB * F2:(j + 1) * HB * F2], st)

            proj_qk(Wqb, qT_s, QSCALE_B)
            b.wait("gpsimd", ve)
            b.wait("gpsimd", st, st_sp)
            b.dma("gpsimd", qT_r[:, :, tok0:tok0 + SUP],
                  qT_s[:].rearrange("p (m t) -> p m t", m=2), st)
            b.dma("gpsimd", kT_r[:, :, tok0:tok0 + SUP],
                  kT_s[:].rearrange("p (m t) -> p m t", m=2), st)
            ubank_first = [True] * 5
            for j in (0, 1):
                for n in (0, 1):
                    ps = (ps_a, ps_b)[n]
                    b.bwait(ps, bf)
                    for c in range(8):
                        b.op("tensor", lambda e, c=c, j=j, n=n, ps=ps: e.matmul(
                            ps[:, 0:512],
                            xnT[:, c * SUP + j * SUB: c * SUP + (j + 1) * SUB],
                            Wvb[:, c * D + n * 512: c * D + (n + 1) * 512],
                            start=(c == 0), stop=(c == 7)), mm if c == 7 else None)
                    b.wait("vector", mm)
                    b.op("vector", lambda e, n=n, ps=ps: e.tensor_copy(
                        vpb[:].rearrange("p (h c) -> p h c", h=16)
                        [:, 8 * n:8 * (n + 1), 0:64],
                        ps[:, 0:512].rearrange("p (h c) -> p h c", h=8)), ve)
                    b.bdone(ps, bf)
                ve_vpb = ve.n
                b.wait("gpsimd", ve, ve_vpb)
                b.dma("gpsimd", dram["vpb_sp"][i * 2 + j],
                      vpb[:], st)
                b.wait("tensor", gp, gp_j[j])
                b.wait("tensor", ve, max(ve_vpb, ve_phi[j]))
                for h in range(HB):
                    for fc in (0, 1):
                        s_ = h * 2 + fc
                        bank, colk = s_ // 7, (s_ % 7) * 65
                        first = ubank_first[bank] and j == 0
                        if j == 0 and ubank_first[bank]:
                            ubank_first[bank] = False
                        stop = (j == 1 and (s_ % 7 == 6 or s_ == 31))
                        b.op("tensor", lambda e, h=h, fc=fc, j=j, bank=bank, colk=colk,
                             first=first, stop=stop: e.matmul(
                            u_ps[bank][:, colk:colk + 65],
                            phk[:, j * HB * F2 + h * F2 + fc * SUB:
                                j * HB * F2 + h * F2 + (fc + 1) * SUB],
                            vpb[:, h * 65: (h + 1) * 65],
                            start=first, stop=stop),
                            mm if s_ == 31 else None)
                mm_u[j] = mm.n
                if j == 0 and i + 1 < NSUP:
                    b.wait("sync", ve, ve_ln)
                    for jj in (0, 1):
                        b.dma("sync", x2_s[:, jj * D:(jj + 1) * D],
                              dram["x2_sp"][tok0 + SUP + jj * SUB:
                                            tok0 + SUP + (jj + 1) * SUB, :], ld)
                    b.wait("vector", ld)
                    self.emit_ln2(x2_s[:], xn_s[:], stats, mrneg, rstd)
                    ve_ln = ve.n

            b.wait("vector", mm)
            b.wait("vector", st, st_sp)
            for bank in range(5):
                ncols = 455 if bank < 4 else 260
                b.op("vector", lambda e, bank=bank, ncols=ncols: e.tensor_copy(
                    U_s[:, bank * 455: bank * 455 + ncols], u_ps[bank][:, 0:ncols]),
                    ve if bank == 4 else None)
            b.wait("vector", ve)
            b.op("vector", lambda e: e.tensor_add(S_loc[:], S_loc[:], U_s[:]), ve)
            ve_u = ve.n
            b.wait("gpsimd", ve, ve_u)
            b.dma("gpsimd", dram["u_sp"][i], U_s[:], st)
            st_sp = st.n
            ve_free = ve.n

        b.wait("gpsimd", ve)
        b.dma("gpsimd", dram["s_cc_in"], S_loc[:], st)
        self.st_sloc = st.n
        ph.close()

    def emit_s_gather(self):
        b, sb, dram = self, self.sb, self.dram
        ld, st, ve, cc = (self.sems[k] for k in ("ld", "st", "ve", "cc"))
        if self.seq_cores > 1:
            b.op("gpsimd", lambda e: e.collective_compute(
                "AllGather", OP.bypass,
                ins=[dram["s_cc_in"]], outs=[dram["s_cc_out"]],
                replica_groups=[[g * self.seq_cores + r for r in range(self.seq_cores)]
                                for g in range(8 // self.seq_cores)]), cc)
            b.wait("sync", cc)
            phs_ = contextlib.ExitStack()
            blk = b.sbuf("s_blk", [SUB, 1040], stack=phs_)
            for r in range(self.seq_cores):
                for hf in (0, 1):
                    b.dma("sync", blk[:],
                          dram["s_cc_out"][r * SUB:(r + 1) * SUB,
                                           hf * 1040:(hf + 1) * 1040], ld)
                    b.wait("vector", ld)
                    b.op("vector", lambda e, r=r, hf=hf: e.scalar_tensor_tensor(
                        sb["s_run"][:, hf * 1040:(hf + 1) * 1040], blk[:],
                        sb["pm_s"][:, r:r + 1],
                        sb["s_run"][:, hf * 1040:(hf + 1) * 1040],
                        op0=OP.mult, op1=OP.add), ve)
                    b.wait("vector", ve)
                    b.wait("sync", ve)
            phs_.close()
        self.ve_srun = self.sems["ve"].n

    def emit_phase_e(self):
        b, sb, dram = self, self.sb, self.dram
        T, NSUP = self.T, self.NSUP
        ld, st, mm, ve, sc = (self.sems[k] for k in ("ld", "st", "mm", "ve", "sc"))
        ph = contextlib.ExitStack()

        Wob = b.sbuf("Wob_s", [SUB, 8 * D], stack=ph)
        for c in range(8):
            b.dma("sync", Wob[:, c * D:(c + 1) * D], dram["Wob"][c * SUB:(c + 1) * SUB, :], ld)
        WTe, CTe, Vre = sb["WT_e"], sb["CT_e"], sb["Vr_e"]

        x2e = b.sbuf("e_x2", [SUB, 2 * D], stack=ph)
        qTE = b.sbuf("e_qTE", [SUB, 4 * SUP], stack=ph)
        kTE = b.sbuf("e_kTE", [SUB, 4 * SUP], stack=ph)
        vpbe = b.sbuf("e_vpb", [SUB, 2 * 1040], stack=ph)
        Ue = b.sbuf("e_U", [SUB, 2080], stack=ph)
        rep1 = b.sbuf("e_rep1", [SUB, HB * 2 * SUB], stack=ph)
        rep2 = b.sbuf("e_rep2", [SUB, 8 * SUB], stack=ph)
        at_s = b.sbuf("e_at", [SUB, HB * SUB], stack=ph)
        triu4 = b.sbuf("e_triu4", [SUB, 512], stack=ph)
        ob_s = b.sbuf("e_ob", [SUB, D], stack=ph)
        obT = b.sbuf("e_obT", [SUB, D], stack=ph)
        x3_s = b.sbuf("e_x3", [SUB, 4 * D], stack=ph)
        xn_s = b.sbuf("e_xn", [SUB, 2 * D], stack=ph)
        xn4T = b.sbuf("e_xn4T", [SUB, 2 * D], stack=ph)
        x4_s = b.sbuf("e_x4", [SUB, D], stack=ph)
        tTe = b.sbuf("e_tT", [33, SUP], stack=ph)
        b.op("vector", lambda e: e.memset(tTe[32:33, :], 1.0))
        recip = b.sbuf("e_recip", [SUB, 16], stack=ph)
        scr_dv = b.sbuf("e_scrdv", [SUB, 1040], stack=ph)
        rb = b.sbuf("e_rb", [SUB, D], stack=ph)
        stats = b.sbuf("e_stats", [SUB, 16], stack=ph)
        mrneg = b.sbuf("e_mrneg", [SUB, 2], stack=ph)
        rstd = b.sbuf("e_rstd", [SUB, 2], stack=ph)

        ps_o0 = b.psum("e_ps_o0", [SUB, 512], stack=ph)
        ps_o1 = b.psum("e_ps_o1", [SUB, 512], stack=ph)
        ps_o2 = b.psum("e_ps_o2", [SUB, 512], stack=ph)
        at_ps0 = b.psum("e_ps_at0", [SUB, 512], stack=ph)
        at_ps1 = b.psum("e_ps_at1", [SUB, 512], stack=ph)
        ps_tr = b.psum("e_ps_tr", [SUB, 512], stack=ph)
        ps_w0 = b.psum("e_ps_w0", [SUB, 512], stack=ph)
        ps_w1 = b.psum("e_ps_w1", [SUB, 512], stack=ph)
        o_banks = [(ps_o0, 7), (ps_o1, 7), (ps_o2, 2)]
        psl = (ps_tr, ps_w0)
        at_banks = [at_ps0, at_ps1, ps_tr, ps_w1]
        at_sems = [self.sem(f"s_at{k}") for k in range(4)]
        bf = {}
        for q in range(4):
            b.op("vector", lambda e, q=q: e.tensor_copy(
                triu4[:, q * SUB:(q + 1) * SUB], sb["triu"][:]),
                ve if q == 3 else None)

        q_r1 = (dram["qT_sp"].rearrange("(h f2 a) t -> a h f2 t", h=16, f2=2)
                .unsqueeze(1).broadcast_to([8, 16, 16, 2, T]))
        q_r2c = dram["qT_sp"].rearrange("(h bb) t -> bb h t", h=16)

        def group_loads(i, g_mm, g_ve):
            tok0 = i * SUP
            if g_mm:
                b.wait("sync", mm, g_mm)
            if g_ve:
                b.wait("sync", ve, g_ve)
            for j in (0, 1):
                b.dma("sync", x2e[:, j * D:(j + 1) * D],
                      dram["x2_sp"][tok0 + j * SUB: tok0 + (j + 1) * SUB, :], ld)
                b.dma("sync", vpbe[:, j * 1040:(j + 1) * 1040],
                      dram["vpb_sp"][i * 2 + j], ld)
            for hh in range(4):
                for (dstE, srcd) in ((qTE, dram["qT_sp"]), (kTE, dram["kT_sp"])):
                    b.dma("sync",
                          dstE[hh * 32: hh * 32 + 16, :]
                          .rearrange("p (m t) -> p m t", m=4),
                          srcd[:, tok0:tok0 + SUP]
                          .rearrange("(m hf) t -> m hf t", hf=64)
                          [:, 16 * hh:16 * (hh + 1), :].transpose([1, 0, 2]), ld)
            b.dma("sync", Ue[:], dram["u_sp"][i], ld)
            return ld.n

        def phi_load(i, j, guard_mm):
            tok0 = i * SUP
            if guard_mm:
                b.wait("sync", mm, guard_mm)
            for a in range(8):
                b.dma("sync",
                      rep1[a * 16:(a + 1) * 16, :]
                      .rearrange("p (h f2 t) -> p h f2 t", h=16, f2=2),
                      q_r1[a, :, :, :, tok0 + j * SUB: tok0 + (j + 1) * SUB], ld)
            ld1 = ld.n
            for hh in (0, 1):
                for a in range(8):
                    b.dma("sync", rep2[a * 16:(a + 1) * 16, :] if hh == 0 else
                          rep2[a * 16:(a + 1) * 16, :],
                          q_r2c[:, 8 * hh:8 * (hh + 1),
                                tok0 + j * SUB: tok0 + (j + 1) * SUB], ld)
                ld_rep = ld.n
                b.wait("gpsimd", ld, ld_rep)
                gp = self.sems["gp"]
                for f2 in (0, 1):
                    b.op("gpsimd", lambda e, hh=hh, f2=f2: e.tensor_mul(
                        rep1[:].rearrange("p (h f2 t) -> p h f2 t", h=16, f2=2)
                        [:, 8 * hh:8 * (hh + 1), f2, :],
                        rep1[:].rearrange("p (h f2 t) -> p h f2 t", h=16, f2=2)
                        [:, 8 * hh:8 * (hh + 1), f2, :],
                        rep2[:].rearrange("p (h t) -> p h t", h=8)), gp)
                if hh == 0:
                    b.wait("sync", gp)
            return ("gp", self.sems["gp"].n)

        ld_g = group_loads(0, 0, 0)
        ve_phq = phi_load(0, 0, 0)
        self.emit_s_gather()
        st_out = st.n

        for i in range(NSUP):
            tok0 = i * SUP
            x3b = x3_s[:, (i % 2) * 2 * D:(i % 2) * 2 * D + 2 * D]
            mm_oi = {}
            for j in (0, 1):
                for (ps, nsl) in o_banks:
                    b.bwait(ps, bf)
                b.wait("tensor", self.sems[ve_phq[0]], ve_phq[1])
                b.wait("tensor", ld, ld_g)
                if i == 0 and j == 0:
                    b.wait("tensor", ve, self.ve_srun)
                slot = 0
                for (ps, nsl) in o_banks:
                    for s_ in range(nsl):
                        h = slot + s_
                        for fc in (0, 1):
                            b.op("tensor", lambda e, h=h, fc=fc, ps=ps, s_=s_: e.matmul(
                                ps[:, s_ * 65:(s_ + 1) * 65],
                                rep1[:, (2 * h + fc) * SUB:(2 * h + fc + 1) * SUB],
                                sb["s_run"][:, (2 * h + fc) * 65:(2 * h + fc + 1) * 65],
                                start=(s_ == 0 and fc == 0), stop=False),
                                mm if (h == HB - 1 and fc == 1) else None)
                    slot += nsl
                mm_oi[j] = mm.n
                if j == 0:
                    ve_phq = phi_load(i, 1, mm_oi[0])
                else:
                    self.emit_transpose_1024(xn_s[:, 0:D], xn4T, 0, SUP, psl, bf)
                def o_slot(h):
                    if h < 7:
                        return ps_o0, h
                    if h < 14:
                        return ps_o1, h - 7
                    return ps_o2, h - 14
                for si in range(j + 1):
                    diag = (si == j)
                    for bq in range(4):
                        b.bwait(at_banks[bq], bf)
                        r = bq * 32
                        for s in range(4):
                            h = s * 4 + bq
                            m4 = h // 4
                            b.op("tensor", lambda e, r=r, s=s, m4=m4, si=si, j=j,
                                 bq=bq: e.matmul(
                                at_banks[bq][:, s * SUB:(s + 1) * SUB],
                                kTE[r:r + 16, m4 * SUP + si * SUB: m4 * SUP + (si + 1) * SUB],
                                qTE[r:r + 16, m4 * SUP + j * SUB: m4 * SUP + (j + 1) * SUB],
                                start=True, stop=True, tile_position=(r, 0),
                                skip_group_check=(r > 0)),
                                at_sems[bq] if s == 3 else None)
                    for bq in range(4):
                        b.wait("scalar", at_sems[bq], at_sems[bq].n)
                        b.op("scalar", lambda e, bq=bq: e.activation(
                            at_s[:, bq * 512:(bq + 1) * 512],
                            at_banks[bq][:, 0:512], AF.Square), sc)
                        bf[id(at_banks[bq])] = ("sc", sc.n)
                        if diag:
                            b.wait("vector", sc)
                            b.op("vector", lambda e, bq=bq: e.tensor_mul(
                                at_s[:, bq * 512:(bq + 1) * 512],
                                at_s[:, bq * 512:(bq + 1) * 512], triu4[:]), ve)
                            b.wait("tensor", ve)
                        else:
                            b.wait("tensor", sc)
                        for s in range(4):
                            h = s * 4 + bq
                            ps, s_ = o_slot(h)
                            stop = diag and all(
                                o_slot(s2 * 4 + b2)[0] is not ps
                                for b2 in range(4) for s2 in range(4)
                                if (b2, s2) > (bq, s))
                            b.op("tensor", lambda e, h=h, ps=ps, s_=s_, si=si, bq=bq,
                                 s=s, stop=stop: e.matmul(
                                ps[:, s_ * 65:(s_ + 1) * 65],
                                at_s[:, (bq * 4 + s) * SUB:(bq * 4 + s + 1) * SUB],
                                vpbe[:, si * 1040 + h * 65: si * 1040 + (h + 1) * 65],
                                start=False, stop=stop),
                                mm if stop and bq == 3 and s == 3 else None)
                mm_intra = mm.n
                self.emit_div_evac(o_banks, ob_s, recip, rb, scr_dv, EPS_B, bf=bf)
                if self.dbg:
                    b.wait("gpsimd", ve)
                    b.dma("gpsimd",
                          dram["dbg_onum"][tok0 + j * SUB: tok0 + (j + 1) * SUB, :],
                          scr_dv[:], st)
                    b.dma("gpsimd", dram["dbg_ob"][tok0 + j * SUB: tok0 + (j + 1) * SUB, :],
                          ob_s[:], st)
                self.emit_transpose_1024(ob_s, obT, 0, SUB, psl, bf)
                b.wait("tensor", ve)
                for n in (0, 1):
                    ps = (ps_w0, ps_w1)[n]
                    b.bwait(ps, bf)
                    for c in range(8):
                        b.op("tensor", lambda e, c=c, n=n, ps=ps: e.matmul(
                            ps[:, 0:512], obT[:, c * SUB:(c + 1) * SUB],
                            Wob[:, c * D + n * 512: c * D + (n + 1) * 512],
                            start=(c == 0), stop=(c == 7)), mm if c == 7 else None)
                b.wait("vector", mm)
                for n in (0, 1):
                    ps = (ps_w0, ps_w1)[n]
                    b.op("vector", lambda e, j=j, n=n, ps=ps, x3b=x3b: e.tensor_tensor(
                        x3b[:, j * D + n * 512: j * D + (n + 1) * 512], ps[:, 0:512],
                        x2e[:, j * D + n * 512: j * D + (n + 1) * 512], op=OP.add), ve)
                    b.bdone(ps, bf)
                if j == 1:
                    ve_x3 = ve.n
                if self.dbg:
                    b.wait("gpsimd", ve)
                    b.dma("gpsimd", dram["dbg_x3"][tok0 + j * SUB: tok0 + (j + 1) * SUB, :],
                          x3b[:, j * D:(j + 1) * D], st)
                if j == 0:
                    self.emit_ln(x3b[:, 0:D], xn_s[:, 0:D], stats,
                                 mrneg[:, 0:1], rstd[:, 0:1])

            self.emit_ln(x3b[:, D:2 * D], xn_s[:, D:2 * D], stats,
                         mrneg[:, 1:2], rstd[:, 1:2])
            self.emit_transpose_1024(xn_s[:, D:2 * D], xn4T, SUB, SUP, psl, bf)

            b.wait("vector", mm, mm_oi[1])
            b.wait("vector", ld, ld_g)
            b.op("vector", lambda e: e.tensor_add(sb["s_run"][:], sb["s_run"][:], Ue[:]),
                 ve)
            ve_sru = ve.n
            if self.dbg and i + 1 < NSUP:
                b.wait("gpsimd", ve)
                b.dma("gpsimd", dram["dbg_srun"][i + 1], sb["s_run"][:], st)

            if i + 1 < NSUP:
                ld_g = group_loads(i + 1, mm_intra, max(ve_x3, ve_sru))
                ve_phq = phi_load(i + 1, 0, mm_oi[1])

            self.emit_ffn_out(xn4T, x3b, at_ps0, ps_w0, ps_w1, tTe, x4_s,
                              (WTe, CTe, Vre), tok0, st, bf=bf)
        b.wait("gpsimd", st)
        ph.close()

    def emit_ffn_out(self, xnT, res_s, ps_t, ps_w0, ps_w1, tT_s, x4_s, wts, tok0, st,
                     bf=None):
        b, sb, dram = self, self.sb, self.dram
        mm, ve = self.sems["mm"], self.sems["ve"]
        WT, CT, Vr = wts
        if bf is None:
            bf = {}
        b.wait("tensor", ve)
        b.bwait(ps_t, bf)
        for c in range(8):
            b.op("tensor", lambda e, c=c: e.matmul(
                ps_t[0:32, 0:SUP], Vr[:, c * 32:(c + 1) * 32],
                xnT[:, c * SUP:(c + 1) * SUP], start=(c == 0), stop=(c == 7)),
                mm if c == 7 else None)
        b.wait("vector", mm)
        b.op("vector", lambda e: e.tensor_copy(tT_s[0:32, 0:SUP], ps_t[0:32, 0:SUP]), ve)
        b.bdone(ps_t, bf)
        b.wait("tensor", ve)
        for j in (0, 1):
            for n in (0, 1):
                ps = (ps_w0, ps_w1)[n]
                b.bwait(ps, bf)
                for c in range(8):
                    b.op("tensor", lambda e, c=c, j=j, n=n, ps=ps: e.matmul(
                        ps[:, 0:512], xnT[:, c * SUP + j * SUB: c * SUP + (j + 1) * SUB],
                        WT[:, c * D + n * 512: c * D + (n + 1) * 512],
                        start=(c == 0), stop=False))
                b.op("tensor", lambda e, j=j, n=n, ps=ps: e.matmul(
                    ps[:, 0:512], tT_s[0:33, j * SUB:(j + 1) * SUB],
                    CT[0:33, n * 512:(n + 1) * 512], start=False, stop=True), mm)
            b.wait("vector", mm)
            b.wait("vector", st)
            for n in (0, 1):
                ps = (ps_w0, ps_w1)[n]
                b.op("vector", lambda e, j=j, n=n, ps=ps: e.tensor_tensor(
                    x4_s[:, n * 512:(n + 1) * 512], ps[:, 0:512],
                    res_s[:, j * D + n * 512: j * D + (n + 1) * 512], op=OP.add), ve)
                b.bdone(ps, bf)
            b.wait("gpsimd", ve)
            b.dma("gpsimd", dram["out"][tok0 + j * SUB: tok0 + (j + 1) * SUB, :],
                  x4_s[:], st)


def build_program(T, seq_cores, dbg=False):
    return Builder(T, seq_cores, dbg=dbg).build()


def _host_layout(inputs, T_core, n_cores, seq_cores):
    A = np.ascontiguousarray
    x = inputs["L0_attn_a_x"]
    B, N, _ = x.shape
    w = {
        "Wqa": A(inputs["L0_attn_a_Wq"]), "Wka": A(inputs["L0_attn_a_Wk"]),
        "WvaT": A(inputs["L0_attn_a_Wv"].T), "WoaT": A(inputs["L0_attn_a_Wo"].T),
        "PT": A(inputs["L0_attn_a_P"].T),
        "bqc": A(inputs["L0_attn_a_bq"].reshape(HA, 64).T),
        "bkc": A(inputs["L0_attn_a_bk"].reshape(HA, 64).T),
        "bv": A(inputs["L0_attn_a_bv"].reshape(1, D)),
        "bo": A(inputs["L0_attn_a_bo"].reshape(1, D)),
        "WT": A(inputs["L0_ffn_W"].T), "bf": A(inputs["L0_ffn_b"].reshape(1, D)),
        "Vr": A(inputs["L0_ffn_V_r"]), "CT": A(inputs["L0_ffn_C"].T),
        "Wqb": A(inputs["L0_attn_b_Wq"]), "Wkb": A(inputs["L0_attn_b_Wk"]),
        "Wvb": A(inputs["L0_attn_b_Wv"]), "Wob": A(inputs["L0_attn_b_Wo"]),
    }
    in_maps = []
    for c in range(n_cores):
        bidx = c // seq_cores
        q = c % seq_cores
        m = dict(w)
        m["x"] = A(x[bidx, q * T_core:(q + 1) * T_core, :])
        m["pmask"] = A(np.asarray([[1.0 if j < q else 0.0 for j in range(4)]],
                                  dtype=np.float32))
        in_maps.append(m)
    return in_maps


def kernel(**inputs):
    n_cores = 8
    seq_cores = 4
    x = np.asarray(inputs["L0_attn_a_x"], dtype=np.float32)
    B, N, _ = x.shape
    T_core = (B * N) // n_cores
    nc = build_program(T_core, seq_cores)
    in_maps = _host_layout({k: np.asarray(v, dtype=np.float32)
                            for k, v in inputs.items()},
                           T_core, n_cores, seq_cores)
    res = run_bass_kernel_spmd(nc, in_maps, core_ids=list(range(n_cores)))
    full = np.empty((B, N, D), dtype=np.float32)
    for c in range(n_cores):
        bidx = c // seq_cores
        q = c % seq_cores
        full[bidx, q * T_core:(q + 1) * T_core, :] = res.results[c]["out"]
    return full


# version 19
# speedup vs baseline: 1.0428x; 1.0070x over previous
import contextlib
import numpy as np
import concourse.bass as bass
import concourse.mybir as mybir
from concourse.bass_utils import run_bass_kernel_spmd

F32 = mybir.dt.float32
I32 = mybir.dt.int32
AF = mybir.ActivationFunctionType
OP = mybir.AluOpType
AX = mybir.AxisListType

D = 1024
HA = 16
FA = 64
HB = 16
FB = 16
F2 = 256
HD = 64
SUB = 128
SUP = 256
LN_EPS = 1e-5
EPS_A = 1e-6
EPS_B = 1e-5
NORM_A = float(64.0 ** -0.25)
QSCALE_B = float(FB ** -0.5)

ENGINES = ("tensor", "vector", "scalar", "gpsimd", "sync")


class Prog:
    def __init__(self):
        self.streams = {e: [] for e in ENGINES}

    def emit(self, eng, fn):
        self.streams[eng].append(fn)

    def replay(self, eng_name):
        def run(eng):
            for fn in self.streams[eng_name]:
                fn(eng)
        return run


class Sem:
    def __init__(self, stack, nc, name):
        self.h = stack.enter_context(nc.semaphore(name))
        self.n = 0

    def bump(self, k):
        self.n += k
        return self.n


class Builder:
    def __init__(self, T, seq_cores, dbg=False, stop_after=None):
        self.T = T
        self.seq_cores = seq_cores
        self.dbg = dbg
        self.stop_after = stop_after
        self.NSUP = T // SUP
        self.nc = bass.Bass("TRN2", target_bir_lowering=False,
                            detect_race_conditions=False)
        self.stack = contextlib.ExitStack()
        self.prog = Prog()
        self.dram = {}
        self.sb = {}

    def dram_in(self, name, shape):
        self.dram[name] = self.nc.dram_tensor(name, list(shape), F32,
                                              kind="ExternalInput").ap()
        return self.dram[name]

    def dram_out(self, name, shape):
        self.dram[name] = self.nc.dram_tensor(name, list(shape), F32,
                                              kind="ExternalOutput").ap()
        return self.dram[name]

    def dram_tmp(self, name, shape):
        self.dram[name] = self.nc.dram_tensor(name, list(shape), F32,
                                              kind="Internal").ap()
        return self.dram[name]

    def sbuf(self, name, shape, dtype=F32, stack=None):
        t = (stack or self.stack).enter_context(
            self.nc.sbuf_tensor(name, list(shape), dtype))
        self.sb[name] = t
        return t

    def psum(self, name, shape, stack=None):
        assert shape[0] == SUB
        cols = ((shape[1] + 511) // 512) * 512
        t = (stack or self.stack).enter_context(
            self.nc.psum_tensor(name, [SUB, cols], F32))
        self.sb[name] = t
        return t

    def sem(self, name):
        return Sem(self.stack, self.nc, name)

    def dma(self, eng, out, in_, sem):
        sem.bump(16)
        self.prog.emit(eng, lambda e, out=out, in_=in_, s=sem: e.dma_start(
            out=out, in_=in_).then_inc(s.h, 16))
        return sem.n

    def wait(self, eng, sem, val=None):
        v = sem.n if val is None else val
        self.prog.emit(eng, lambda e, s=sem, v=v: e.wait_ge(s.h, v))

    def op(self, eng, fn, sem=None):
        if sem is None:
            self.prog.emit(eng, fn)
        else:
            sem.bump(1)
            self.prog.emit(eng, lambda e, fn=fn, s=sem: fn(e).then_inc(s.h, 1))

    def barrier(self):
        bar = self.bar
        self.wait("sync", self.sems["ld"])
        self.wait("gpsimd", self.sems["st"])
        target = bar.n + len(ENGINES)
        for e in ENGINES:
            bar.bump(1)
            self.prog.emit(e, lambda eng, s=bar: eng.sem_inc(s.h, 1))
        for e in ENGINES:
            self.prog.emit(e, lambda eng, s=bar, t=target: eng.wait_ge(s.h, t))

    def build(self):
        b, nc = self, self.nc
        T, NSUP = self.T, self.NSUP

        b.dram_in("x", [T, D])
        for nm, shp in [("Wqa", [D, D]), ("Wka", [D, D]), ("bqc", [64, HA]),
                        ("bkc", [64, HA]), ("WvaT", [D, D]),
                        ("WoaT", [D, D]), ("PT", [FA, FA]),
                        ("bv", [1, D]), ("bo", [1, D]),
                        ("WT", [D, D]), ("bf", [1, D]), ("Vr", [D, 32]), ("CT", [32, D]),
                        ("Wqb", [D, F2]), ("Wkb", [D, F2]), ("Wvb", [D, D]), ("Wob", [D, D]),
                        ("pmask", [1, 4])]:
            b.dram_in(nm, shp)
        b.dram_out("out", [T, D])

        mk_sp = b.dram_out if self.dbg else b.dram_tmp
        mk_sp("qfT_sp", [D, T])
        mk_sp("x2_sp", [T, D])
        mk_sp("qT_sp", [F2, T])
        mk_sp("kT_sp", [F2, T])
        b.dram_tmp("vpb_sp", [T // SUB, SUB, HB * 65])
        b.dram_tmp("u_sp", [NSUP, SUB, 32 * 65])
        b.dram_tmp("kv_cc_in", [SUB, 8 * 65])
        b.dram_tmp("s_cc_in", [SUB, 32 * 65])
        if self.seq_cores > 1:
            b.dram_tmp("kv_cc_out", [self.seq_cores * SUB, 8 * 65])
            b.dram_tmp("s_cc_out", [self.seq_cores * SUB, 32 * 65])
        if self.dbg:
            for nm in ("qfT_sp", "x2_sp", "qT_sp", "kT_sp"):
                pass
            for nm, shp in [("dbg_xn1", [T, D]), ("dbg_kv", [SUB, 8 * 65]),
                            ("dbg_x1", [T, D]), ("dbg_phk", [T, HB * F2]),
                            ("dbg_ob", [T, D]), ("dbg_x3", [T, D]),
                            ("dbg_onum", [T, HB * 65]),
                            ("dbg_phq", [T, HB * F2]),
                            ("dbg_srun", [NSUP, SUB, 32 * 65]),
                            ("dbg_r1raw", [SUB, HB * 2 * SUB]),
                            ("dbg_r2", [2, SUB, 8 * SUB])]:
                b.dram_out(nm, shp)

        b.sbuf("WT_e", [SUB, 8 * D])
        b.sbuf("CT_e", [33, D])
        b.sbuf("Vr_e", [SUB, 8 * 32])
        b.sbuf("ident", [SUB, SUB])
        b.sbuf("triu", [SUB, SUB])
        
        b.sbuf("ones_row", [1, 256])
        b.sbuf("pm_s", [SUB, 4])
        b.sbuf("normPT2", [SUB, FA])
        b.sbuf("bias_of", [1, 2 * D])

        self.sems = {k: self.sem("s_" + k) for k in
                     ("ld", "st", "mm", "ve", "sc", "gp", "cc")}
        self.bar = self.sem("s_bar")

        self.emit_init()
        phases = [("A", self.emit_phase_a),
                  ("C1", self.emit_phase_c1), ("C2", self.emit_phase_c2),
                  ("E", self.emit_phase_e)]
        barrier_before = {"C1", "C2", "E"}
        done = []
        for nm, fn in phases:
            if nm in barrier_before:
                self.barrier()
            fn()
            done.append(nm)
            if self.stop_after == nm:
                break
        if self.stop_after is not None and "E" not in done:
            self.barrier()
            zt = self.sbuf("z_fill", [SUB, D])
            ve, st = self.sems["ve"], self.sems["st"]
            self.op("vector", lambda e: e.memset(zt[:], 0.0), ve)
            self.wait("gpsimd", ve)
            for r in range(T // SUB):
                self.dma("gpsimd", self.dram["out"][r * SUB:(r + 1) * SUB, :], zt[:], st)
            self.wait("gpsimd", st)

        with self.stack:
            with nc.Block() as block:
                block.tensor(self.prog.replay("tensor"))
                block.vector(self.prog.replay("vector"))
                block.scalar(self.prog.replay("scalar"))
                block.gpsimd(self.prog.replay("gpsimd"))
                block.sync(self.prog.replay("sync"))
        return nc

    def emit_init(self):
        b, sb, dram = self, self.sb, self.dram
        ld, ve, gp = self.sems["ld"], self.sems["ve"], self.sems["gp"]

        b.op("gpsimd", lambda e: e.iota(idx32[:], pattern=[[1, SUB]], base=0,
                                        channel_multiplier=-1), gp)
        b.wait("vector", gp)
        b.op("vector", lambda e: e.tensor_single_scalar(sb["ident"][:], idx32[:],
                                                        0, op=OP.is_equal))
        b.op("vector", lambda e: e.tensor_single_scalar(sb["triu"][:], idx32[:],
                                                        0, op=OP.is_ge))
        b.op("vector", lambda e: e.memset(sb["ones_row"][:], 1.0))

        b.dma("sync", sb["pm_s"][:], dram["pmask"].partition_broadcast(SUB), ld)
        for i, nm in enumerate(["bo", "bf"]):
            b.dma("sync", sb["bias_of"][0:1, i * D:(i + 1) * D], dram[nm], ld)
        insc = contextlib.ExitStack()
        idx32 = b.sbuf("idx32", [SUB, SUB], I32, stack=insc)
        pt_raw = b.sbuf("pt_raw", [FA, FA], stack=insc)
        self._init_scope = insc
        b.dma("sync", pt_raw[:], dram["PT"], ld)
        b.wait("vector", ld)
        for half in (0, 1):
            b.op("vector", lambda e, h=half: e.tensor_single_scalar(
                sb["normPT2"][h * 64:(h + 1) * 64, :], pt_raw[:], NORM_A, op=OP.mult),
                ve)

    def emit_ln(self, x_ap, xn_ap, stats, mrneg, rstd, bump_ve=True):
        b = self
        ve, sc = self.sems["ve"], self.sems["sc"]
        b.wait("vector", sc)
        b.wait("vector", ve)
        b.op("vector", lambda e: e.reduce_sum(stats[:, 0:1], x_ap, axis=AX.X), ve)
        b.wait("vector", ve)
        b.wait("scalar", ve)
        b.op("scalar", lambda e: e.activation(xn_ap, x_ap, AF.Square,
                                              accum_out=stats[:, 1:2]), sc)
        b.wait("vector", sc)
        b.op("vector", lambda e: e.tensor_scalar_mul(stats[:, 2:3], stats[:, 0:1],
                                                     1.0 / D))
        b.op("vector", lambda e: e.tensor_scalar_mul(stats[:, 3:4], stats[:, 1:2],
                                                     1.0 / D), ve)
        b.wait("vector", ve)
        b.op("vector", lambda e: e.tensor_mul(stats[:, 4:5], stats[:, 2:3],
                                              stats[:, 2:3]), ve)
        b.wait("vector", ve)
        b.op("vector", lambda e: e.scalar_tensor_tensor(
            stats[:, 5:6], stats[:, 4:5], -1.0, stats[:, 3:4],
            op0=OP.mult, op1=OP.add), ve)
        b.wait("vector", ve)
        b.op("vector", lambda e: e.tensor_scalar_add(stats[:, 6:7], stats[:, 5:6],
                                                     LN_EPS), ve)
        b.wait("scalar", ve)
        b.op("scalar", lambda e: e.activation(stats[:, 7:8], stats[:, 6:7],
                                              AF.Sqrt), sc)
        b.wait("vector", sc)
        b.op("vector", lambda e: e.reciprocal(rstd[:], stats[:, 7:8]), ve)
        b.wait("vector", ve)
        b.op("vector", lambda e: e.scalar_tensor_tensor(
            mrneg[:], stats[:, 2:3], -1.0, rstd[:], op0=OP.mult, op1=OP.mult), ve)
        b.wait("vector", ve)
        b.op("vector", lambda e: e.tensor_scalar(
            xn_ap, x_ap, rstd[:], mrneg[:], op0=OP.mult, op1=OP.add),
            ve if bump_ve else None)

    def emit_ln2(self, x2_ap, xn2_ap, stats, mrneg, rstd, sq=None):
        b = self
        ve, sc = self.sems["ve"], self.sems["sc"]
        sq0 = sq if sq is not None else xn2_ap[:, 0:D]
        sq1 = sq if sq is not None else xn2_ap[:, D:2 * D]
        b.wait("vector", sc)
        b.wait("vector", ve)
        b.op("vector", lambda e: e.reduce_sum(stats[:, 0:1], x2_ap[:, 0:D], axis=AX.X))
        b.op("vector", lambda e: e.reduce_sum(stats[:, 1:2], x2_ap[:, D:2 * D],
                                              axis=AX.X), ve)
        b.wait("vector", ve)
        b.wait("scalar", ve)
        b.op("scalar", lambda e: e.activation(sq0, x2_ap[:, 0:D], AF.Square,
                                              accum_out=stats[:, 2:3]))
        b.op("scalar", lambda e: e.activation(sq1, x2_ap[:, D:2 * D],
                                              AF.Square, accum_out=stats[:, 3:4]), sc)
        b.wait("vector", sc)
        b.op("vector", lambda e: e.tensor_scalar_mul(stats[:, 4:6], stats[:, 0:2],
                                                     1.0 / D))
        b.op("vector", lambda e: e.tensor_scalar_mul(stats[:, 6:8], stats[:, 2:4],
                                                     1.0 / D), ve)
        b.wait("vector", ve)
        b.op("vector", lambda e: e.tensor_mul(stats[:, 8:10], stats[:, 4:6],
                                              stats[:, 4:6]), ve)
        b.wait("vector", ve)
        b.op("vector", lambda e: e.scalar_tensor_tensor(
            stats[:, 10:12], stats[:, 8:10], -1.0, stats[:, 6:8],
            op0=OP.mult, op1=OP.add), ve)
        b.wait("vector", ve)
        b.op("vector", lambda e: e.tensor_scalar_add(stats[:, 12:14], stats[:, 10:12],
                                                     LN_EPS), ve)
        b.wait("scalar", ve)
        b.op("scalar", lambda e: e.activation(stats[:, 14:16], stats[:, 12:14],
                                              AF.Sqrt), sc)
        b.wait("vector", sc)
        b.op("vector", lambda e: e.reciprocal(rstd[:, 0:2], stats[:, 14:16]), ve)
        b.wait("vector", ve)
        b.op("vector", lambda e: e.scalar_tensor_tensor(
            mrneg[:, 0:2], stats[:, 4:6], -1.0, rstd[:, 0:2],
            op0=OP.mult, op1=OP.mult), ve)
        b.wait("vector", ve)
        for j in (0, 1):
            b.op("vector", lambda e, j=j: e.tensor_scalar(
                xn2_ap[:, j * D:(j + 1) * D], x2_ap[:, j * D:(j + 1) * D],
                rstd[:, j:j + 1], mrneg[:, j:j + 1], op0=OP.mult, op1=OP.add), ve)

    def emit_transpose_1024(self, src, dst, off, stride, psl, bank_free, ve_src=None):
        b, sb = self, self.sb
        mm, ve = self.sems["mm"], self.sems["ve"]
        dstv = dst[:].rearrange("p (c t) -> p c t", t=stride)
        b.wait("tensor", ve, ve_src)
        for g in (0, 1):
            ps = psl[g]
            key = id(ps)
            v = bank_free.get(key)
            if v:
                if isinstance(v, tuple):
                    b.wait("tensor", self.sems[v[0]], v[1])
                else:
                    b.wait("tensor", ve, v)
            for cc in range(4):
                c = g * 4 + cc
                b.op("tensor", lambda e, c=c, cc=cc, ps=ps: e.transpose(
                    ps[:, cc * SUB:(cc + 1) * SUB], src[:, c * SUB:(c + 1) * SUB],
                    sb["ident"][:]), mm if cc == 3 else None)
            b.wait("vector", mm)
            b.op("vector", lambda e, g=g, ps=ps: e.tensor_copy(
                dstv[:, 4 * g:4 * g + 4, off // 1:off // 1 + SUB]
                if stride != SUB else dst[:, g * 512:(g + 1) * 512],
                ps[:, 0:512].rearrange("p (c t) -> p c t", t=SUB)
                if stride != SUB else ps[:, 0:512]), ve)
            bank_free[key] = ve.n

    def bwait(self, ps, bf):
        v = bf.get(id(ps))
        if v:
            if isinstance(v, tuple):
                self.wait("tensor", self.sems[v[0]], v[1])
            else:
                self.wait("tensor", self.sems["ve"], v)

    def bdone(self, ps, bf):
        bf[id(ps)] = self.sems["ve"].n

    def emit_phase_a(self):
        b, sb, dram = self, self.sb, self.dram
        T, NSUP = self.T, self.NSUP
        ld, st, mm, ve, sc = (self.sems[k] for k in ("ld", "st", "mm", "ve", "sc"))
        self._init_scope.close()
        ph = contextlib.ExitStack()
        b.wait("sync", self.sems["ve"])

        WpqT = b.sbuf("WpqT_s", [SUB, 8 * D], stack=ph)
        WpkT = b.sbuf("WpkT_s", [SUB, 8 * D], stack=ph)
        WvaT = b.sbuf("WvaT_s", [SUB, 8 * D], stack=ph)
        bpq = b.sbuf("bpq_s", [SUB, 8], stack=ph)
        bpk = b.sbuf("bpk_s", [SUB, 8], stack=ph)
        for c in range(8):
            b.dma("sync", WvaT[:, c * D:(c + 1) * D], dram["WvaT"][c * SUB:(c + 1) * SUB, :], ld)
        ld_w = ld.n
        for c in range(8):
            b.dma("sync", sb["WT_e"][:, c * D:(c + 1) * D],
                  dram["WT"][c * SUB:(c + 1) * SUB, :], ld)
            b.dma("sync", sb["Vr_e"][:, c * 32:(c + 1) * 32],
                  dram["Vr"][c * SUB:(c + 1) * SUB, :], ld)
        b.dma("sync", sb["CT_e"][0:32, :], dram["CT"], ld)
        b.dma("sync", sb["CT_e"][32:33, :], dram["bf"], ld)
        ld_ffnw = ld.n

        x_s = b.sbuf("a_x", [SUB, 2 * D], stack=ph)
        xn_s = b.sbuf("a_xn", [SUB, 2 * D], stack=ph)
        xnT = b.sbuf("a_xnT", [SUB, 2 * D], stack=ph)
        QfT = b.sbuf("a_QfT", [SUB, 8 * SUP], stack=ph)
        KfT = b.sbuf("a_KfT", [SUB, 8 * SUP], stack=ph)
        Kf = b.sbuf("a_Kf", [SUB, D], stack=ph)
        vpa = b.sbuf("a_vpa", [SUB, HA * 65], stack=ph)
        stats = b.sbuf("a_stats", [SUB, 16], stack=ph)
        mrneg = b.sbuf("a_mrneg", [SUB, 2], stack=ph)
        rstd = b.sbuf("a_rstd", [SUB, 2], stack=ph)
        kv_s = b.sbuf("a_kv", [SUB, 8 * 65], stack=ph)
        bias_qkv = b.sbuf("bias_qkv", [1, D], stack=ph)
        b.dma("sync", bias_qkv[0:1, 0:D], dram["bv"], ld)

        for h in range(HA):
            b.op("vector", lambda e, h=h: e.memset(
                vpa[:, h * 65 + 64: h * 65 + 65], 1.0))
        b.op("vector", lambda e: e.engine_nop(), ve)

        ps_p0 = b.psum("ps_p0", [SUB, 512], stack=ph)
        ps_p1 = b.psum("ps_p1", [SUB, 512], stack=ph)
        ps_tr = b.psum("ps_tr", [SUB, 512], stack=ph)
        ps_v0 = b.psum("ps_v0", [SUB, 512], stack=ph)
        ps_v1 = b.psum("ps_v1", [SUB, 512], stack=ph)
        kv_psl = [b.psum(f"kv_ps{k}", [SUB, 512], stack=ph) for k in range(3)]
        ps_f = ps_p0
        bf = {}

        pre = contextlib.ExitStack()
        tmp64 = b.sbuf("a_tmp64", [64, D], stack=pre)
        bc64 = b.sbuf("a_bc64", [64, HA], stack=pre)
        prot = (ps_p0, ps_p1)
        for wi, (wd, WpT, bd, bp) in enumerate(
                ((dram["Wqa"], WpqT, dram["bqc"], bpq),
                 (dram["Wka"], WpkT, dram["bkc"], bpk))):
            for h in range(HA):
                b.wait("sync", ve)
                b.dma("sync", tmp64[:], wd[h * 64:(h + 1) * 64, :], ld)
                ps = prot[h % 2]
                b.bwait(ps, bf)
                b.wait("tensor", ld)
                for c in range(8):
                    b.op("tensor", lambda e, c=c, ps=ps: e.matmul(
                        ps[:, c * 64:(c + 1) * 64], tmp64[:, c * SUB:(c + 1) * SUB],
                        sb["normPT2"][0:64, :], start=(c == 0), stop=(c == 7)),
                        mm if c == 7 else None)
                b.wait("vector", mm)
                off = (h // 2) * SUB + (h % 2) * 64
                b.op("vector", lambda e, off=off, ps=ps, WpT=WpT: e.tensor_copy(
                    WpT[:].rearrange("p (c t) -> p c t", t=8 * SUB)
                    [:, :, off:off + 64],
                    ps[:, 0:512].rearrange("p (c t) -> p c t", t=64)), ve)
                b.bdone(ps, bf)
            b.wait("sync", ve)
            b.dma("sync", bc64[:], bd, ld)
            b.wait("tensor", ld)
            b.bwait(ps_f, bf)
            b.op("tensor", lambda e, bp=bp: e.matmul(
                ps_f[0:64, 0:HA], sb["normPT2"][0:64, :], bc64[:],
                start=True, stop=True), mm)
            b.wait("vector", mm)
            for half in (0, 1):
                b.op("vector", lambda e, half=half, bp=bp: e.tensor_copy(
                    bp[half * 64:(half + 1) * 64, :],
                    ps_f[0:64, 0:HA].rearrange("p (m two) -> p m two", two=2)
                    [:, :, half]), ve)
            b.wait("vector", ve)
            b.op("vector", lambda e, bp=bp: e.tensor_scalar_add(bp[:], bp[:], EPS_A),
                 ve)
            b.bdone(ps_f, bf)
        pre.close()

        kv_first = [True] * 3

        def kv_pair(p):
            return (kv_psl[p // 3], (p % 3) * 130)

        st_qf = st.n
        prot3 = (ps_p0, ps_p1)
        for j in (0, 1):
            b.dma("sync", x_s[:, j * D:(j + 1) * D],
                  dram["x"][j * SUB:(j + 1) * SUB, :], ld)
        b.wait("vector", ld)
        self.emit_ln2(x_s[:], xn_s[:], stats, mrneg, rstd)
        ve_ln = ve.n
        for i in range(NSUP):
            tok0 = i * SUP
            if self.dbg:
                b.wait("gpsimd", ve, ve_ln)
                for j in (0, 1):
                    b.dma("gpsimd", dram["dbg_xn1"][tok0 + j * SUB: tok0 + (j + 1) * SUB, :],
                          xn_s[:, j * D:(j + 1) * D], st)
            for j in (0, 1):
                self.emit_transpose_1024(xn_s[:, j * D:(j + 1) * D], xnT, j * SUB, SUP,
                                         (ps_tr, ps_v0), bf, ve_src=ve_ln)
            ve_xnT = ve.n
            b.wait("tensor", ld, ld_w)
            b.wait("tensor", ve, ve_xnT)
            for (W, dst, bp) in ((WpqT, QfT, bpq), (WpkT, KfT, bpk)):
                for m in range(8):
                    ps = prot3[m % 2]
                    b.bwait(ps, bf)
                    for c in range(8):
                        b.op("tensor", lambda e, W=W, m=m, c=c, ps=ps: e.matmul(
                            ps[:, 0:SUP], W[:, (c * 8 + m) * SUB:(c * 8 + m + 1) * SUB],
                            xnT[:, c * SUP:(c + 1) * SUP],
                            start=(c == 0), stop=(c == 7)), mm if c == 7 else None)
                    b.wait("vector", mm)
                    if dst is QfT:
                        b.wait("vector", st, st_qf)
                    b.op("vector", lambda e, m=m, ps=ps, dst=dst, bp=bp: e.tensor_scalar(
                        dst[:, m * SUP:(m + 1) * SUP], ps[:, 0:SUP],
                        bp[:, m:m + 1], EPS_A, op0=OP.add, op1=OP.max), ve)
                    b.bdone(ps, bf)
            ve_qf = ve.n
            b.wait("gpsimd", ve, ve_qf)
            for m in range(8):
                b.dma("gpsimd", dram["qfT_sp"][m * SUB:(m + 1) * SUB, tok0:tok0 + SUP],
                      QfT[:, m * SUP:(m + 1) * SUP], st)
            st_qf = st.n

            if i + 1 < NSUP:
                b.wait("sync", ve, ve_ln)
                for j in (0, 1):
                    b.dma("sync", x_s[:, j * D:(j + 1) * D],
                          dram["x"][tok0 + SUP + j * SUB: tok0 + SUP + (j + 1) * SUB, :],
                          ld)
                b.wait("vector", ld)
                self.emit_ln2(x_s[:], xn_s[:], stats, mrneg, rstd)
                ve_ln = ve.n

            b.wait("tensor", ve, ve_qf)
            for j in (0, 1):
                for g in (0, 1):
                    ps = (ps_tr, ps_v0)[g]
                    b.bwait(ps, bf)
                    for cc in range(4):
                        m = g * 4 + cc
                        b.op("tensor", lambda e, m=m, cc=cc, j=j, ps=ps: e.transpose(
                            ps[:, cc * SUB:(cc + 1) * SUB],
                            KfT[:, m * SUP + j * SUB: m * SUP + (j + 1) * SUB],
                            sb["ident"][:]), mm if cc == 3 else None)
                    b.wait("vector", mm)
                    b.op("vector", lambda e, g=g, ps=ps: e.tensor_copy(
                        Kf[:, g * 512:(g + 1) * 512], ps[:, 0:512]), ve)
                    b.bdone(ps, bf)
                for n in (0, 1):
                    ps = (ps_v0, ps_v1)[n]
                    b.bwait(ps, bf)
                    for c in range(8):
                        b.op("tensor", lambda e, c=c, j=j, n=n, ps=ps: e.matmul(
                            ps[:, 0:512],
                            xnT[:, c * SUP + j * SUB: c * SUP + (j + 1) * SUB],
                            WvaT[:, c * D + n * 512: c * D + (n + 1) * 512],
                            start=(c == 0), stop=False))
                    b.op("tensor", lambda e, n=n, ps=ps: e.matmul(
                        ps[:, 0:512], sb["ones_row"][0:1, 0:SUB],
                        bias_qkv[0:1, n * 512:(n + 1) * 512],
                        start=False, stop=True), mm)
                    b.wait("vector", mm)
                    b.op("vector", lambda e, n=n, ps=ps: e.tensor_copy(
                        vpa[:].rearrange("p (h c) -> p h c", h=16)
                        [:, 8 * n:8 * (n + 1), 0:64],
                        ps[:, 0:512].rearrange("p (h c) -> p h c", h=8)), ve)
                    b.bdone(ps, bf)
                b.wait("tensor", ve)
                for p in range(8):
                    ps, col = kv_pair(p)
                    bank = p // 3
                    first = kv_first[bank]
                    kv_first[bank] = False
                    stop = (i == NSUP - 1 and j == 1 and
                            (p % 3 == 2 or p == 7))
                    last_sup = (j == 1 and p == 7)
                    b.op("tensor", lambda e, p=p, ps=ps, col=col, first=first,
                         stop=stop: e.matmul(
                        ps[:, col:col + 130],
                        Kf[:, p * SUB:(p + 1) * SUB],
                        vpa[:, 2 * p * 65: 2 * p * 65 + 130],
                        start=first, stop=stop),
                        mm if last_sup else None)

        b.wait("vector", mm)
        for bi_ in range(3):
            npair = (3, 3, 2)[bi_]
            base = bi_ * 3
            b.op("vector", lambda e, bi_=bi_, npair=npair, base=base: e.tensor_copy(
                kv_s[0:64, base * 65: (base + npair) * 65]
                .rearrange("p (s c) -> p s c", c=65),
                kv_psl[bi_][0:64, 0:npair * 130]
                .rearrange("p (s c) -> p s c", c=130)[:, :, 0:65]))
            b.op("vector", lambda e, bi_=bi_, npair=npair, base=base: e.tensor_copy(
                kv_s[64:128, base * 65: (base + npair) * 65]
                .rearrange("p (s c) -> p s c", c=65),
                kv_psl[bi_][64:128, 0:npair * 130]
                .rearrange("p (s c) -> p s c", c=130)[:, :, 65:130]),
                ve if bi_ == 2 else None)
        ve_kv = ve.n
        b.wait("gpsimd", ve, ve_kv)
        b.dma("gpsimd", dram["kv_cc_in"], kv_s[:], st)
        if self.dbg:
            b.dma("gpsimd", dram["dbg_kv"], kv_s[:], st)
        self.st_kv = st.n
        cc = self.sems["cc"]
        if self.seq_cores > 1:
            b.wait("gpsimd", st, self.st_kv)
            b.op("gpsimd", lambda e: e.collective_compute(
                "AllGather", OP.bypass,
                ins=[dram["kv_cc_in"]], outs=[dram["kv_cc_out"]],
                replica_groups=[[g * self.seq_cores + r for r in range(self.seq_cores)]
                                for g in range(8 // self.seq_cores)]), cc)
        self.cc_kv = cc.n
        ph.close()

    def emit_kv_gather(self):
        b, sb, dram = self, self.sb, self.dram
        ld, st, ve, cc = (self.sems[k] for k in ("ld", "st", "ve", "cc"))
        kvscope = contextlib.ExitStack()
        b.sbuf("kvg", [SUB, 8 * 65], stack=kvscope)
        if self.seq_cores > 1:
            b.wait("sync", cc, self.cc_kv)
            phk_ = contextlib.ExitStack()
            tmp = b.sbuf("kv_blk", [SUB, 4 * 520], stack=phk_)
            for r in range(self.seq_cores):
                b.dma("sync", tmp[:, r * 520:(r + 1) * 520],
                      dram["kv_cc_out"][r * SUB:(r + 1) * SUB, :], ld)
            b.wait("vector", ld)
            b.op("vector", lambda e: e.tensor_add(sb["kvg"][:], tmp[:, 0:520],
                                                  tmp[:, 520:1040]))
            if self.seq_cores == 4:
                b.op("vector", lambda e: e.tensor_add(sb["kvg"][:], sb["kvg"][:],
                                                      tmp[:, 1040:1560]))
                b.op("vector", lambda e: e.tensor_add(sb["kvg"][:], sb["kvg"][:],
                                                      tmp[:, 1560:2080]))
            b.op("vector", lambda e: e.engine_nop(), ve)
            phk_.close()
        else:
            b.dma("sync", sb["kvg"][:], dram["kv_cc_in"], ld)
            b.wait("vector", ld)
            b.op("vector", lambda e: e.engine_nop(), ve)
        b.op("vector", lambda e: e.memset(sb["kvg_z"][:], 0.0), ve)
        b.wait("vector", ve)
        for h in range(HA):
            b.op("vector", lambda e, h=h: e.tensor_copy(
                sb["kvg_z"][64 * (h % 2):64 * (h % 2) + 64, h * 65:(h + 1) * 65],
                sb["kvg"][64 * (h % 2):64 * (h % 2) + 64,
                          (h // 2) * 65:(h // 2) * 65 + 65]),
                ve if h == HA - 1 else None)
        self.ve_kvg = ve.n
        kvscope.close()

    def emit_div_evac(self, o_banks, o_s, recip, rb, scr, eps, bf=None):
        b = self
        ve, mm = self.sems["ve"], self.sems["mm"]
        b.wait("vector", mm)
        col = 0
        scol = 0
        for (ps, nsl) in o_banks:
            b.op("vector", lambda e, ps=ps, n=nsl, c=col, eps=eps: e.tensor_scalar_add(
                recip[:, c:c + n],
                ps[:, 0:n * 65].rearrange("p (s c) -> p s c", c=65)[:, :, 64], eps))
            b.op("vector", lambda e, ps=ps, n=nsl, sc_=scol: e.tensor_copy(
                scr[:, sc_:sc_ + n * 65], ps[:, 0:n * 65]), ve)
            col += nsl
            scol += nsl * 65
        if bf is not None:
            for (ps, nsl) in o_banks:
                b.bdone(ps, bf)
        b.op("vector", lambda e: e.engine_nop(), ve)
        b.wait("vector", ve)
        b.op("vector", lambda e: e.reciprocal(recip[:, 0:16], recip[:, 0:16]), ve)
        b.wait("vector", ve)
        b.op("vector", lambda e: e.tensor_copy(
            rb[:].rearrange("p (s f) -> p s f", f=64),
            recip[:, 0:16].unsqueeze(2).broadcast_to([SUB, 16, 64])), ve)
        b.wait("vector", ve)
        col = 0
        scol = 0
        for (ps, nsl) in o_banks:
            b.op("vector", lambda e, n=nsl, c=col, sc_=scol: e.tensor_tensor(
                o_s[:, c * 64:(c + n) * 64].rearrange("p (s f) -> p s f", f=64),
                scr[:, sc_:sc_ + n * 65].rearrange("p (s c) -> p s c", c=65)[:, :, 0:64],
                rb[:, c * 64:(c + n) * 64].rearrange("p (s f) -> p s f", f=64),
                op=OP.mult), ve)
            col += nsl
            scol += nsl * 65

    def emit_ffn(self, xnT, res_s, ps_t, ps_w0, ps_w1, tT_s, cb, wts=None, bf=None,
                 ve_ready=None, mid_cb=None):
        b, sb = self, self.sb
        mm, ve = self.sems["mm"], self.sems["ve"]
        WT, CT, Vr = wts if wts is not None else (sb["WT_e"], sb["CT_e"], sb["Vr_e"])
        if bf is None:
            bf = {}
        b.wait("tensor", ve, ve_ready)
        b.bwait(ps_t, bf)
        for c in range(8):
            b.op("tensor", lambda e, c=c: e.matmul(
                ps_t[0:32, 0:SUP], Vr[:, c * 32:(c + 1) * 32],
                xnT[:, c * SUP:(c + 1) * SUP], start=(c == 0), stop=(c == 7)),
                mm if c == 7 else None)
        b.wait("vector", mm)
        b.op("vector", lambda e: e.tensor_copy(tT_s[0:32, 0:SUP], ps_t[0:32, 0:SUP]), ve)
        b.bdone(ps_t, bf)
        ve_tT = ve.n
        if mid_cb is not None:
            mid_cb()
        b.wait("tensor", ve, ve_tT)
        for j in (0, 1):
            for n in (0, 1):
                ps = (ps_w0, ps_w1)[n]
                b.bwait(ps, bf)
                for c in range(8):
                    b.op("tensor", lambda e, c=c, j=j, n=n, ps=ps: e.matmul(
                        ps[:, 0:512], xnT[:, c * SUP + j * SUB: c * SUP + (j + 1) * SUB],
                        WT[:, c * D + n * 512: c * D + (n + 1) * 512],
                        start=(c == 0), stop=False))
                b.op("tensor", lambda e, j=j, n=n, ps=ps: e.matmul(
                    ps[:, 0:512], tT_s[0:33, j * SUB:(j + 1) * SUB],
                    CT[0:33, n * 512:(n + 1) * 512], start=False, stop=True), mm)
            b.wait("vector", mm)
            out_ap = cb(j)
            for n in (0, 1):
                ps = (ps_w0, ps_w1)[n]
                b.op("vector", lambda e, j=j, n=n, ps=ps, out_ap=out_ap: e.tensor_tensor(
                    out_ap[:, n * 512:(n + 1) * 512], ps[:, 0:512],
                    res_s[:, j * D + n * 512: j * D + (n + 1) * 512], op=OP.add), ve)
                b.bdone(ps, bf)

    def emit_phase_c1(self):
        b, sb, dram = self, self.sb, self.dram
        T, NSUP = self.T, self.NSUP
        ld, st, mm, ve, sc = (self.sems[k] for k in ("ld", "st", "mm", "ve", "sc"))
        ph = contextlib.ExitStack()

        WoaT = b.sbuf("WoaT_s", [SUB, 8 * D], stack=ph)
        b.sbuf("kvg_z", [SUB, HA * 65], stack=ph)
        for c in range(8):
            b.dma("sync", WoaT[:, c * D:(c + 1) * D], dram["WoaT"][c * SUB:(c + 1) * SUB, :], ld)
        ld_w = ld.n

        x_s = b.sbuf("c_x", [SUB, 2 * D], stack=ph)
        QfT = b.sbuf("c_QfT", [SUB, 2 * D], stack=ph)
        o_s = b.sbuf("c_o", [SUB, D], stack=ph)
        oT = b.sbuf("c_oT", [SUB, D], stack=ph)
        x1_s = b.sbuf("c_x1", [SUB, 4 * D], stack=ph)
        xn_s = b.sbuf("c_xn", [SUB, 2 * D], stack=ph)
        xnT = b.sbuf("c_xnT", [SUB, 2 * D], stack=ph)
        x2_s = b.sbuf("c_x2", [SUB, 2 * D], stack=ph)
        tT_s = b.sbuf("c_tT", [33, SUP], stack=ph)
        b.op("vector", lambda e: e.memset(tT_s[32:33, :], 1.0))
        recip = b.sbuf("c_recip", [SUB, 16], stack=ph)
        scr_dv = b.sbuf("c_scrdv", [SUB, 1040], stack=ph)
        rb = b.sbuf("c_rb", [SUB, D], stack=ph)
        stats = b.sbuf("c_stats", [SUB, 16], stack=ph)
        mrneg = b.sbuf("c_mrneg", [SUB, 2], stack=ph)
        rstd = b.sbuf("c_rstd", [SUB, 2], stack=ph)

        ps_o0 = b.psum("c_ps_o0", [SUB, 512], stack=ph)
        ps_o1 = b.psum("c_ps_o1", [SUB, 512], stack=ph)
        ps_o2 = b.psum("c_ps_o2", [SUB, 512], stack=ph)
        ps_tr = b.psum("c_ps_tr", [SUB, 512], stack=ph)
        ps_w0 = b.psum("c_ps_w0", [SUB, 512], stack=ph)
        ps_w1 = b.psum("c_ps_w1", [SUB, 512], stack=ph)
        ps_t = b.psum("c_ps_t", [SUB, 512], stack=ph)
        o_banks = [(ps_o0, 6), (ps_o1, 6), (ps_o2, 4)]
        psl = (ps_tr, ps_t)
        bf = {}

        qfT_r = dram["qfT_sp"].rearrange("(m p) t -> p m t", m=8)

        def load_sup(i, guard_ve):
            tok0 = i * SUP
            if guard_ve:
                b.wait("sync", ve, guard_ve)
            for j in (0, 1):
                b.dma("sync", x_s[:, j * D:(j + 1) * D],
                      dram["x"][tok0 + j * SUB: tok0 + (j + 1) * SUB, :], ld)
            b.dma("sync", QfT[:].rearrange("p (m t) -> p m t", m=8),
                  qfT_r[:, :, tok0:tok0 + SUP], ld)
            return ld.n

        ld_x = load_sup(0, 0)
        self.emit_kv_gather()
        st_x2 = [st.n]

        def stage2(k, res, x1_next):
            for j in (0, 1):
                self.emit_transpose_1024(xn_s[:, j * D:(j + 1) * D], xnT, j * SUB, SUP,
                                         psl, bf)
            ve_xnT = ve.n
            b.wait("vector", st, st_x2[0])
            mid = None
            if x1_next is not None:
                mid = lambda: self.emit_ln2(x1_next, xn_s, stats, mrneg, rstd,
                                            sq=o_s[:, 0:D])
            self.emit_ffn(xnT, res, ps_t, ps_w0, ps_w1, tT_s,
                          lambda j: x2_s[:, j * D:(j + 1) * D], bf=bf,
                          ve_ready=ve_xnT, mid_cb=mid)
            ve_x2 = ve.n
            b.wait("gpsimd", ve, ve_x2)
            for j in (0, 1):
                b.dma("gpsimd",
                      dram["x2_sp"][k * SUP + j * SUB: k * SUP + (j + 1) * SUB, :],
                      x2_s[:, j * D:(j + 1) * D], st)
            st_x2[0] = st.n

        x1_prev = None
        for i in range(NSUP):
            tok0 = i * SUP
            x1b = x1_s[:, (i % 2) * 2 * D:(i % 2) * 2 * D + 2 * D]
            if i == 0:
                b.wait("tensor", ve, self.ve_kvg)
            for j in (0, 1):
                b.wait("tensor", ld, ld_x)
                hbase = 0
                for (ps, nsl) in o_banks:
                    b.bwait(ps, bf)
                    for pr in range(nsl // 2):
                        h0 = hbase + 2 * pr
                        b.op("tensor", lambda e, h0=h0, ps=ps, pr=pr, j=j,
                             nsl=nsl: e.matmul(
                            ps[:, pr * 130:(pr + 1) * 130],
                            QfT[:, (h0 // 2) * SUP + j * SUB:(h0 // 2) * SUP + (j + 1) * SUB],
                            sb["kvg_z"][:, h0 * 65:(h0 + 2) * 65],
                            start=(pr == 0), stop=(pr == nsl // 2 - 1)),
                            mm if pr == nsl // 2 - 1 else None)
                    hbase += nsl
                self.emit_div_evac(o_banks, o_s, recip, rb, scr_dv, EPS_A, bf=bf)
                self.emit_transpose_1024(o_s, oT, 0, SUB, psl, bf)
                for n in (0, 1):
                    ps = (ps_w0, ps_w1)[n]
                    b.bwait(ps, bf)
                    for c in range(8):
                        b.op("tensor", lambda e, c=c, n=n, ps=ps: e.matmul(
                            ps[:, 0:512], oT[:, c * SUB:(c + 1) * SUB],
                            WoaT[:, c * D + n * 512: c * D + (n + 1) * 512],
                            start=(c == 0), stop=False))
                    b.op("tensor", lambda e, n=n, ps=ps: e.matmul(
                        ps[:, 0:512], sb["ones_row"][0:1, 0:SUB],
                        sb["bias_of"][0:1, n * 512:(n + 1) * 512],
                        start=False, stop=True), mm)
                b.wait("vector", mm)
                for n in (0, 1):
                    ps = (ps_w0, ps_w1)[n]
                    b.op("vector", lambda e, j=j, n=n, ps=ps, x1b=x1b: e.tensor_tensor(
                        x1b[:, j * D + n * 512: j * D + (n + 1) * 512], ps[:, 0:512],
                        x_s[:, j * D + n * 512: j * D + (n + 1) * 512], op=OP.add), ve)
                    b.bdone(ps, bf)
                if self.dbg:
                    b.wait("gpsimd", ve)
                    b.dma("gpsimd", dram["dbg_x1"][tok0 + j * SUB: tok0 + (j + 1) * SUB, :],
                          x1b[:, j * D:(j + 1) * D], st)
            if i + 1 < NSUP:
                ld_x = load_sup(i + 1, ve.n)
            if i == 0:
                self.emit_ln2(x1b, xn_s, stats, mrneg, rstd)
            else:
                stage2(i - 1, x1_prev, x1b)
            x1_prev = x1b
        stage2(NSUP - 1, x1_prev, None)
        ph.close()

    def emit_phase_c2(self):
        b, sb, dram = self, self.sb, self.dram
        T, NSUP = self.T, self.NSUP
        ld, st, mm, ve, sc = (self.sems[k] for k in ("ld", "st", "mm", "ve", "sc"))
        b.sbuf("s_run", [SUB, 32 * 65])
        b.op("vector", lambda e: e.memset(sb["s_run"][:], 0.0))
        ph = contextlib.ExitStack()


        Wqb = b.sbuf("Wqb_s", [SUB, 2 * D], stack=ph)
        Wkb = b.sbuf("Wkb_s", [SUB, 2 * D], stack=ph)
        Wvb = b.sbuf("Wvb_s", [SUB, 8 * D], stack=ph)
        for c in range(8):
            b.dma("sync", Wqb[:, c * F2:(c + 1) * F2], dram["Wqb"][c * SUB:(c + 1) * SUB, :], ld)
            b.dma("sync", Wkb[:, c * F2:(c + 1) * F2], dram["Wkb"][c * SUB:(c + 1) * SUB, :], ld)
            b.dma("sync", Wvb[:, c * D:(c + 1) * D], dram["Wvb"][c * SUB:(c + 1) * SUB, :], ld)
        ld_w = ld.n

        x2_s = b.sbuf("c2_x2", [SUB, 2 * D], stack=ph)
        xn_s = b.sbuf("c2_xn", [SUB, 2 * D], stack=ph)
        xnT = b.sbuf("c2_xnT", [SUB, 2 * D], stack=ph)
        qT_s = b.sbuf("c2_qT", [SUB, 2 * SUP], stack=ph)
        kT_s = b.sbuf("c2_kT", [SUB, 2 * SUP], stack=ph)
        k_s = b.sbuf("c2_k", [SUB, 2 * F2], stack=ph)
        phk = b.sbuf("c2_phk", [SUB, 2 * HB * F2], stack=ph)
        vpb = b.sbuf("c2_vpb", [SUB, 1040], stack=ph)
        U_s = b.sbuf("c2_U", [SUB, 2080], stack=ph)
        S_loc = b.sbuf("c2_Sloc", [SUB, 2080], stack=ph)
        stats = b.sbuf("c2_stats", [SUB, 16], stack=ph)
        mrneg = b.sbuf("c2_mrneg", [SUB, 2], stack=ph)
        rstd = b.sbuf("c2_rstd", [SUB, 2], stack=ph)

        ps_a = b.psum("c2_ps_a", [SUB, 512], stack=ph)
        ps_b = b.psum("c2_ps_b", [SUB, 512], stack=ph)
        ps_tr = b.psum("c2_ps_tr", [SUB, 512], stack=ph)
        u_ps = [b.psum(f"c2_ps_u{k}", [SUB, 512], stack=ph) for k in range(5)]
        psl = (ps_tr, ps_a)
        bf = {}

        b.op("vector", lambda e: e.memset(S_loc[:], 0.0))
        for h in range(HB):
            b.op("vector", lambda e, h=h: e.memset(
                vpb[:, h * 65 + 64: h * 65 + 65], 1.0))
        b.op("vector", lambda e: e.engine_nop(), ve)

        qT_r = dram["qT_sp"].rearrange("(m p) t -> p m t", m=2)
        kT_r = dram["kT_sp"].rearrange("(m p) t -> p m t", m=2)
        st_sp = st.n
        mm_u = {0: 0, 1: 0}
        gp = self.sems["gp"]
        for j in (0, 1):
            b.dma("sync", x2_s[:, j * D:(j + 1) * D],
                  dram["x2_sp"][j * SUB:(j + 1) * SUB, :], ld)
        b.wait("vector", ld)
        self.emit_ln2(x2_s[:], xn_s[:], stats, mrneg, rstd)
        ve_ln = ve.n
        for i in range(NSUP):
            tok0 = i * SUP
            for j in (0, 1):
                self.emit_transpose_1024(xn_s[:, j * D:(j + 1) * D], xnT, j * SUB, SUP,
                                         psl, bf, ve_src=ve_ln)
            ve_xnT = ve.n

            b.wait("tensor", ld, ld_w)
            b.wait("tensor", ve, ve_xnT)

            def proj_qk(W, dst, scale):
                for mq in (0, 1):
                    ps = (ps_a, ps_b)[mq]
                    b.bwait(ps, bf)
                    for c in range(8):
                        b.op("tensor", lambda e, W=W, mq=mq, c=c, ps=ps: e.matmul(
                            ps[:, 0:SUP], W[:, (c * 2 + mq) * SUB:(c * 2 + mq + 1) * SUB],
                            xnT[:, c * SUP:(c + 1) * SUP],
                            start=(c == 0), stop=(c == 7)), mm if c == 7 else None)
                    b.wait("vector", mm)
                    if scale is None:
                        b.op("vector", lambda e, mq=mq, ps=ps, dst=dst: e.tensor_copy(
                            dst[:, mq * SUP:(mq + 1) * SUP], ps[:, 0:SUP]), ve)
                    else:
                        b.op("vector", lambda e, mq=mq, ps=ps, dst=dst, sc_=scale:
                             e.tensor_single_scalar(dst[:, mq * SUP:(mq + 1) * SUP],
                                                    ps[:, 0:SUP], sc_, op=OP.mult), ve)
                    b.bdone(ps, bf)

            proj_qk(Wkb, kT_s, None)
            gp_j = {}
            ve_phi = {}
            for j in (0, 1):
                b.wait("tensor", ve)
                b.bwait(ps_tr, bf)
                for mq in (0, 1):
                    b.op("tensor", lambda e, mq=mq, j=j: e.transpose(
                        ps_tr[:, mq * SUB:(mq + 1) * SUB],
                        kT_s[:, mq * SUP + j * SUB: mq * SUP + (j + 1) * SUB],
                        sb["ident"][:]), mm if mq == 1 else None)
                b.wait("vector", mm)
                b.op("vector", lambda e, j=j: e.tensor_copy(
                    k_s[:, j * F2:(j + 1) * F2], ps_tr[:, 0:256]), ve)
                b.bdone(ps_tr, bf)
                b.wait("gpsimd", ve)
                b.wait("gpsimd", mm, mm_u[j])
                b.wait("vector", mm, mm_u[j])
                for h in range(HB):
                    eng = "gpsimd" if h < 8 else "vector"
                    sem = gp if h < 8 else ve
                    b.op(eng, lambda e, j=j, h=h: e.tensor_tensor(
                        phk[:, j * HB * F2 + h * F2:
                            j * HB * F2 + (h + 1) * F2]
                        .rearrange("p (a c) -> p a c", c=FB),
                        k_s[:, j * F2 + h * FB: j * F2 + h * FB + FB]
                        .unsqueeze(2).broadcast_to([SUB, FB, FB]),
                        k_s[:, j * F2 + h * FB: j * F2 + h * FB + FB]
                        .unsqueeze(1).broadcast_to([SUB, FB, FB]),
                        op=OP.mult), sem)
                gp_j[j] = gp.n
                ve_phi[j] = ve.n
                if self.dbg:
                    b.wait("gpsimd", gp)
                    b.wait("gpsimd", ve)
                    b.dma("gpsimd", dram["dbg_phk"][tok0 + j * SUB: tok0 + (j + 1) * SUB, :],
                          phk[:, j * HB * F2:(j + 1) * HB * F2], st)

            proj_qk(Wqb, qT_s, QSCALE_B)
            b.wait("gpsimd", ve)
            b.wait("gpsimd", st, st_sp)
            b.dma("gpsimd", qT_r[:, :, tok0:tok0 + SUP],
                  qT_s[:].rearrange("p (m t) -> p m t", m=2), st)
            b.dma("gpsimd", kT_r[:, :, tok0:tok0 + SUP],
                  kT_s[:].rearrange("p (m t) -> p m t", m=2), st)
            ubank_first = [True] * 5
            for j in (0, 1):
                for n in (0, 1):
                    ps = (ps_a, ps_b)[n]
                    b.bwait(ps, bf)
                    for c in range(8):
                        b.op("tensor", lambda e, c=c, j=j, n=n, ps=ps: e.matmul(
                            ps[:, 0:512],
                            xnT[:, c * SUP + j * SUB: c * SUP + (j + 1) * SUB],
                            Wvb[:, c * D + n * 512: c * D + (n + 1) * 512],
                            start=(c == 0), stop=(c == 7)), mm if c == 7 else None)
                    b.wait("vector", mm)
                    b.op("vector", lambda e, n=n, ps=ps: e.tensor_copy(
                        vpb[:].rearrange("p (h c) -> p h c", h=16)
                        [:, 8 * n:8 * (n + 1), 0:64],
                        ps[:, 0:512].rearrange("p (h c) -> p h c", h=8)), ve)
                    b.bdone(ps, bf)
                ve_vpb = ve.n
                b.wait("gpsimd", ve, ve_vpb)
                b.dma("gpsimd", dram["vpb_sp"][i * 2 + j],
                      vpb[:], st)
                b.wait("tensor", gp, gp_j[j])
                b.wait("tensor", ve, max(ve_vpb, ve_phi[j]))
                for h in range(HB):
                    for fc in (0, 1):
                        s_ = h * 2 + fc
                        bank, colk = s_ // 7, (s_ % 7) * 65
                        first = ubank_first[bank] and j == 0
                        if j == 0 and ubank_first[bank]:
                            ubank_first[bank] = False
                        stop = (j == 1 and (s_ % 7 == 6 or s_ == 31))
                        b.op("tensor", lambda e, h=h, fc=fc, j=j, bank=bank, colk=colk,
                             first=first, stop=stop: e.matmul(
                            u_ps[bank][:, colk:colk + 65],
                            phk[:, j * HB * F2 + h * F2 + fc * SUB:
                                j * HB * F2 + h * F2 + (fc + 1) * SUB],
                            vpb[:, h * 65: (h + 1) * 65],
                            start=first, stop=stop),
                            mm if s_ == 31 else None)
                mm_u[j] = mm.n
                if j == 0 and i + 1 < NSUP:
                    b.wait("sync", ve, ve_ln)
                    for jj in (0, 1):
                        b.dma("sync", x2_s[:, jj * D:(jj + 1) * D],
                              dram["x2_sp"][tok0 + SUP + jj * SUB:
                                            tok0 + SUP + (jj + 1) * SUB, :], ld)
                    b.wait("vector", ld)
                    self.emit_ln2(x2_s[:], xn_s[:], stats, mrneg, rstd)
                    ve_ln = ve.n

            b.wait("vector", mm)
            b.wait("vector", st, st_sp)
            for bank in range(5):
                ncols = 455 if bank < 4 else 260
                b.op("vector", lambda e, bank=bank, ncols=ncols: e.tensor_copy(
                    U_s[:, bank * 455: bank * 455 + ncols], u_ps[bank][:, 0:ncols]),
                    ve if bank == 4 else None)
            b.wait("vector", ve)
            b.op("vector", lambda e: e.tensor_add(S_loc[:], S_loc[:], U_s[:]), ve)
            ve_u = ve.n
            b.wait("gpsimd", ve, ve_u)
            b.dma("gpsimd", dram["u_sp"][i], U_s[:], st)
            st_sp = st.n
            ve_free = ve.n

        b.wait("gpsimd", ve)
        b.dma("gpsimd", dram["s_cc_in"], S_loc[:], st)
        self.st_sloc = st.n
        cc = self.sems["cc"]
        if self.seq_cores > 1:
            b.wait("gpsimd", st, self.st_sloc)
            b.op("gpsimd", lambda e: e.collective_compute(
                "AllGather", OP.bypass,
                ins=[dram["s_cc_in"]], outs=[dram["s_cc_out"]],
                replica_groups=[[g * self.seq_cores + r for r in range(self.seq_cores)]
                                for g in range(8 // self.seq_cores)]), cc)
        self.cc_s = cc.n
        ph.close()

    def emit_s_gather(self):
        b, sb, dram = self, self.sb, self.dram
        ld, st, ve, cc = (self.sems[k] for k in ("ld", "st", "ve", "cc"))
        if self.seq_cores > 1:
            b.wait("sync", cc, self.cc_s)
            phs_ = contextlib.ExitStack()
            blk = b.sbuf("s_blk", [SUB, 1040], stack=phs_)
            for r in range(self.seq_cores):
                for hf in (0, 1):
                    b.dma("sync", blk[:],
                          dram["s_cc_out"][r * SUB:(r + 1) * SUB,
                                           hf * 1040:(hf + 1) * 1040], ld)
                    b.wait("vector", ld)
                    b.op("vector", lambda e, r=r, hf=hf: e.scalar_tensor_tensor(
                        sb["s_run"][:, hf * 1040:(hf + 1) * 1040], blk[:],
                        sb["pm_s"][:, r:r + 1],
                        sb["s_run"][:, hf * 1040:(hf + 1) * 1040],
                        op0=OP.mult, op1=OP.add), ve)
                    b.wait("vector", ve)
                    b.wait("sync", ve)
            phs_.close()
        self.ve_srun = self.sems["ve"].n

    def emit_phase_e(self):
        b, sb, dram = self, self.sb, self.dram
        T, NSUP = self.T, self.NSUP
        ld, st, mm, ve, sc = (self.sems[k] for k in ("ld", "st", "mm", "ve", "sc"))
        ph = contextlib.ExitStack()

        Wob = b.sbuf("Wob_s", [SUB, 8 * D], stack=ph)
        for c in range(8):
            b.dma("sync", Wob[:, c * D:(c + 1) * D], dram["Wob"][c * SUB:(c + 1) * SUB, :], ld)
        WTe, CTe, Vre = sb["WT_e"], sb["CT_e"], sb["Vr_e"]

        x2e = b.sbuf("e_x2", [SUB, 2 * D], stack=ph)
        qTE = b.sbuf("e_qTE", [SUB, 4 * SUP], stack=ph)
        kTE = b.sbuf("e_kTE", [SUB, 4 * SUP], stack=ph)
        vpbe = b.sbuf("e_vpb", [SUB, 2 * 1040], stack=ph)
        Ue = b.sbuf("e_U", [SUB, 2080], stack=ph)
        rep1 = b.sbuf("e_rep1", [SUB, HB * 2 * SUB], stack=ph)
        rep2 = b.sbuf("e_rep2", [SUB, 8 * SUB], stack=ph)
        at_s = b.sbuf("e_at", [SUB, HB * SUB], stack=ph)
        triu4 = b.sbuf("e_triu4", [SUB, 512], stack=ph)
        ob_s = b.sbuf("e_ob", [SUB, D], stack=ph)
        obT = b.sbuf("e_obT", [SUB, D], stack=ph)
        x3_s = b.sbuf("e_x3", [SUB, 4 * D], stack=ph)
        xn_s = b.sbuf("e_xn", [SUB, 2 * D], stack=ph)
        xn4T = b.sbuf("e_xn4T", [SUB, 2 * D], stack=ph)
        x4_s = b.sbuf("e_x4", [SUB, D], stack=ph)
        tTe = b.sbuf("e_tT", [33, SUP], stack=ph)
        b.op("vector", lambda e: e.memset(tTe[32:33, :], 1.0))
        recip = b.sbuf("e_recip", [SUB, 16], stack=ph)
        scr_dv = b.sbuf("e_scrdv", [SUB, 1040], stack=ph)
        rb = b.sbuf("e_rb", [SUB, D], stack=ph)
        stats = b.sbuf("e_stats", [SUB, 16], stack=ph)
        mrneg = b.sbuf("e_mrneg", [SUB, 2], stack=ph)
        rstd = b.sbuf("e_rstd", [SUB, 2], stack=ph)

        ps_o0 = b.psum("e_ps_o0", [SUB, 512], stack=ph)
        ps_o1 = b.psum("e_ps_o1", [SUB, 512], stack=ph)
        ps_o2 = b.psum("e_ps_o2", [SUB, 512], stack=ph)
        at_ps0 = b.psum("e_ps_at0", [SUB, 512], stack=ph)
        at_ps1 = b.psum("e_ps_at1", [SUB, 512], stack=ph)
        ps_tr = b.psum("e_ps_tr", [SUB, 512], stack=ph)
        ps_w0 = b.psum("e_ps_w0", [SUB, 512], stack=ph)
        ps_w1 = b.psum("e_ps_w1", [SUB, 512], stack=ph)
        o_banks = [(ps_o0, 7), (ps_o1, 7), (ps_o2, 2)]
        psl = (ps_tr, ps_w0)
        at_banks = [at_ps0, at_ps1, ps_tr, ps_w1]
        at_sems = [self.sem(f"s_at{k}") for k in range(4)]
        bf = {}
        for q in range(4):
            b.op("vector", lambda e, q=q: e.tensor_copy(
                triu4[:, q * SUB:(q + 1) * SUB], sb["triu"][:]),
                ve if q == 3 else None)

        q_r1 = (dram["qT_sp"].rearrange("(h f2 a) t -> a h f2 t", h=16, f2=2)
                .unsqueeze(1).broadcast_to([8, 16, 16, 2, T]))
        q_r2c = dram["qT_sp"].rearrange("(h bb) t -> bb h t", h=16)

        def group_loads(i, g_mm, g_ve):
            tok0 = i * SUP
            if g_mm:
                b.wait("sync", mm, g_mm)
            if g_ve:
                b.wait("sync", ve, g_ve)
            for j in (0, 1):
                b.dma("sync", x2e[:, j * D:(j + 1) * D],
                      dram["x2_sp"][tok0 + j * SUB: tok0 + (j + 1) * SUB, :], ld)
                b.dma("sync", vpbe[:, j * 1040:(j + 1) * 1040],
                      dram["vpb_sp"][i * 2 + j], ld)
            for hh in range(4):
                for (dstE, srcd) in ((qTE, dram["qT_sp"]), (kTE, dram["kT_sp"])):
                    b.dma("sync",
                          dstE[hh * 32: hh * 32 + 16, :]
                          .rearrange("p (m t) -> p m t", m=4),
                          srcd[:, tok0:tok0 + SUP]
                          .rearrange("(m hf) t -> m hf t", hf=64)
                          [:, 16 * hh:16 * (hh + 1), :].transpose([1, 0, 2]), ld)
            b.dma("sync", Ue[:], dram["u_sp"][i], ld)
            return ld.n

        def phi_load(i, j, guard_mm):
            tok0 = i * SUP
            if guard_mm:
                b.wait("sync", mm, guard_mm)
            for a in range(8):
                b.dma("sync",
                      rep1[a * 16:(a + 1) * 16, :]
                      .rearrange("p (h f2 t) -> p h f2 t", h=16, f2=2),
                      q_r1[a, :, :, :, tok0 + j * SUB: tok0 + (j + 1) * SUB], ld)
            ld1 = ld.n
            for hh in (0, 1):
                for a in range(8):
                    b.dma("sync", rep2[a * 16:(a + 1) * 16, :] if hh == 0 else
                          rep2[a * 16:(a + 1) * 16, :],
                          q_r2c[:, 8 * hh:8 * (hh + 1),
                                tok0 + j * SUB: tok0 + (j + 1) * SUB], ld)
                ld_rep = ld.n
                b.wait("gpsimd", ld, ld_rep)
                gp = self.sems["gp"]
                for f2 in (0, 1):
                    b.op("gpsimd", lambda e, hh=hh, f2=f2: e.tensor_mul(
                        rep1[:].rearrange("p (h f2 t) -> p h f2 t", h=16, f2=2)
                        [:, 8 * hh:8 * (hh + 1), f2, :],
                        rep1[:].rearrange("p (h f2 t) -> p h f2 t", h=16, f2=2)
                        [:, 8 * hh:8 * (hh + 1), f2, :],
                        rep2[:].rearrange("p (h t) -> p h t", h=8)), gp)
                if hh == 0:
                    b.wait("sync", gp)
            return ("gp", self.sems["gp"].n)

        ld_g = group_loads(0, 0, 0)
        ve_phq = phi_load(0, 0, 0)
        self.emit_s_gather()
        st_out = st.n

        for i in range(NSUP):
            tok0 = i * SUP
            x3b = x3_s[:, (i % 2) * 2 * D:(i % 2) * 2 * D + 2 * D]
            mm_oi = {}
            for j in (0, 1):
                for (ps, nsl) in o_banks:
                    b.bwait(ps, bf)
                b.wait("tensor", self.sems[ve_phq[0]], ve_phq[1])
                b.wait("tensor", ld, ld_g)
                if i == 0 and j == 0:
                    b.wait("tensor", ve, self.ve_srun)
                slot = 0
                for (ps, nsl) in o_banks:
                    for s_ in range(nsl):
                        h = slot + s_
                        for fc in (0, 1):
                            b.op("tensor", lambda e, h=h, fc=fc, ps=ps, s_=s_: e.matmul(
                                ps[:, s_ * 65:(s_ + 1) * 65],
                                rep1[:, (2 * h + fc) * SUB:(2 * h + fc + 1) * SUB],
                                sb["s_run"][:, (2 * h + fc) * 65:(2 * h + fc + 1) * 65],
                                start=(s_ == 0 and fc == 0), stop=False),
                                mm if (h == HB - 1 and fc == 1) else None)
                    slot += nsl
                mm_oi[j] = mm.n
                if j == 0:
                    ve_phq = phi_load(i, 1, mm_oi[0])
                else:
                    self.emit_transpose_1024(xn_s[:, 0:D], xn4T, 0, SUP, psl, bf)
                def o_slot(h):
                    if h < 7:
                        return ps_o0, h
                    if h < 14:
                        return ps_o1, h - 7
                    return ps_o2, h - 14
                for si in range(j + 1):
                    diag = (si == j)
                    for bq in range(4):
                        b.bwait(at_banks[bq], bf)
                        r = bq * 32
                        for s in range(4):
                            h = s * 4 + bq
                            m4 = h // 4
                            b.op("tensor", lambda e, r=r, s=s, m4=m4, si=si, j=j,
                                 bq=bq: e.matmul(
                                at_banks[bq][:, s * SUB:(s + 1) * SUB],
                                kTE[r:r + 16, m4 * SUP + si * SUB: m4 * SUP + (si + 1) * SUB],
                                qTE[r:r + 16, m4 * SUP + j * SUB: m4 * SUP + (j + 1) * SUB],
                                start=True, stop=True, tile_position=(r, 0),
                                skip_group_check=(r > 0)),
                                at_sems[bq] if s == 3 else None)
                    for bq in range(4):
                        b.wait("scalar", at_sems[bq], at_sems[bq].n)
                        b.op("scalar", lambda e, bq=bq: e.activation(
                            at_s[:, bq * 512:(bq + 1) * 512],
                            at_banks[bq][:, 0:512], AF.Square), sc)
                        bf[id(at_banks[bq])] = ("sc", sc.n)
                        if diag:
                            b.wait("vector", sc)
                            b.op("vector", lambda e, bq=bq: e.tensor_mul(
                                at_s[:, bq * 512:(bq + 1) * 512],
                                at_s[:, bq * 512:(bq + 1) * 512], triu4[:]), ve)
                            b.wait("tensor", ve)
                        else:
                            b.wait("tensor", sc)
                        for s in range(4):
                            h = s * 4 + bq
                            ps, s_ = o_slot(h)
                            stop = diag and all(
                                o_slot(s2 * 4 + b2)[0] is not ps
                                for b2 in range(4) for s2 in range(4)
                                if (b2, s2) > (bq, s))
                            b.op("tensor", lambda e, h=h, ps=ps, s_=s_, si=si, bq=bq,
                                 s=s, stop=stop: e.matmul(
                                ps[:, s_ * 65:(s_ + 1) * 65],
                                at_s[:, (bq * 4 + s) * SUB:(bq * 4 + s + 1) * SUB],
                                vpbe[:, si * 1040 + h * 65: si * 1040 + (h + 1) * 65],
                                start=False, stop=stop),
                                mm if stop and bq == 3 and s == 3 else None)
                mm_intra = mm.n
                self.emit_div_evac(o_banks, ob_s, recip, rb, scr_dv, EPS_B, bf=bf)
                if self.dbg:
                    b.wait("gpsimd", ve)
                    b.dma("gpsimd",
                          dram["dbg_onum"][tok0 + j * SUB: tok0 + (j + 1) * SUB, :],
                          scr_dv[:], st)
                    b.dma("gpsimd", dram["dbg_ob"][tok0 + j * SUB: tok0 + (j + 1) * SUB, :],
                          ob_s[:], st)
                self.emit_transpose_1024(ob_s, obT, 0, SUB, psl, bf)
                b.wait("tensor", ve)
                for n in (0, 1):
                    ps = (ps_w0, ps_w1)[n]
                    b.bwait(ps, bf)
                    for c in range(8):
                        b.op("tensor", lambda e, c=c, n=n, ps=ps: e.matmul(
                            ps[:, 0:512], obT[:, c * SUB:(c + 1) * SUB],
                            Wob[:, c * D + n * 512: c * D + (n + 1) * 512],
                            start=(c == 0), stop=(c == 7)), mm if c == 7 else None)
                b.wait("vector", mm)
                for n in (0, 1):
                    ps = (ps_w0, ps_w1)[n]
                    b.op("vector", lambda e, j=j, n=n, ps=ps, x3b=x3b: e.tensor_tensor(
                        x3b[:, j * D + n * 512: j * D + (n + 1) * 512], ps[:, 0:512],
                        x2e[:, j * D + n * 512: j * D + (n + 1) * 512], op=OP.add), ve)
                    b.bdone(ps, bf)
                if j == 1:
                    ve_x3 = ve.n
                if self.dbg:
                    b.wait("gpsimd", ve)
                    b.dma("gpsimd", dram["dbg_x3"][tok0 + j * SUB: tok0 + (j + 1) * SUB, :],
                          x3b[:, j * D:(j + 1) * D], st)
                if j == 0:
                    self.emit_ln(x3b[:, 0:D], xn_s[:, 0:D], stats,
                                 mrneg[:, 0:1], rstd[:, 0:1])

            self.emit_ln(x3b[:, D:2 * D], xn_s[:, D:2 * D], stats,
                         mrneg[:, 1:2], rstd[:, 1:2])
            self.emit_transpose_1024(xn_s[:, D:2 * D], xn4T, SUB, SUP, psl, bf)

            b.wait("vector", mm, mm_oi[1])
            b.wait("vector", ld, ld_g)
            b.op("vector", lambda e: e.tensor_add(sb["s_run"][:], sb["s_run"][:], Ue[:]),
                 ve)
            ve_sru = ve.n
            if self.dbg and i + 1 < NSUP:
                b.wait("gpsimd", ve)
                b.dma("gpsimd", dram["dbg_srun"][i + 1], sb["s_run"][:], st)

            if i + 1 < NSUP:
                ld_g = group_loads(i + 1, mm_intra, max(ve_x3, ve_sru))
                ve_phq = phi_load(i + 1, 0, mm_oi[1])

            self.emit_ffn_out(xn4T, x3b, at_ps0, ps_w0, ps_w1, tTe, x4_s,
                              (WTe, CTe, Vre), tok0, st, bf=bf)
        b.wait("gpsimd", st)
        ph.close()

    def emit_ffn_out(self, xnT, res_s, ps_t, ps_w0, ps_w1, tT_s, x4_s, wts, tok0, st,
                     bf=None):
        b, sb, dram = self, self.sb, self.dram
        mm, ve = self.sems["mm"], self.sems["ve"]
        WT, CT, Vr = wts
        if bf is None:
            bf = {}
        b.wait("tensor", ve)
        b.bwait(ps_t, bf)
        for c in range(8):
            b.op("tensor", lambda e, c=c: e.matmul(
                ps_t[0:32, 0:SUP], Vr[:, c * 32:(c + 1) * 32],
                xnT[:, c * SUP:(c + 1) * SUP], start=(c == 0), stop=(c == 7)),
                mm if c == 7 else None)
        b.wait("vector", mm)
        b.op("vector", lambda e: e.tensor_copy(tT_s[0:32, 0:SUP], ps_t[0:32, 0:SUP]), ve)
        b.bdone(ps_t, bf)
        b.wait("tensor", ve)
        for j in (0, 1):
            for n in (0, 1):
                ps = (ps_w0, ps_w1)[n]
                b.bwait(ps, bf)
                for c in range(8):
                    b.op("tensor", lambda e, c=c, j=j, n=n, ps=ps: e.matmul(
                        ps[:, 0:512], xnT[:, c * SUP + j * SUB: c * SUP + (j + 1) * SUB],
                        WT[:, c * D + n * 512: c * D + (n + 1) * 512],
                        start=(c == 0), stop=False))
                b.op("tensor", lambda e, j=j, n=n, ps=ps: e.matmul(
                    ps[:, 0:512], tT_s[0:33, j * SUB:(j + 1) * SUB],
                    CT[0:33, n * 512:(n + 1) * 512], start=False, stop=True), mm)
            b.wait("vector", mm)
            b.wait("vector", st)
            for n in (0, 1):
                ps = (ps_w0, ps_w1)[n]
                b.op("vector", lambda e, j=j, n=n, ps=ps: e.tensor_tensor(
                    x4_s[:, n * 512:(n + 1) * 512], ps[:, 0:512],
                    res_s[:, j * D + n * 512: j * D + (n + 1) * 512], op=OP.add), ve)
                b.bdone(ps, bf)
            b.wait("gpsimd", ve)
            b.dma("gpsimd", dram["out"][tok0 + j * SUB: tok0 + (j + 1) * SUB, :],
                  x4_s[:], st)


def build_program(T, seq_cores, dbg=False):
    return Builder(T, seq_cores, dbg=dbg).build()


def _host_layout(inputs, T_core, n_cores, seq_cores):
    A = np.ascontiguousarray
    x = inputs["L0_attn_a_x"]
    B, N, _ = x.shape
    w = {
        "Wqa": A(inputs["L0_attn_a_Wq"]), "Wka": A(inputs["L0_attn_a_Wk"]),
        "WvaT": A(inputs["L0_attn_a_Wv"].T), "WoaT": A(inputs["L0_attn_a_Wo"].T),
        "PT": A(inputs["L0_attn_a_P"].T),
        "bqc": A(inputs["L0_attn_a_bq"].reshape(HA, 64).T),
        "bkc": A(inputs["L0_attn_a_bk"].reshape(HA, 64).T),
        "bv": A(inputs["L0_attn_a_bv"].reshape(1, D)),
        "bo": A(inputs["L0_attn_a_bo"].reshape(1, D)),
        "WT": A(inputs["L0_ffn_W"].T), "bf": A(inputs["L0_ffn_b"].reshape(1, D)),
        "Vr": A(inputs["L0_ffn_V_r"]), "CT": A(inputs["L0_ffn_C"].T),
        "Wqb": A(inputs["L0_attn_b_Wq"]), "Wkb": A(inputs["L0_attn_b_Wk"]),
        "Wvb": A(inputs["L0_attn_b_Wv"]), "Wob": A(inputs["L0_attn_b_Wo"]),
    }
    in_maps = []
    for c in range(n_cores):
        bidx = c // seq_cores
        q = c % seq_cores
        m = dict(w)
        m["x"] = A(x[bidx, q * T_core:(q + 1) * T_core, :])
        m["pmask"] = A(np.asarray([[1.0 if j < q else 0.0 for j in range(4)]],
                                  dtype=np.float32))
        in_maps.append(m)
    return in_maps


def kernel(**inputs):
    n_cores = 8
    seq_cores = 4
    x = np.asarray(inputs["L0_attn_a_x"], dtype=np.float32)
    B, N, _ = x.shape
    T_core = (B * N) // n_cores
    nc = build_program(T_core, seq_cores)
    in_maps = _host_layout({k: np.asarray(v, dtype=np.float32)
                            for k, v in inputs.items()},
                           T_core, n_cores, seq_cores)
    res = run_bass_kernel_spmd(nc, in_maps, core_ids=list(range(n_cores)))
    full = np.empty((B, N, D), dtype=np.float32)
    for c in range(n_cores):
        bidx = c // seq_cores
        q = c % seq_cores
        full[bidx, q * T_core:(q + 1) * T_core, :] = res.results[c]["out"]
    return full
